# version 15
# baseline (speedup 1.0000x reference)
import contextlib
import math
import os

import numpy as np
import ml_dtypes
import concourse.bass as bass
import concourse.mybir as mybir
from concourse.ap import AP
from concourse.bass_utils import run_bass_kernel_spmd

F32 = mybir.dt.float32
BF16 = mybir.dt.bfloat16
I32 = mybir.dt.int32
AF = mybir.ActivationFunctionType
ALU = mybir.AluOpType
AX = mybir.AxisListType
BF = ml_dtypes.bfloat16

H, D, S, DM, DI = 16, 64, 4096, 1024, 2816
NT, FT, IT, NQC = S // 128, DM // 128, DI // 128, S // 512
BUCKETS, MAX_POS, ATT_SPAN = 256, 512, 256
A2 = 2 * ATT_SPAN
SCALE = math.sqrt(D * 3)
INV_SCALE = 1.0 / SCALE
DW = 640
WB = 2 * DW + 1
WP = 1281
QMAX = 127.0
EPS = 1e-5
NBAND = 17
DBG = os.environ.get("KDBG", "") == "1"
COL_FP = DW + 520
COL_FN = DW - 520
RCOL_FP = (WB - 1) - COL_FP
RCOL_FN = (WB - 1) - COL_FN


def _bucket_table(dmin, dmax):
    d = np.arange(dmin, dmax + 1)
    rel = np.clip(d, -MAX_POS + 1, MAX_POS - 1)
    sign = np.sign(rel).astype(np.float64)
    abs_pos = np.maximum(np.abs(rel), 1)
    log_base = math.log((MAX_POS - 1) / (BUCKETS // 2))
    mid = BUCKETS // 2
    log_pos = np.ceil(
        np.log(abs_pos.astype(np.float32) / mid) / np.float32(log_base) * (mid - 1)
    ) + mid
    b = np.where(abs_pos < mid, abs_pos.astype(np.float64), log_pos) * sign
    return np.clip(b.astype(np.int64) + ATT_SPAN, 0, A2 - 1)


def _zone_ranges(kt, qc):
    out = []
    for i in range(4):
        dd = (4 * qc + i) - kt
        z = "near" if abs(dd) <= 4 else ("farP" if dd >= 5 else "farN")
        if out and out[-1][0] == z:
            out[-1] = (z, out[-1][1], 128 * (i + 1))
        else:
            out.append((z, 128 * i, 128 * (i + 1)))
    return out


def _band_window(kt):
    return max(0, 128 * kt - 512), min(S, 128 * kt + 640)


class _Sems:
    def __init__(self):
        self.cnt = {}
        self.h = {}

    def add(self, name, handle):
        self.h[name] = handle
        self.cnt[name] = 0

    def inc(self, instr, name, n=1):
        instr.then_inc(self.h[name], n)
        self.cnt[name] += n
        return self.cnt[name]

    def val(self, name):
        return self.cnt[name]


def build_nc():
    nc = bass.Bass("TRN2", target_bir_lowering=False)

    xfm = nc.dram_tensor("xfm", [DM, S], F32, kind="ExternalInput").ap()
    wqi = nc.dram_tensor("wqi", [DM, DM], F32, kind="ExternalInput").ap()
    wki = nc.dram_tensor("wki", [DM, DM], F32, kind="ExternalInput").ap()
    wvi = nc.dram_tensor("wvi", [DM, DM], F32, kind="ExternalInput").ap()
    bqi = nc.dram_tensor("bqi", [DM], F32, kind="ExternalInput").ap()
    bki = nc.dram_tensor("bki", [DM], F32, kind="ExternalInput").ap()
    bvi = nc.dram_tensor("bvi", [DM], F32, kind="ExternalInput").ap()
    relgT = nc.dram_tensor("relgT", [DM, WB], F32, kind="ExternalInput").ap()
    relgrT = nc.dram_tensor("relgrT", [DM, WB], F32, kind="ExternalInput").ap()
    wgi = nc.dram_tensor("wgi", [DI, DM], F32, kind="ExternalInput").ap()
    wui = nc.dram_tensor("wui", [DI, DM], F32, kind="ExternalInput").ap()
    wdi = nc.dram_tensor("wdi", [DM, DI], F32, kind="ExternalInput").ap()
    ident = nc.dram_tensor("ident", [128, 128], BF16, kind="ExternalInput").ap()
    clipx = nc.dram_tensor("clipx", [1], F32, kind="ExternalInput").ap()
    clipd = nc.dram_tensor("clipd", [1], F32, kind="ExternalInput").ap()
    outfm = nc.dram_tensor("outfm", [DM, S], F32, kind="ExternalOutput").ap()

    dk = "ExternalOutput" if DBG else "Internal"
    qd = nc.dram_tensor("qd", [DM, S], BF16, kind=dk).ap()
    kd = nc.dram_tensor("kd", [DM, S], BF16, kind=dk).ap()
    vhatd = nc.dram_tensor("vhatd", [S, 16 * 65], BF16, kind=dk).ap()
    posd = nc.dram_tensor("posd", [2 * DM, WB], BF16, kind=dk).ap()
    pgd = nc.dram_tensor("pgd", [H * S, WP], BF16, kind="Internal").ap()
    cgd = nc.dram_tensor("cgd", [H * S, WP], BF16, kind="Internal").ap()
    ead = nc.dram_tensor("ead", [H * 2, S], F32, kind=dk).ap()
    biasd = nc.dram_tensor("biasd", [H * 2, S], BF16, kind=dk).ap()
    attnd = nc.dram_tensor("attnd", [DM, S], BF16, kind=dk).ap()
    y1d = nc.dram_tensor("y1d", [DM, S], BF16, kind=dk).ap()
    hd = nc.dram_tensor("hd", [DI, S], BF16, kind=dk).ap()
    xqdbg = nc.dram_tensor("xqdbg", [DM, S], BF16, kind=dk).ap()
    gdbg = nc.dram_tensor("gdbg", [DI, S], F32, kind=dk).ap()
    scdbg = nc.dram_tensor("scdbg", [300], F32, kind=dk).ap()
    udbg = nc.dram_tensor("udbg", [DI, S], F32, kind=dk).ap()
    wgqd = nc.dram_tensor("wgqd", [DI, DM], BF16, kind=dk).ap()
    wuqd = nc.dram_tensor("wuqd", [DI, DM], BF16, kind=dk).ap()
    wdqd = nc.dram_tensor("wdqd", [DM, DI], BF16, kind="Internal").ap()
    bounced = nc.dram_tensor("bounced", [256], F32, kind="Internal").ap()
    cc1i = nc.dram_tensor("cc1i", [128], F32, kind="Internal").ap()
    cc1o = nc.dram_tensor("cc1o", [128], F32, kind="Internal").ap()
    cc2i = nc.dram_tensor("cc2i", [128], F32, kind="Internal").ap()
    cc2o = nc.dram_tensor("cc2o", [128], F32, kind="Internal").ap()

    sm = _Sems()
    stack = contextlib.ExitStack()
    for name in ["ld", "gl", "te", "se", "ve", "st", "qk", "ex", "pv", "bd",
                 "cmb", "bandc", "bandp", "cc", "ph", "vb", "ebl"]:
        sm.add(name, stack.enter_context(nc.semaphore(name)))

    with stack:
        block = stack.enter_context(nc.Block())
        biasq = stack.enter_context(nc.sbuf_tensor("biasq", [128, FT], F32))
        biask = stack.enter_context(nc.sbuf_tensor("biask", [128, FT], F32))
        bvrowb = stack.enter_context(nc.sbuf_tensor("bvrowb", [1, DM], BF16))
        onesrow_f = stack.enter_context(nc.sbuf_tensor("onesrow_f", [1, 128], F32))
        onescol_f = stack.enter_context(nc.sbuf_tensor("onescol_f", [128, 1], F32))
        onescolb = stack.enter_context(nc.sbuf_tensor("onescolb", [1, 128], BF16))
        zrow_b = stack.enter_context(nc.sbuf_tensor("zrow_b", [1, 512], BF16))
        zcol_b = stack.enter_context(nc.sbuf_tensor("zcol_b", [1, 65], BF16))
        sg_sb = stack.enter_context(nc.sbuf_tensor("sg_sb", [128, IT], F32))
        su_sb = stack.enter_context(nc.sbuf_tensor("su_sb", [128, IT], F32))
        sd_sb = stack.enter_context(nc.sbuf_tensor("sd_sb", [128, FT], F32))
        deqg = stack.enter_context(nc.sbuf_tensor("deqg", [128, IT], F32))
        dequ = stack.enter_context(nc.sbuf_tensor("dequ", [128, IT], F32))
        deqd = stack.enter_context(nc.sbuf_tensor("deqd", [128, FT], F32))
        ident_sb = stack.enter_context(nc.sbuf_tensor("ident_sb", [128, 128], BF16))
        eacc = stack.enter_context(nc.sbuf_tensor("eacc", [128, 4 * NT], F32))
        bcacc = stack.enter_context(nc.sbuf_tensor("bcacc", [128, 4 * NT], BF16))
        maxc = stack.enter_context(nc.sbuf_tensor("maxc", [128, 2], F32))
        rowred = stack.enter_context(nc.sbuf_tensor("rowred", [1, 128], F32))
        sca = stack.enter_context(nc.sbuf_tensor("sca", [1, 8], F32))
        qsc = stack.enter_context(nc.sbuf_tensor("qsc", [128, 4], F32))
        psb = [stack.enter_context(nc.psum_tensor(f"ps{i}", [128, 512], F32))
               for i in range(4)]
        psdm = stack.enter_context(nc.psum_tensor("psdm", [128, 2048], F32))

        with (
            nc.sbuf_tensor("wstage", [128, 2 * DI], F32) as wstage,
            nc.sbuf_tensor("wtmp", [128, DI], F32) as wtmp,
            nc.sbuf_tensor("wint", [128, DI], I32) as wint,
            nc.sbuf_tensor("wqb", [128, 2 * DI], BF16) as wqb,
            nc.sbuf_tensor("rcol", [128, 1], F32) as rcol,
        ):
            jobs = []
            for w_in, w_out, s_sb, ncols, ntl in ((wgi, wgqd, sg_sb, DM, IT),
                                                  (wui, wuqd, su_sb, DM, IT),
                                                  (wdi, wdqd, sd_sb, DI, FT)):
                for t in range(ntl):
                    jobs.append((w_in, w_out, s_sb, ncols, t))
            ld0, ve0, st0 = sm.val("ld"), sm.val("ve"), sm.val("st")

            @block.sync
            def _(sync):
                for j, (w_in, w_out, s_sb, ncols, t) in enumerate(jobs):
                    b = j % 2
                    if j >= 2:
                        sync.wait_ge(sm.h["ve"], ve0 + (j - 1))
                    sm.inc(sync.dma_start(out=wstage[:, b * DI:b * DI + ncols],
                                          in_=w_in[t * 128:(t + 1) * 128, :]), "ld", 16)

            @block.vector
            def _(v):
                for j, (w_in, w_out, s_sb, ncols, t) in enumerate(jobs):
                    b = j % 2
                    v.wait_ge(sm.h["ld"], ld0 + 16 * (j + 1))
                    if j >= 2:
                        v.wait_ge(sm.h["st"], st0 + 16 * (j - 1))
                    st_ap = wstage[:, b * DI:b * DI + ncols]
                    v.reduce_max(s_sb[:, t:t + 1], st_ap, axis=AX.X,
                                 apply_absolute_value=True)
                    v.drain()
                    v.tensor_scalar(s_sb[:, t:t + 1], s_sb[:, t:t + 1], 1.0 / QMAX,
                                    None, ALU.mult)
                    v.drain()
                    v.reciprocal(rcol[:, 0:1], s_sb[:, t:t + 1])
                    v.drain()
                    v.tensor_scalar(wtmp[:, 0:ncols], st_ap, rcol[:, 0:1], None, ALU.mult)
                    v.drain()
                    v.tensor_copy(wint[:, 0:ncols], wtmp[:, 0:ncols])
                    v.drain()
                    sm.inc(v.tensor_copy(wqb[:, b * DI:b * DI + ncols],
                                         wint[:, 0:ncols]), "ve", 1)

            @block.scalar
            def _(s):
                for j, (w_in, w_out, s_sb, ncols, t) in enumerate(jobs):
                    b = j % 2
                    s.wait_ge(sm.h["ve"], ve0 + (j + 1))
                    sm.inc(s.dma_start(out=w_out[t * 128:(t + 1) * 128, :],
                                       in_=wqb[:, b * DI:b * DI + ncols]), "st", 16)
            w0_st = sm.val("st")

            @block.vector
            def _(v):
                v.wait_ge(sm.h["st"], w0_st)
                sm.inc(v.drain(), "ph", 1)
        ph_w0 = sm.val("ph")

        with nc.sbuf_tensor("X_sb", [128, FT * S], BF16) as X_sb:
            @block.gpsimd
            def _(g):
                g.wait_ge(sm.h["ph"], ph_w0)
                for ft in range(FT):
                    sm.inc(g.dma_start(out=X_sb[:, ft * S:(ft + 1) * S],
                                       in_=xfm[ft * 128:(ft + 1) * 128, :]), "gl", 16)
                sm.inc(g.dma_start(out=bvrowb[0:1, :],
                                   in_=AP(bvi.tensor, 0, [[DM, 1], [1, DM]])), "gl", 16)
            a_glx = sm.val("gl")

            @block.sync
            def _(sync):
                sync.wait_ge(sm.h["ph"], ph_w0)
                sm.inc(sync.dma_start(out=biasq[:, :],
                                      in_=AP(bqi.tensor, 0, [[1, 128], [128, FT]]),
                                      allow_slow_non_contiguous=True), "ld", 16)
                sm.inc(sync.dma_start(out=biask[:, :],
                                      in_=AP(bki.tensor, 0, [[1, 128], [128, FT]]),
                                      allow_slow_non_contiguous=True), "ld", 16)
                sm.inc(sync.dma_start(out=ident_sb[:, :], in_=ident[:, :]), "ld", 16)
            a_ld = sm.val("ld")

            @block.vector
            def _(v):
                v.wait_ge(sm.h["ph"], ph_w0)
                v.memset(onesrow_f[0:1, :], 1.0)
                v.memset(onescol_f[:, 0:1], 1.0)
                v.memset(onescolb[0:1, :], 1.0)
                v.memset(zrow_b[0:1, :], 1.0)
                sm.inc(v.memset(zcol_b[0:1, :], 0.0), "ve", 1)
            a_ve = sm.val("ve")

            with (
                nc.sbuf_tensor("Wq_sb", [128, FT * DM], BF16) as Wq_sb,
                nc.sbuf_tensor("Wk_sb", [128, FT * DM], BF16) as Wk_sb,
                nc.sbuf_tensor("relc", [128, FT * 512], BF16) as relc,
                nc.sbuf_tensor("posst", [128, 2 * 512], BF16) as posst,
                nc.sbuf_tensor("qstage", [128, 2 * 512], BF16) as qstage,
                nc.sbuf_tensor("kstage", [128, 2 * 512], BF16) as kstage,
            ):
                @block.gpsimd
                def _(g):
                    for w_in, w_sb in ((wqi, Wq_sb), (wki, Wk_sb)):
                        for ht in range(FT):
                            sm.inc(g.dma_start(out=w_sb[:, ht * DM:(ht + 1) * DM],
                                               in_=w_in[ht * 128:(ht + 1) * 128, :]),
                                   "gl", 16)
                a_glw = sm.val("gl")

                dchw = [512, 512, WB - 1024]
                te0, se0, st1, gl1 = (sm.val(x) for x in ("te", "se", "st", "gl"))
                gl_at = {ci: gl1 + 16 * FT * (ci + 1) for ci in range(6)}

                @block.gpsimd
                def _(g):
                    for ci in range(6):
                        side, dc = divmod(ci, 3)
                        w = dchw[dc]
                        src = relgT if side == 0 else relgrT
                        if ci >= 1:
                            g.wait_ge(sm.h["te"], te0 + FT * ci)
                        for ht in range(FT):
                            sm.inc(g.dma_start(
                                out=relc[:, ht * 512:ht * 512 + w],
                                in_=src[ht * 128:(ht + 1) * 128, dc * 512:dc * 512 + w]),
                                "gl", 16)

                @block.tensor
                def _(t):
                    t.wait_ge(sm.h["gl"], a_glw)
                    t.wait_ge(sm.h["ld"], a_ld)
                    for ci in range(6):
                        side, dc = divmod(ci, 3)
                        w = dchw[dc]
                        w_sb = Wq_sb if side == 0 else Wk_sb
                        t.wait_ge(sm.h["gl"], gl_at[ci])
                        for ft in range(FT):
                            j = ci * FT + ft
                            pp = psb[j % 2]
                            if j >= 2:
                                t.wait_ge(sm.h["se"], se0 + j - 1)
                            for ht in range(FT):
                                mmi = t.matmul(
                                    pp[:, 0:w],
                                    w_sb[:, ht * DM + ft * 128: ht * DM + (ft + 1) * 128],
                                    relc[:, ht * 512:ht * 512 + w],
                                    start=(ht == 0), stop=(ht == FT - 1))
                            sm.inc(mmi, "te", 1)
                a_tep = sm.val("te")

                @block.vector
                def _(v):
                    for ci in range(6):
                        side, dc = divmod(ci, 3)
                        w = dchw[dc]
                        bcol = biasq if side == 0 else biask
                        for ft in range(FT):
                            j = ci * FT + ft
                            pp = psb[j % 2]
                            v.wait_ge(sm.h["te"], te0 + j + 1)
                            if j >= 2:
                                v.wait_ge(sm.h["st"], st1 + 16 * (j - 1))
                            sm.inc(v.tensor_scalar(
                                posst[:, (j % 2) * 512:(j % 2) * 512 + w],
                                pp[:, 0:w], bcol[:, ft:ft + 1], None, ALU.add),
                                   "se", 1)

                @block.sync
                def _(sync):
                    for ci in range(6):
                        side, dc = divmod(ci, 3)
                        w = dchw[dc]
                        for ft in range(FT):
                            j = ci * FT + ft
                            sync.wait_ge(sm.h["se"], se0 + j + 1)
                            sm.inc(sync.dma_start(
                                out=posd[side * DM + ft * 128: side * DM + (ft + 1) * 128,
                                         dc * 512:dc * 512 + w],
                                in_=posst[:, (j % 2) * 512:(j % 2) * 512 + w]), "st", 16)

                te1, se1, st2 = sm.val("te"), sm.val("se"), sm.val("st")
                projjobs = [(w, ft, ch) for w in ("k", "q") for ft in range(FT)
                            for ch in range(8)]
                qof = {}
                nq = 0
                for j, (which, ft, ch) in enumerate(projjobs):
                    if which == "q":
                        qof[j] = nq
                        nq += 1

                @block.tensor
                def _(t):
                    t.wait_ge(sm.h["gl"], a_glx)
                    t.wait_ge(sm.h["se"], se1)
                    for j, (which, ft, ch) in enumerate(projjobs):
                        pp = psb[j % 2]
                        if j >= 2:
                            t.wait_ge(sm.h["se"], se1 + j - 1)
                        w_sb = Wk_sb if which == "k" else Wq_sb
                        for ht in range(FT):
                            mmi = t.matmul(
                                pp[:, :],
                                w_sb[:, ht * DM + ft * 128: ht * DM + (ft + 1) * 128],
                                X_sb[:, ht * S + ch * 512: ht * S + (ch + 1) * 512],
                                start=(ht == 0), stop=(ht == FT - 1))
                        sm.inc(mmi, "te", 1)

                @block.vector
                def _(v):
                    for j, (which, ft, ch) in enumerate(projjobs):
                        pp = psb[j % 2]
                        v.wait_ge(sm.h["te"], te1 + j + 1)
                        bcol = biask if which == "k" else biasq
                        stg = kstage if which == "k" else qstage
                        qb = j % 2
                        if j >= 4:
                            v.wait_ge(sm.h["st"], st2 + 16 * (j - 3))
                        dst = stg[:, qb * 512:(qb + 1) * 512]
                        sm.inc(v.tensor_scalar(dst, pp[:, :], bcol[:, ft:ft + 1],
                                               None, ALU.add), "se", 1)
                a_seq = sm.val("se")

                @block.sync
                def _(sync):
                    for j, (which, ft, ch) in enumerate(projjobs):
                        qb = j % 2
                        dstd = kd if which == "k" else qd
                        stg = kstage if which == "k" else qstage
                        sync.wait_ge(sm.h["se"], se1 + j + 1)
                        sm.inc(sync.dma_start(
                            out=dstd[ft * 128:(ft + 1) * 128, ch * 512:(ch + 1) * 512],
                            in_=stg[:, qb * 512:(qb + 1) * 512]), "st", 16)
                a_stq = sm.val("st")

                @block.vector
                def _(v):
                    v.wait_ge(sm.h["se"], a_seq)
                    v.wait_ge(sm.h["st"], a_stq)
                    sm.inc(v.drain(), "ph", 1)
            ph_a1 = sm.val("ph")

            with (
                nc.sbuf_tensor("Wv_sb", [128, FT * DM], BF16) as Wv_sb,
                nc.sbuf_tensor("vstage", [128, 2 * 1040], BF16) as vstage,
            ):
                tev0, vev0, stv0 = (sm.val(x) for x in ("te", "ve", "st"))

                @block.gpsimd
                def _(g):
                    g.wait_ge(sm.h["ph"], ph_a1)
                    for ht in range(FT):
                        sm.inc(g.dma_start(out=Wv_sb[:, ht * DM:(ht + 1) * DM],
                                           in_=wvi[ht * 128:(ht + 1) * 128, :]), "gl", 16)
                a_glv = sm.val("gl")

                vjobs = [(tt, fc) for tt in range(NT) for fc in range(2)]

                @block.tensor
                def _(t):
                    t.wait_ge(sm.h["gl"], a_glv)
                    t.wait_ge(sm.h["ve"], a_ve)
                    for j, (tt, fc) in enumerate(vjobs):
                        pp = psb[2 + j % 2]
                        if j >= 2:
                            t.wait_ge(sm.h["ve"], vev0 + j - 1)
                        for ht in range(FT):
                            t.matmul(pp[:, :],
                                     X_sb[:, ht * S + tt * 128: ht * S + (tt + 1) * 128],
                                     Wv_sb[:, ht * DM + fc * 512: ht * DM + (fc + 1) * 512],
                                     start=(ht == 0), stop=False)
                        sm.inc(t.matmul(pp[:, :], onescolb[0:1, :],
                                        bvrowb[0:1, fc * 512:(fc + 1) * 512],
                                        start=False, stop=True), "te", 1)

                @block.vector
                def _(v):
                    for j, (tt, fc) in enumerate(vjobs):
                        pp = psb[2 + j % 2]
                        vb = tt % 2
                        v.wait_ge(sm.h["te"], tev0 + j + 1)
                        if tt >= 2 and fc == 0:
                            v.wait_ge(sm.h["st"], stv0 + 16 * (tt - 1))
                        dst = AP(vstage, vb * 1040 + fc * 8 * 65,
                                 [[2080, 128], [65, 8], [1, 64]])
                        v.tensor_copy(dst, AP(pp, 0, [[512, 128], [64, 8], [1, 64]]))
                        if fc == 1:
                            v.drain()
                            sm.inc(v.memset(
                                AP(vstage, vb * 1040 + 64, [[2080, 128], [65, 16]]),
                                1.0), "ve", 1)
                        else:
                            sm.inc(v.drain(), "ve", 1)

                @block.sync
                def _(sync):
                    for tt in range(NT):
                        vb = tt % 2
                        sync.wait_ge(sm.h["ve"], vev0 + 2 * tt + 2)
                        sm.inc(sync.dma_start(out=vhatd[tt * 128:(tt + 1) * 128, :],
                                              in_=vstage[:, vb * 1040:(vb + 1) * 1040]),
                               "st", 16)
                a_stv = sm.val("st")

                @block.vector
                def _(v):
                    v.wait_ge(sm.h["st"], a_stv)
                    sm.inc(v.drain(), "ph", 1)
            ph_a2 = sm.val("ph")
        with (
            nc.sbuf_tensor("qftA", [128, S], BF16) as qftA,
            nc.sbuf_tensor("kftA", [128, S], BF16) as kftA,
            nc.sbuf_tensor("posq_sb", [128, WB], BF16) as posq_sb,
            nc.sbuf_tensor("posk_sb", [128, WB], BF16) as posk_sb,
            nc.sbuf_tensor("cpst", [128, 4 * WP], BF16) as cpst,
        ):
            dchw = [512, 512, WB - 1024]
            ea_st_prev = None
            a3_se_prev = None
            for ft in range(FT):
                ld0, te0, ve0, se0, st0 = (sm.val(x) for x in
                                           ("ld", "te", "ve", "se", "st"))
                cpjobs = [(side, tl, hh) for side in range(2) for tl in range(NT)
                          for hh in range(2)]

                @block.sync
                def _(sync, ft=ft, ve0=ve0):
                    if ft == 0:
                        sync.wait_ge(sm.h["ph"], ph_a2)
                    else:
                        sync.wait_ge(sm.h["ve"], ve0)
                    sm.inc(sync.dma_start(out=qftA[:, :],
                                          in_=qd[ft * 128:(ft + 1) * 128, :]), "ld", 16)
                    sm.inc(sync.dma_start(out=kftA[:, :],
                                          in_=kd[ft * 128:(ft + 1) * 128, :]), "ld", 16)
                    sm.inc(sync.dma_start(out=posq_sb[:, :],
                                          in_=posd[ft * 128:(ft + 1) * 128, :]), "ld", 16)
                    sm.inc(sync.dma_start(out=posk_sb[:, :],
                                          in_=posd[DM + ft * 128: DM + (ft + 1) * 128, :]),
                           "ld", 16)
                a3_ld = sm.val("ld")

                jps = [(side, tl) for side in range(2) for tl in range(NT)]

                def dve_idx(jp, dc, hh):
                    return 6 * jp + (1 + hh if dc == 0 else 3 + hh)

                def act_idx(jp, hh):
                    return 2 * jp + 1 + hh

                @block.tensor
                def _(t, ft=ft, te0=te0, ve0=ve0, se0=se0, a3_ld=a3_ld, jps=jps):
                    t.wait_ge(sm.h["ld"], a3_ld)
                    if ft == 0:
                        t.wait_ge(sm.h["ph"], ph_a2)
                    for jp, (side, tl) in enumerate(jps):
                        lhs_sb = kftA if side == 0 else qftA
                        lcols = tl * 128
                        rhs_sb = posq_sb if side == 0 else posk_sb
                        for dc in range(3):
                            for hh in range(2):
                                m = jp * 6 + dc * 2 + hh
                                w = dchw[dc]
                                pp = psb[m % 4]
                                if m >= 4:
                                    jp_, r = divmod(m - 4, 6)
                                    dc_, hh_ = r // 2, r % 2
                                    if dc_ == 1:
                                        t.wait_ge(sm.h["se"],
                                                  se0 + act_idx(jp_, hh_))
                                    else:
                                        t.wait_ge(sm.h["ve"],
                                                  ve0 + dve_idx(jp_, dc_, hh_))
                                mmi = t.matmul(
                                    pp[:, 0:w],
                                    lhs_sb[64 * hh:64 * (hh + 1),
                                           lcols:lcols + 128],
                                    rhs_sb[64 * hh:64 * (hh + 1),
                                           dc * 512:dc * 512 + w],
                                    start=True, stop=True,
                                    tile_position=(64 * hh, 0))
                                sm.inc(mmi, "te", 1)

                @block.vector
                def _(v, ft=ft, te0=te0, ve0=ve0, se0=se0, st0=st0, jps=jps,
                      ea_st_prev=ea_st_prev, a3_se_prev=a3_se_prev):
                    if ea_st_prev is not None:
                        v.wait_ge(sm.h["st"], ea_st_prev)
                        v.wait_ge(sm.h["se"], a3_se_prev)
                    for jp, (side, tl) in enumerate(jps):
                        for dc in (0, 2):
                            for hh in range(2):
                                j = 2 * jp + hh
                                stbase = (j % 4) * WP
                                if dc == 0 and j >= 4:
                                    v.wait_ge(sm.h["st"], st0 + 16 * (j - 3))
                                m = jp * 6 + dc * 2 + hh
                                w = dchw[dc]
                                pp = psb[m % 4]
                                v.wait_ge(sm.h["te"], te0 + m + 1)
                                sm.inc(v.tensor_copy(
                                    cpst[:, stbase + dc * 512:
                                         stbase + dc * 512 + w],
                                    pp[:, 0:w]), "ve", 1)
                        v.drain()
                        for hh in range(2):
                            j = 2 * jp + hh
                            stbase = (j % 4) * WP
                            base = hh * 2 * NT
                            if side == 0:
                                v.tensor_copy(
                                    bcacc[:, base + tl: base + tl + 1],
                                    cpst[:, stbase + COL_FP: stbase + COL_FP + 1])
                                sm.inc(v.tensor_copy(
                                    bcacc[:, base + NT + tl: base + NT + tl + 1],
                                    cpst[:, stbase + COL_FN: stbase + COL_FN + 1]),
                                    "ve", 1)
                            else:
                                v.tensor_copy(
                                    eacc[:, base + tl: base + tl + 1],
                                    cpst[:, stbase + RCOL_FP: stbase + RCOL_FP + 1])
                                sm.inc(v.tensor_copy(
                                    eacc[:, base + NT + tl: base + NT + tl + 1],
                                    cpst[:, stbase + RCOL_FN: stbase + RCOL_FN + 1]),
                                    "ve", 1)

                @block.scalar
                def _(s, ft=ft, te0=te0, st0=st0, jps=jps):
                    for jp, (side, tl) in enumerate(jps):
                        for hh in range(2):
                            j = 2 * jp + hh
                            stbase = (j % 4) * WP
                            if j >= 4:
                                s.wait_ge(sm.h["st"], st0 + 16 * (j - 3))
                            m = jp * 6 + 2 + hh
                            pp = psb[m % 4]
                            s.wait_ge(sm.h["te"], te0 + m + 1)
                            sm.inc(s.copy(cpst[:, stbase + 512: stbase + 1024],
                                          pp[:, 0:512]), "se", 1)

                @block.scalar
                def _(s, ft=ft, ve0=ve0, ea_st_prev=ea_st_prev, jps=jps):
                    if ea_st_prev is not None:
                        s.wait_ge(sm.h["st"], ea_st_prev)
                    s.wait_ge(sm.h["ve"], ve0 + 6 * len(jps))
                    s.activation(bcacc[:, :], bcacc[:, :], AF.Exp,
                                 scale=INV_SCALE)
                    sm.inc(s.activation(eacc[:, :], eacc[:, :], AF.Exp,
                                        scale=INV_SCALE), "se", 1)
                a3_se = sm.val("se")

                @block.sync
                def _(sync, ft=ft, ve0=ve0, se0=se0, a3_se=a3_se, jps=jps):
                    for jp, (side, tl) in enumerate(jps):
                        for hh in range(2):
                            j = 2 * jp + hh
                            h = 2 * ft + hh
                            stbase = (j % 4) * WP
                            sync.wait_ge(sm.h["ve"], ve0 + 6 * jp + 3 + hh)
                            sync.wait_ge(sm.h["se"], se0 + act_idx(jp, hh))
                            dstd = pgd if side == 0 else cgd
                            sm.inc(sync.dma_start(
                                out=AP(dstd.tensor, (h * S + tl * 128) * WP,
                                       [[WP, 128], [1, WB]]),
                                in_=cpst[:, stbase:stbase + WB]), "st", 16)
                    sync.wait_ge(sm.h["se"], a3_se)
                    for hh in range(2):
                        h = 2 * ft + hh
                        base = hh * 2 * NT
                        for side in range(2):
                            sm.inc(sync.dma_start(
                                out=AP(ead.tensor, (h * 2 + side) * S,
                                       [[1, 128], [128, NT]]),
                                in_=eacc[:, base + side * NT: base + (side + 1) * NT],
                                allow_slow_non_contiguous=True), "st", 16)
                            sm.inc(sync.dma_start(
                                out=AP(biasd.tensor, (h * 2 + side) * S,
                                       [[NT, 128], [1, NT]]),
                                in_=bcacc[:, base + side * NT: base + (side + 1) * NT]),
                                "st", 16)
                ea_st_prev = sm.val("st")
                a3_se_prev = a3_se

            @block.vector
            def _(v):
                v.wait_ge(sm.h["st"], sm.val("st"))
                sm.inc(v.drain(), "ph", 1)
        ph_a3 = sm.val("ph")

        with (
            nc.sbuf_tensor("qhB", [128, 2 * S], BF16) as qhB,
            nc.sbuf_tensor("khB", [128, 2 * S], BF16) as khB,
            nc.sbuf_tensor("vh_sb", [128, 2 * NT * 65], BF16) as vh_sb,
            nc.sbuf_tensor("earow", [1, 4 * S], F32) as earow,
            nc.sbuf_tensor("band_sb", [128, NBAND * 1152], BF16) as band_sb,
            nc.sbuf_tensor("expS", [128, 2 * 2048], BF16) as expS,
            nc.sbuf_tensor("vhp_sb", [128, 2 * NT * 65], BF16) as vhp_sb,
            nc.sbuf_tensor("vhn_sb", [128, 2 * NT * 65], BF16) as vhn_sb,
            nc.sbuf_tensor("ebraw", [128, 4 * NT], F32) as ebraw,
            nc.sbuf_tensor("zsb", [128, 2 * 3 * 512], F32) as zsb,
            nc.sbuf_tensor("csum", [128, 2 * 512], F32) as csum,
            nc.sbuf_tensor("rrow", [1, 2 * 512], F32) as rrow,
            nc.sbuf_tensor("attnst", [128, 2 * 512], BF16) as attnst,
        ):
            slot_need = {i: 0 for i in range(NBAND)}
            bandc_at, bandp_at, issued = {}, {}, set()
            qk_at, ex_at, pv_after, bd_at = {}, {}, {}, {}
            cmb_at = {}
            comb_done = {}
            attnbuf_need = {0: 0, 1: 0}
            ps5_need = [0]
            qft_ld_at = {}
            vh_ld_at = {}
            vb_at = {}
            eb_ld_at = {}
            pending_combine = []

            def emit_combine_step(step):
                (ch, cqc, chb) = pending_combine[0]
                zb = (cqc % 2) * 1536
                cb = (cqc % 2) * 512
                if step == 0:
                    @block.tensor
                    def _(t, ch=ch, cqc=cqc, chb=chb, zb=zb):
                        t.wait_ge(sm.h["cmb"], cmb_at[(ch, cqc)])
                        t.wait_ge(sm.h["ve"], ps5_need[0])
                        sm.inc(t.matmul(psb[3][0:65, 0:512], onesrow_f[0:1, 0:65],
                                        earow[0:1, chb * 2 * S + cqc * 512:
                                              chb * 2 * S + (cqc + 1) * 512],
                                        start=True, stop=True), "te", 1)
                    comb_te[0] = sm.val("te")
                elif step == 1:
                    @block.vector
                    def _(v, zb=zb, t1=comb_te[0]):
                        v.wait_ge(sm.h["te"], t1)
                        sm.inc(v.tensor_tensor(zsb[0:65, zb + 512:zb + 1024],
                                               zsb[0:65, zb + 512:zb + 1024],
                                               psb[3][0:65, 0:512], ALU.mult),
                               "ve", 1)
                    comb_ve[0] = sm.val("ve")
                elif step == 2:
                    @block.tensor
                    def _(t, ch=ch, cqc=cqc, chb=chb, v1=comb_ve[0]):
                        t.wait_ge(sm.h["ve"], v1)
                        sm.inc(t.matmul(psb[3][0:65, 0:512], onesrow_f[0:1, 0:65],
                                        earow[0:1, chb * 2 * S + S + cqc * 512:
                                              chb * 2 * S + S + (cqc + 1) * 512],
                                        start=True, stop=True), "te", 1)
                    comb_te[0] = sm.val("te")
                elif step == 3:
                    @block.vector
                    def _(v, zb=zb, cb=cb, t2=comb_te[0]):
                        v.wait_ge(sm.h["te"], t2)
                        v.tensor_tensor(zsb[0:65, zb + 1024:zb + 1536],
                                        zsb[0:65, zb + 1024:zb + 1536],
                                        psb[3][0:65, 0:512], ALU.mult)
                        v.drain()
                        v.tensor_tensor(csum[0:65, cb:cb + 512],
                                        zsb[0:65, zb:zb + 512],
                                        zsb[0:65, zb + 512:zb + 1024], ALU.add)
                        v.drain()
                        v.tensor_tensor(csum[0:65, cb:cb + 512],
                                        csum[0:65, cb:cb + 512],
                                        zsb[0:65, zb + 1024:zb + 1536], ALU.add)
                        v.drain()
                        v.reciprocal(rrow[0:1, cb:cb + 512],
                                     csum[64:65, cb:cb + 512])
                        sm.inc(v.drain(), "ve", 1)
                    comb_ve[0] = sm.val("ve")
                elif step == 4:
                    @block.tensor
                    def _(t, cb=cb, s2=comb_ve[0]):
                        t.wait_ge(sm.h["ve"], s2)
                        sm.inc(t.matmul(psb[3][0:64, 0:512], onesrow_f[0:1, 0:64],
                                        rrow[0:1, cb:cb + 512],
                                        start=True, stop=True), "te", 1)
                    comb_te[0] = sm.val("te")
                else:
                    (chx, cqcx, chbx) = pending_combine.pop(0)

                    @block.vector
                    def _(v, cb=cb, t3=comb_te[0], bneed=attnbuf_need[cqc % 2]):
                        v.wait_ge(sm.h["te"], t3)
                        v.wait_ge(sm.h["st"], bneed)
                        v.tensor_tensor(csum[0:64, cb:cb + 512],
                                        csum[0:64, cb:cb + 512],
                                        psb[3][0:64, 0:512], ALU.mult)
                        v.drain()
                        sm.inc(v.tensor_copy(
                            attnst[0:64, (cqc % 2) * 512:(cqc % 2 + 1) * 512],
                            csum[0:64, cb:cb + 512]), "ve", 1)
                    ps5_need[0] = sm.val("ve")
                    comb_done[(chx, cqcx)] = sm.val("ve")
                    veA = sm.val("ve")

                    @block.sync
                    def _(sync, chx=chx, cqcx=cqcx, veA=veA):
                        sync.wait_ge(sm.h["ve"], veA)
                        sm.inc(sync.dma_start(
                            out=attnd[64 * chx:64 * (chx + 1),
                                      cqcx * 512:(cqcx + 1) * 512],
                            in_=attnst[0:64, (cqcx % 2) * 512:(cqcx % 2 + 1) * 512]),
                            "st", 16)
                    attnbuf_need[cqcx % 2] = sm.val("st")

            comb_te = [0]
            comb_ve = [0]
            comb_se = [0]

            def first_qc(kt):
                return max(0, (kt - 7 + 3) // 4)

            def kts_first_at(qc):
                return [kt for kt in range(NT) if first_qc(kt) == qc]

            for h in range(H):
                hh = h % 2
                ft = h // 2
                hb = h % 2
                if True:

                    @block.sync
                    def _(sync, h=h, hb=hb, ft=ft):
                        if h == 0:
                            sync.wait_ge(sm.h["ph"], ph_a3)
                        if h >= 2:
                            sync.wait_ge(sm.h["pv"], pv_after[(h - 2, NQC - 1, NT - 1)])
                            sync.wait_ge(sm.h["qk"], qk_at[(h - 2, NQC - 1, NT - 1)])
                        for par in range(2):
                            sm.inc(sync.dma_start(
                                out=qhB[64 * par:64 * par + 64, hb * S:(hb + 1) * S],
                                in_=qd[ft * 128 + 64 * hh: ft * 128 + 64 * hh + 64, :]),
                                "ld", 16)
                            sm.inc(sync.dma_start(
                                out=khB[64 * par:64 * par + 64, hb * S:(hb + 1) * S],
                                in_=kd[ft * 128 + 64 * hh: ft * 128 + 64 * hh + 64, :]),
                                "ld", 16)
                        sm.inc(sync.dma_start(
                            out=vh_sb[:, hb * NT * 65:(hb + 1) * NT * 65],
                            in_=AP(vhatd.tensor, h * 65,
                                   [[1040, 128], [1040 * 128, NT], [1, 65]])), "ld", 16)
                        sm.inc(sync.dma_start(
                            out=earow[0:1, hb * 2 * S:(hb + 1) * 2 * S],
                            in_=AP(ead.tensor, h * 2 * S, [[2 * S, 1], [1, 2 * S]])),
                               "ld", 16)
                    vh_ld_at[h] = sm.val("ld")

                    @block.gpsimd
                    def _(g, h=h, hb=hb):
                        if h >= 2:
                            g.wait_ge(sm.h["vb"], vb_at[h - 2])
                        else:
                            g.wait_ge(sm.h["ph"], ph_a3)
                        for side in range(2):
                            sm.inc(g.dma_start(
                                out=ebraw[:, hb * 2 * NT + side * NT:
                                          hb * 2 * NT + (side + 1) * NT],
                                in_=AP(biasd.tensor, (h * 2 + side) * S,
                                       [[NT, 128], [1, NT]])), "ebl", 16)
                    eb_ld_at[h] = sm.val("ebl")

                    @block.vector
                    def _(v, h=h, hb=hb):
                        v.wait_ge(sm.h["ld"], vh_ld_at[h])
                        v.wait_ge(sm.h["ebl"], eb_ld_at[h])
                        op = None
                        for side in range(2):
                            dst = vhp_sb if side == 0 else vhn_sb
                            for kt in range(NT):
                                op = v.tensor_scalar(
                                    dst[:, hb * NT * 65 + kt * 65:
                                        hb * NT * 65 + (kt + 1) * 65],
                                    vh_sb[:, hb * NT * 65 + kt * 65:
                                          hb * NT * 65 + (kt + 1) * 65],
                                    ebraw[:, hb * 2 * NT + side * NT + kt:
                                          hb * 2 * NT + side * NT + kt + 1],
                                    None, ALU.mult)
                        sm.inc(op, "vb", 1)
                    vb_at[h] = sm.val("vb")

                    for qc in range(NQC):
                        want = []
                        if qc == 0:
                            want += kts_first_at(0) + kts_first_at(1)
                        elif qc < NQC - 1:
                            want += kts_first_at(qc + 1)
                        for kt in [k for k in want if (h, k) not in issued]:
                            issued.add((h, kt))
                            slot = kt % NBAND
                            qlo, qhi = _band_window(kt)
                            Fw = qhi - qlo
                            k0 = 128 * kt
                            need = slot_need[slot]

                            @block.sync
                            def _(sync, h=h, kt=kt, slot=slot, qlo=qlo, Fw=Fw,
                                  k0=k0, need=need):
                                sync.wait_ge(sm.h["qk"], need)
                                src = AP(cgd.tensor,
                                         (h * S + qlo) * WP + (DW + k0 - qlo),
                                         [[WP - 1, Fw], [1, 128]])
                                sm.inc(sync.dma_start(
                                    out=band_sb[:, slot * 1152: slot * 1152 + Fw],
                                    in_=src, transpose=True), "bandc", 16)
                            bandc_at[(h, kt)] = sm.val("bandc")

                            @block.gpsimd
                            def _(g, h=h, kt=kt, slot=slot, qlo=qlo, Fw=Fw, k0=k0):
                                g.wait_ge(sm.h["bandc"], bandc_at[(h, kt)])
                                src = AP(pgd.tensor,
                                         (h * S + k0) * WP + (qlo - k0 + DW),
                                         [[WP - 1, 128], [1, Fw]])
                                sm.inc(g.dma_start(
                                    out=band_sb[:, slot * 1152: slot * 1152 + Fw],
                                    in_=src, accum_op=ALU.add), "bandp", 16)
                            bandp_at[(h, kt)] = sm.val("bandp")

                        if qc == 0:
                            @block.tensor
                            def _(t, h=h):
                                t.wait_ge(sm.h["ld"], vh_ld_at[h])
                                t.wait_ge(sm.h["vb"], vb_at[h])

                        def emit_pv(t, kt2, h=h, qc=qc, hb=hb):
                            eb = ((kt2 // 4) % 2) * 2048 + (kt2 % 4) * 512
                            for z, c0, c1 in _zone_ranges(kt2, qc):
                                zi = {"near": 0, "farP": 1, "farN": 2}[z]
                                vsrc = (vh_sb, vhp_sb, vhn_sb)[zi]
                                sm.inc(t.matmul(
                                    psb[zi][0:65, c0:c1],
                                    vsrc[:, hb * NT * 65 + kt2 * 65:
                                         hb * NT * 65 + (kt2 + 1) * 65],
                                    expS[:, eb + c0: eb + c1],
                                    start=False, stop=True), "pv", 1)
                            pv_after[(h, qc, kt2)] = sm.val("pv")

                        def emit_qk_mm(t, kt, h=h, qc=qc, hb=hb):
                            sbank = psdm
                            half = (kt % 4) * 512
                            par = kt % 2
                            mmi = t.matmul(
                                sbank[:, half:half + 512],
                                khB[64 * par:64 * par + 64,
                                    hb * S + kt * 128: hb * S + (kt + 1) * 128],
                                qhB[64 * par:64 * par + 64,
                                    hb * S + qc * 512: hb * S + (qc + 1) * 512],
                                start=True, stop=True,
                                tile_position=(64 * par, 0))
                            if not any(z == "near"
                                       for z, _, _ in _zone_ranges(kt, qc)):
                                sm.inc(mmi, "qk", 1)
                                qk_at[(h, qc, kt)] = sm.val("qk")

                        def emit_qk_post(t, kt, h=h, qc=qc, hb=hb):
                            sbank = psdm
                            half = (kt % 4) * 512
                            qlo, qhi = _band_window(kt)
                            near = [(c0, c1)
                                    for z, c0, c1 in _zone_ranges(kt, qc)
                                    if z == "near"]
                            if not near:
                                return
                            slot = kt % NBAND
                            boff = slot * 1152 + (qc * 512 - qlo)
                            t.wait_ge(sm.h["bandp"], bandp_at[(h, kt)])
                            mmi = None
                            for c0, c1 in near:
                                mmi = t.matmul(
                                    sbank[:, half + c0: half + c1], ident_sb[:, :],
                                    band_sb[:, boff + c0: boff + c1],
                                    start=False, stop=True)
                            sm.inc(mmi, "qk", 1)
                            qk_at[(h, qc, kt)] = sm.val("qk")
                            slot_need[kt % NBAND] = sm.val("qk")

                        NP = NT // 2
                        ex_base = sm.val("ex")
                        for kt in range(NT):
                            ex_at[(h, qc, kt)] = ex_base + kt // 4 + 1
                        comb_sched = {1: [0], 2: [1, 2], 3: [3], 8: [4], 10: [5]}
                        for p in range(NP):
                            if pending_combine:
                                for st_ in comb_sched.get(p, []):
                                    emit_combine_step(st_)
                            if p == 1:
                                @block.tensor
                                def _(t, h=h, qc=qc):
                                    if (h, qc) != (0, 0):
                                        pq = ((h, qc - 1) if qc > 0
                                              else (h - 1, NQC - 1))
                                        t.wait_ge(sm.h["cmb"], cmb_at[pq])
                                    for z in range(3):
                                        sm.inc(t.matmul(psb[z][0:65, 0:512],
                                                        zcol_b[0:1, :], zrow_b[0:1, :],
                                                        start=True, stop=True),
                                               "te", 1)

                            @block.tensor
                            def _(t, h=h, qc=qc, p=p):
                                prev = None
                                if p >= 2:
                                    prev = ex_at[(h, qc, 2 * (p - 2) + 1)]
                                elif qc > 0:
                                    prev = ex_at[(h, qc - 1, NT - 1)]
                                elif h > 0:
                                    prev = ex_at[(h - 1, NQC - 1, NT - 1)]
                                if prev is not None:
                                    t.wait_ge(sm.h["ex"], prev)
                                emit_qk_mm(t, 2 * p)
                                emit_qk_mm(t, 2 * p + 1)
                                emit_qk_post(t, 2 * p)
                                emit_qk_post(t, 2 * p + 1)
                                if p >= 2:
                                    emit_pv(t, 2 * (p - 2))
                                    emit_pv(t, 2 * (p - 2) + 1)

                        @block.tensor
                        def _(t, h=h, qc=qc):
                            t.wait_ge(sm.h["ex"], ex_at[(h, qc, NT - 1)])
                            for kt2 in (NT - 4, NT - 3, NT - 2, NT - 1):
                                emit_pv(t, kt2)

                        @block.scalar
                        def _(s, h=h, qc=qc):
                            for g in range(NP // 2):
                                prev = None
                                if g >= 2:
                                    prev = pv_after[(h, qc, 4 * (g - 2) + 3)]
                                elif qc > 0:
                                    prev = pv_after[(h, qc - 1,
                                                     NT + 4 * (g - 2) + 3)]
                                elif h > 0:
                                    prev = pv_after[(h - 1, NQC - 1,
                                                     NT + 4 * (g - 2) + 3)]
                                if prev is not None:
                                    s.wait_ge(sm.h["pv"], prev)
                                s.wait_ge(sm.h["qk"],
                                          max(qk_at[(h, qc, 4 * g + k)]
                                              for k in range(4)))
                                eb = (g % 2) * 2048
                                act = s.activation(expS[:, eb: eb + 2048],
                                                   psdm[:, :], AF.Exp,
                                                   scale=INV_SCALE)
                                sm.inc(act, "ex", 1)
                            s.wait_ge(sm.h["pv"], pv_after[(h, qc, NT - 1)])
                            if (h, qc) != (0, 0):
                                pq = (h, qc - 1) if qc > 0 else (h - 1, NQC - 1)
                                s.wait_ge(sm.h["ve"], comb_done[pq])
                            zb = (qc % 2) * 1536
                            s.copy(zsb[0:65, zb:zb + 512], psb[0][0:65, 0:512])
                            s.copy(zsb[0:65, zb + 512:zb + 1024], psb[1][0:65, 0:512])
                            sm.inc(s.copy(zsb[0:65, zb + 1024:zb + 1536],
                                          psb[2][0:65, 0:512]), "cmb", 1)
                        assert sm.val("ex") == ex_base + NP // 2
                        cmb_at[(h, qc)] = sm.val("cmb")

                        pending_combine.append((h, qc, hb))


            while pending_combine:
                for st_ in range(6):
                    emit_combine_step(st_)

            @block.vector
            def _(v):
                v.wait_ge(sm.h["st"], sm.val("st"))
                sm.inc(v.drain(), "ph", 1)
        ph_b = sm.val("ph")

        def emit_ln(src_sb, sq_sb, strow, track_max, out_sb, qc, st_wait):
            ve_in = sm.val("ve")

            @block.scalar
            def _(s, ve_in=ve_in):
                s.wait_ge(sm.h["ve"], ve_in)
                sm.inc(s.activation(sq_sb[:, :], src_sb[:, :], AF.Square), "se", 1)
            seq = sm.val("se")

            @block.tensor
            def _(t, ve_in=ve_in, seq=seq):
                t.wait_ge(sm.h["ve"], ve_in)
                for ftt in range(FT):
                    sm.inc(t.matmul(psb[0][0:1, 0:512], onescol_f[:, 0:1],
                                    src_sb[:, ftt * 512:(ftt + 1) * 512],
                                    start=(ftt == 0), stop=(ftt == FT - 1)), "te", 1)
                t.wait_ge(sm.h["se"], seq)
                for ftt in range(FT):
                    sm.inc(t.matmul(psb[1][0:1, 0:512], onescol_f[:, 0:1],
                                    sq_sb[:, ftt * 512:(ftt + 1) * 512],
                                    start=(ftt == 0), stop=(ftt == FT - 1)), "te", 1)
            tec = sm.val("te")

            @block.vector
            def _(v, tec=tec, strow=strow):
                v.wait_ge(sm.h["te"], tec)
                v.tensor_scalar(strow[0:1, 0:512], psb[0][0:1, 0:512],
                                1.0 / DM, None, ALU.mult)
                v.tensor_scalar(strow[0:1, 512:1024], psb[1][0:1, 0:512],
                                1.0 / DM, None, ALU.mult)
                v.drain()
                v.tensor_tensor(strow[0:1, 1024:1536], strow[0:1, 0:512],
                                strow[0:1, 0:512], ALU.mult)
                v.drain()
                v.tensor_tensor(strow[0:1, 1024:1536], strow[0:1, 512:1024],
                                strow[0:1, 1024:1536], ALU.subtract)
                v.drain()
                sm.inc(v.tensor_scalar(strow[0:1, 1024:1536], strow[0:1, 1024:1536],
                                       EPS, None, ALU.add), "ve", 1)
            vev = sm.val("ve")

            @block.scalar
            def _(s, vev=vev, strow=strow):
                s.wait_ge(sm.h["ve"], vev)
                sm.inc(s.activation(strow[0:1, 1536:2048], strow[0:1, 1024:1536],
                                    AF.Sqrt), "se", 1)
            ses = sm.val("se")

            @block.vector
            def _(v, ses=ses, strow=strow):
                v.wait_ge(sm.h["se"], ses)
                v.reciprocal(strow[0:1, 1536:2048], strow[0:1, 1536:2048])
                v.drain()
                v.tensor_tensor(strow[0:1, 2048:2560], strow[0:1, 1536:2048],
                                strow[0:1, 1536:2048], ALU.mult)
                v.drain()
                v.tensor_tensor(strow[0:1, 2048:2560], strow[0:1, 1024:1536],
                                strow[0:1, 2048:2560], ALU.mult)
                v.drain()
                v.tensor_scalar(strow[0:1, 2048:2560], strow[0:1, 2048:2560],
                                -0.5, 1.5, ALU.mult, ALU.add)
                v.drain()
                v.tensor_tensor(strow[0:1, 1536:2048], strow[0:1, 1536:2048],
                                strow[0:1, 2048:2560], ALU.mult)
                v.drain()
                sm.inc(v.tensor_tensor(strow[0:1, 2560:3072], strow[0:1, 0:512],
                                       strow[0:1, 1536:2048], ALU.mult), "ve", 1)
            ver = sm.val("ve")

            @block.tensor
            def _(t, ver=ver, strow=strow):
                t.wait_ge(sm.h["ve"], ver)
                sm.inc(t.matmul(psb[2][:, 0:512], onesrow_f[0:1, 0:128],
                                strow[0:1, 1536:2048], start=True, stop=True), "te", 1)
                sm.inc(t.matmul(psb[3][:, 0:512], onesrow_f[0:1, 0:128],
                                strow[0:1, 2560:3072], start=True, stop=True), "te", 1)
            teb = sm.val("te")

            @block.vector
            def _(v, teb=teb, track_max=track_max, st_wait=st_wait):
                v.wait_ge(sm.h["te"], teb)
                if st_wait:
                    v.wait_ge(sm.h["st"], st_wait)
                for ftt in range(FT):
                    sl = slice(ftt * 512, (ftt + 1) * 512)
                    v.tensor_tensor(sq_sb[:, sl], src_sb[:, sl],
                                    psb[2][:, 0:512], ALU.mult)
                    v.drain()
                    v.tensor_tensor(sq_sb[:, sl], sq_sb[:, sl],
                                    psb[3][:, 0:512], ALU.subtract)
                    v.drain()
                    if track_max is not None:
                        v.reduce_max(maxc[:, track_max + 1:track_max + 2] if False
                                     else qsc[:, 3:4],
                                     sq_sb[:, sl], axis=AX.X,
                                     apply_absolute_value=True)
                        v.drain()
                        v.tensor_tensor(maxc[:, track_max:track_max + 1],
                                        maxc[:, track_max:track_max + 1],
                                        qsc[:, 3:4], ALU.max)
                        v.drain()
                    sm.inc(v.tensor_copy(out_sb[:, sl], sq_sb[:, sl]), "ve", 1)
            return sm.val("ve")

        with (
            nc.sbuf_tensor("xc", [128, FT * 512], F32) as xc,
            nc.sbuf_tensor("sqc", [128, FT * 512], F32) as sqc,
            nc.sbuf_tensor("ac", [128, FT * 512], BF16) as ac,
            nc.sbuf_tensor("y1st", [128, FT * 512], BF16) as y1st,
            nc.sbuf_tensor("strow", [1, 3072], F32) as strow,
        ):
            @block.vector
            def _(v):
                v.wait_ge(sm.h["ph"], ph_b)
                v.memset(maxc[:, 0:1], 0.0)
                sm.inc(v.memset(maxc[:, 1:2], 0.0), "ve", 1)
            c1_ve0 = sm.val("ve")

            y1_store_at = 0
            for qc in range(NQC):
                ld0 = sm.val("ld")

                @block.sync
                def _(sync, qc=qc, prev_ve=sm.val("ve")):
                    sync.wait_ge(sm.h["ph"], ph_b)
                    sync.wait_ge(sm.h["ve"], prev_ve)
                    sm.inc(sync.dma_start(
                        out=AP(xc, 0, [[FT * 512, 128], [512, FT], [1, 512]]),
                        in_=AP(xfm.tensor, qc * 512,
                               [[S, 128], [S * 128, FT], [1, 512]])), "ld", 16)
                    sm.inc(sync.dma_start(
                        out=AP(ac, 0, [[FT * 512, 128], [512, FT], [1, 512]]),
                        in_=AP(attnd.tensor, qc * 512,
                               [[S, 128], [S * 128, FT], [1, 512]])), "ld", 16)
                c1_ld = sm.val("ld")

                @block.vector
                def _(v, c1_ld=c1_ld):
                    v.wait_ge(sm.h["ld"], c1_ld)
                    v.wait_ge(sm.h["ve"], c1_ve0)
                    sm.inc(v.tensor_tensor(xc[:, :], xc[:, :], ac[:, :], ALU.add),
                           "ve", 1)

                y1_done = emit_ln(xc, sqc, strow, 0, y1st, qc, y1_store_at)

                @block.sync
                def _(sync, qc=qc, y1_done=y1_done):
                    sync.wait_ge(sm.h["ve"], y1_done)
                    sm.inc(sync.dma_start(
                        out=AP(y1d.tensor, qc * 512,
                               [[S, 128], [S * 128, FT], [1, 512]]),
                        in_=AP(y1st, 0,
                               [[FT * 512, 128], [512, FT], [1, 512]])), "st", 16)
                y1_store_at = sm.val("st")

            c1_ve_end = sm.val("ve")

            @block.gpsimd
            def _(g, c1_ve_end=c1_ve_end, y1_store_at=y1_store_at):
                g.wait_ge(sm.h["ve"], c1_ve_end)
                sm.inc(g.dma_start(out=AP(cc1i.tensor, 0, [[1, 128], [1, 1]]),
                                   in_=maxc[:, 0:1]), "cc", 16)
                g.wait_ge(sm.h["cc"], sm.val("cc"))
                sm.inc(g.collective_compute(
                    "AllReduce", ALU.max, replica_groups=[list(range(8))],
                    ins=[cc1i.opt()], outs=[cc1o.opt()]), "cc", 1)
                g.wait_ge(sm.h["cc"], sm.val("cc"))
                sm.inc(g.dma_start(out=rowred[0:1, :],
                                   in_=AP(cc1o.tensor, 0, [[128, 1], [1, 128]])),
                       "cc", 16)
            cc1 = sm.val("cc")

            @block.sync
            def _(sync):
                sm.inc(sync.dma_start(out=sca[0:1, 1:2],
                                      in_=AP(clipx.tensor, 0, [[1, 1], [1, 1]])),
                       "ld", 16)
                sm.inc(sync.dma_start(out=sca[0:1, 6:7],
                                      in_=AP(clipd.tensor, 0, [[1, 1], [1, 1]])),
                       "ld", 16)
            clip_ld = sm.val("ld")

            @block.vector
            def _(v, cc1=cc1, clip_ld=clip_ld):
                v.wait_ge(sm.h["cc"], cc1)
                sm.inc(v.reduce_max(sca[0:1, 0:1], rowred[0:1, :], axis=AX.X),
                       "ve", 1)
            gm_ve = sm.val("ve")

            @block.scalar
            def _(s, gm_ve=gm_ve, clip_ld=clip_ld):
                s.wait_ge(sm.h["ve"], gm_ve)
                s.wait_ge(sm.h["ld"], clip_ld)
                sm.inc(s.activation(sca[0:1, 2:3], sca[0:1, 1:2], AF.Sigmoid),
                       "se", 1)
            sig_se = sm.val("se")

            @block.vector
            def _(v, sig_se=sig_se):
                v.wait_ge(sm.h["se"], sig_se)
                v.tensor_tensor(sca[0:1, 3:4], sca[0:1, 2:3], sca[0:1, 0:1], ALU.mult)
                v.drain()
                v.tensor_scalar(sca[0:1, 3:4], sca[0:1, 3:4], 1.0 / QMAX, None,
                                ALU.mult)
                v.drain()
                v.reciprocal(sca[0:1, 4:5], sca[0:1, 3:4])
                sm.inc(v.drain(), "ve", 1)
            sx_ve = sm.val("ve")

            @block.gpsimd
            def _(g, sx_ve=sx_ve):
                g.wait_ge(sm.h["ve"], sx_ve)
                sm.inc(g.dma_start(out=bounced[128:130], in_=sca[0:1, 3:5]), "cc", 16)
                g.wait_ge(sm.h["cc"], sm.val("cc"))
                sm.inc(g.dma_start(out=qsc[:, 1:2],
                                   in_=AP(bounced.tensor, 128, [[0, 128], [1, 1]])),
                       "cc", 16)
                sm.inc(g.dma_start(out=qsc[:, 0:1],
                                   in_=AP(bounced.tensor, 129, [[0, 128], [1, 1]])),
                       "cc", 16)
            sx_cc = sm.val("cc")

            @block.vector
            def _(v, sx_cc=sx_cc):
                v.wait_ge(sm.h["cc"], sx_cc)
                v.tensor_scalar(deqg[:, :], sg_sb[:, :], qsc[:, 1:2], None, ALU.mult)
                v.drain()
                sm.inc(v.tensor_scalar(dequ[:, :], su_sb[:, :], qsc[:, 1:2], None,
                                       ALU.mult), "ve", 1)
            deq_ve = sm.val("ve")

            if DBG:
                @block.sync
                def _(sync, deq_ve=deq_ve):
                    sync.wait_ge(sm.h["ve"], deq_ve)
                    sm.inc(sync.dma_start(out=scdbg[0:8], in_=sca[0:1, 0:8]), "st", 16)
                    sm.inc(sync.dma_start(out=AP(scdbg.tensor, 8, [[1, 128], [1, 1]]),
                                          in_=qsc[:, 0:1]), "st", 16)
                    sm.inc(sync.dma_start(out=AP(scdbg.tensor, 136, [[1, 128], [1, 1]]),
                                          in_=qsc[:, 1:2]), "st", 16)

        with (
            nc.sbuf_tensor("wgT_sb", [128, FT * DI], BF16) as wgT_sb,
            nc.sbuf_tensor("wuT_sb", [128, FT * DI], BF16) as wuT_sb,
            nc.sbuf_tensor("y1c", [128, FT * 512], BF16) as y1c,
            nc.sbuf_tensor("xtf", [128, FT * 512], F32) as xtf,
            nc.sbuf_tensor("xti", [128, FT * 512], I32) as xti,
            nc.sbuf_tensor("xqb", [128, FT * 512], BF16) as xqb,
            nc.sbuf_tensor("gsb", [128, 2 * 512], F32) as gsb,
            nc.sbuf_tensor("usb", [128, 2 * 512], F32) as usb,
            nc.sbuf_tensor("hfb", [128, 2 * 512], F32) as hfb,
            nc.sbuf_tensor("hst", [128, 2 * 512], BF16) as hst,
        ):
            @block.sync
            def _(sync, deq_ve=deq_ve):
                sync.wait_ge(sm.h["ve"], deq_ve)
                for j in range(FT):
                    src = AP(wgqd.tensor, j * 128, [[DM, DI], [1, 128]])
                    sm.inc(sync.dma_start(out=wgT_sb[:, j * DI:(j + 1) * DI],
                                          in_=src, transpose=True), "ld", 16)
                for j in range(FT):
                    src = AP(wuqd.tensor, j * 128, [[DM, DI], [1, 128]])
                    sm.inc(sync.dma_start(out=wuT_sb[:, j * DI:(j + 1) * DI],
                                          in_=src, transpose=True), "ld", 16)
            wgt_ld = sm.val("ld")

            h_store_at = 0
            hst_at = {}
            for qc in range(NQC):
                @block.sync
                def _(sync, qc=qc, prev=sm.val("ve"), hs=h_store_at):
                    sync.wait_ge(sm.h["ve"], prev)
                    sync.wait_ge(sm.h["st"], hs)
                    sm.inc(sync.dma_start(
                        out=AP(y1c, 0, [[FT * 512, 128], [512, FT], [1, 512]]),
                        in_=AP(y1d.tensor, qc * 512,
                               [[S, 128], [S * 128, FT], [1, 512]])), "ld", 16)
                c2_ld = sm.val("ld")

                @block.vector
                def _(v, c2_ld=c2_ld, wgt_ld=wgt_ld):
                    v.wait_ge(sm.h["ld"], max(c2_ld, wgt_ld))
                    v.tensor_scalar(xtf[:, :], y1c[:, :], qsc[:, 0:1], None, ALU.mult)
                    v.drain()
                    v.tensor_copy(xti[:, :], xtf[:, :])
                    v.drain()
                    sm.inc(v.tensor_scalar(xqb[:, :], xti[:, :], QMAX, -QMAX - 1.0,
                                           ALU.min, ALU.max), "ve", 1)
                xq_ve = sm.val("ve")

                if DBG:
                    @block.sync
                    def _(sync, qc=qc, xq_ve=xq_ve):
                        sync.wait_ge(sm.h["ve"], xq_ve)
                        sm.inc(sync.dma_start(
                            out=AP(xqdbg.tensor, qc * 512,
                                   [[S, 128], [S * 128, FT], [1, 512]]),
                            in_=AP(xqb, 0, [[FT * 512, 128], [512, FT], [1, 512]])),
                            "st", 16)

                se_g = {}
                ve_u = {}
                gd_ve_all = {}
                se_base = sm.val("se")
                for it in range(IT):
                    se_g[it] = se_base + it + 1
                for it in range(IT):
                    @block.tensor
                    def _(t, it=it, qc=qc, xq_ve=xq_ve):
                        t.wait_ge(sm.h["ve"], xq_ve)
                        if it >= 2:
                            t.wait_ge(sm.h["se"], se_g[it - 2])
                            t.wait_ge(sm.h["ve"], ve_u[it - 2])
                        for ht in range(FT):
                            mg = t.matmul(psb[it % 2][:, :],
                                          wgT_sb[:, ht * DI + it * 128:
                                                 ht * DI + (it + 1) * 128],
                                          xqb[:, ht * 512:(ht + 1) * 512],
                                          start=(ht == 0), stop=(ht == FT - 1))
                        sm.inc(mg, "te", 1)
                        for ht in range(FT):
                            mu = t.matmul(psb[2 + it % 2][:, :],
                                          wuT_sb[:, ht * DI + it * 128:
                                                 ht * DI + (it + 1) * 128],
                                          xqb[:, ht * 512:(ht + 1) * 512],
                                          start=(ht == 0), stop=(ht == FT - 1))
                        sm.inc(mu, "te", 1)
                    gu_te = sm.val("te")

                    @block.vector
                    def _(v, it=it, qc=qc, gu_te=gu_te):
                        v.wait_ge(sm.h["te"], gu_te - 1)
                        b = it % 2
                        if DBG:
                            v.tensor_copy(hfb[:, b * 512:(b + 1) * 512], psb[b][:, :])
                            v.drain()
                        sm.inc(v.tensor_scalar(gsb[:, b * 512:(b + 1) * 512],
                                               psb[b][:, :], deqg[:, it:it + 1],
                                               None, ALU.mult), "bd", 1)
                    gd_ve = sm.val("bd")

                    if DBG:
                        @block.sync
                        def _(sync, it=it, qc=qc, gd_ve=gd_ve):
                            sync.wait_ge(sm.h["bd"], gd_ve)
                            sm.inc(sync.dma_start(
                                out=gdbg[it * 128:(it + 1) * 128,
                                         qc * 512:(qc + 1) * 512],
                                in_=hfb[:, (it % 2) * 512:(it % 2 + 1) * 512]),
                                "st", 16)

                    gd_ve_all[it] = gd_ve

                    @block.vector
                    def _(v, it=it, qc=qc, gu_te=gu_te):
                        v.wait_ge(sm.h["te"], gu_te)
                        v.wait_ge(sm.h["se"], se_g[it])
                        pit = it - 2 if it >= 2 else IT - 2 + it
                        if pit in hst_at:
                            v.wait_ge(sm.h["st"], hst_at[pit])
                        b = it % 2
                        v.tensor_scalar(usb[:, b * 512:(b + 1) * 512],
                                        psb[2 + b][:, :], dequ[:, it:it + 1],
                                        None, ALU.mult)
                        v.drain()
                        v.tensor_tensor(hfb[:, b * 512:(b + 1) * 512],
                                        gsb[:, b * 512:(b + 1) * 512],
                                        usb[:, b * 512:(b + 1) * 512], ALU.mult)
                        v.drain()
                        v.reduce_max(qsc[:, 3:4], hfb[:, b * 512:(b + 1) * 512],
                                     axis=AX.X, apply_absolute_value=True)
                        v.drain()
                        v.tensor_tensor(maxc[:, 1:2], maxc[:, 1:2], qsc[:, 3:4],
                                        ALU.max)
                        v.drain()
                        sm.inc(v.tensor_copy(hst[:, b * 512:(b + 1) * 512],
                                             hfb[:, b * 512:(b + 1) * 512]), "ve", 1)
                    ve_u[it] = sm.val("ve")

                    @block.sync
                    def _(sync, it=it, qc=qc):
                        sync.wait_ge(sm.h["ve"], ve_u[it])
                        sm.inc(sync.dma_start(
                            out=hd[it * 128:(it + 1) * 128, qc * 512:(qc + 1) * 512],
                            in_=hst[:, (it % 2) * 512:(it % 2 + 1) * 512]), "st", 16)
                    hst_at[it] = sm.val("st")

                @block.scalar
                def _(s, qc=qc, gd_ve_all=dict(gd_ve_all)):
                    for it in range(IT):
                        s.wait_ge(sm.h["bd"], gd_ve_all[it])
                        b = it % 2
                        sm.inc(s.activation(gsb[:, b * 512:(b + 1) * 512],
                                            gsb[:, b * 512:(b + 1) * 512],
                                            AF.Silu), "se", 1)
                assert sm.val("se") == se_base + IT
                h_store_at = sm.val("st")

            c2_ve_end = sm.val("ve")

            @block.gpsimd
            def _(g, c2_ve_end=c2_ve_end):
                g.wait_ge(sm.h["ve"], c2_ve_end)
                sm.inc(g.dma_start(out=AP(cc2i.tensor, 0, [[1, 128], [1, 1]]),
                                   in_=maxc[:, 1:2]), "cc", 16)
                g.wait_ge(sm.h["cc"], sm.val("cc"))
                sm.inc(g.collective_compute(
                    "AllReduce", ALU.max, replica_groups=[list(range(8))],
                    ins=[cc2i.opt()], outs=[cc2o.opt()]), "cc", 1)
                g.wait_ge(sm.h["cc"], sm.val("cc"))
                sm.inc(g.dma_start(out=rowred[0:1, :],
                                   in_=AP(cc2o.tensor, 0, [[128, 1], [1, 128]])),
                       "cc", 16)
            cc2 = sm.val("cc")

            @block.vector
            def _(v, cc2=cc2):
                v.wait_ge(sm.h["cc"], cc2)
                sm.inc(v.reduce_max(sca[0:1, 5:6], rowred[0:1, :], axis=AX.X),
                       "ve", 1)
            gm2_ve = sm.val("ve")

            @block.scalar
            def _(s, gm2_ve=gm2_ve):
                s.wait_ge(sm.h["ve"], gm2_ve)
                sm.inc(s.activation(sca[0:1, 7:8], sca[0:1, 6:7], AF.Sigmoid),
                       "se", 1)
            sig2_se = sm.val("se")

            @block.vector
            def _(v, sig2_se=sig2_se):
                v.wait_ge(sm.h["se"], sig2_se)
                v.tensor_tensor(sca[0:1, 6:7], sca[0:1, 7:8], sca[0:1, 5:6], ALU.mult)
                v.drain()
                v.tensor_scalar(sca[0:1, 6:7], sca[0:1, 6:7], 1.0 / QMAX, None,
                                ALU.mult)
                v.drain()
                v.reciprocal(sca[0:1, 7:8], sca[0:1, 6:7])
                sm.inc(v.drain(), "ve", 1)
            sh_ve = sm.val("ve")

            @block.gpsimd
            def _(g, sh_ve=sh_ve):
                g.wait_ge(sm.h["ve"], sh_ve)
                sm.inc(g.dma_start(out=bounced[130:132], in_=sca[0:1, 6:8]), "cc", 16)
                g.wait_ge(sm.h["cc"], sm.val("cc"))
                sm.inc(g.dma_start(out=qsc[:, 3:4],
                                   in_=AP(bounced.tensor, 130, [[0, 128], [1, 1]])),
                       "cc", 16)
                sm.inc(g.dma_start(out=qsc[:, 2:3],
                                   in_=AP(bounced.tensor, 131, [[0, 128], [1, 1]])),
                       "cc", 16)
            sh_cc = sm.val("cc")

            @block.vector
            def _(v, sh_cc=sh_cc):
                v.wait_ge(sm.h["cc"], sh_cc)
                sm.inc(v.tensor_scalar(deqd[:, :], sd_sb[:, :], qsc[:, 3:4], None,
                                       ALU.mult), "ve", 1)
            deqd_ve = sm.val("ve")

        with (
            nc.sbuf_tensor("wdT_sb", [128, IT * DM], BF16) as wdT_sb,
            nc.sbuf_tensor("hc", [128, IT * 512], BF16) as hc,
            nc.sbuf_tensor("hqb", [128, IT * 512], BF16) as hqb,
            nc.sbuf_tensor("tf", [128, 2048], F32) as tf,
            nc.sbuf_tensor("ti", [128, 2048], I32) as ti,
            nc.sbuf_tensor("y1c2", [128, FT * 512], BF16) as y1c2,
            nc.sbuf_tensor("rc2", [128, FT * 512], F32) as rc2,
            nc.sbuf_tensor("sq2", [128, FT * 512], F32) as sq2,
            nc.sbuf_tensor("outst", [128, FT * 512], F32) as outst,
            nc.sbuf_tensor("strow2", [1, 3072], F32) as strow2,
        ):
            @block.sync
            def _(sync, deqd_ve=deqd_ve):
                sync.wait_ge(sm.h["ve"], deqd_ve)
                for j in range(IT):
                    src = AP(wdqd.tensor, j * 128, [[DI, DM], [1, 128]])
                    sm.inc(sync.dma_start(out=wdT_sb[:, j * DM:(j + 1) * DM],
                                          in_=src, transpose=True), "ld", 16)
            wdt_ld = sm.val("ld")

            out_st = 0
            for qc in range(NQC):
                @block.sync
                def _(sync, qc=qc, prev=sm.val("ve"), os=out_st):
                    sync.wait_ge(sm.h["ve"], prev)
                    sync.wait_ge(sm.h["st"], os)
                    sm.inc(sync.dma_start(
                        out=AP(hc, 0, [[IT * 512, 128], [512, IT], [1, 512]]),
                        in_=AP(hd.tensor, qc * 512,
                               [[S, 128], [S * 128, IT], [1, 512]])), "ld", 16)
                    sm.inc(sync.dma_start(
                        out=AP(y1c2, 0, [[FT * 512, 128], [512, FT], [1, 512]]),
                        in_=AP(y1d.tensor, qc * 512,
                               [[S, 128], [S * 128, FT], [1, 512]])), "ld", 16)
                c3_ld = sm.val("ld")

                groups = [(0, 4), (4, 8), (8, 12), (12, 16), (16, 20), (20, 22)]

                @block.vector
                def _(v, c3_ld=c3_ld, wdt_ld=wdt_ld, groups=groups):
                    v.wait_ge(sm.h["ld"], max(c3_ld, wdt_ld))
                    for g0, g1 in groups:
                        w = (g1 - g0) * 512
                        v.tensor_scalar(tf[:, 0:w], hc[:, g0 * 512:g1 * 512],
                                        qsc[:, 2:3], None, ALU.mult)
                        v.drain()
                        v.tensor_copy(ti[:, 0:w], tf[:, 0:w])
                        v.drain()
                        v.tensor_scalar(hqb[:, g0 * 512:g1 * 512], ti[:, 0:w],
                                        QMAX, -QMAX - 1.0, ALU.min, ALU.max)
                        v.drain()
                    sm.inc(v.drain(), "ve", 1)
                hq_ve = sm.val("ve")

                ve_d = {}
                for fo in range(FT):
                    @block.tensor
                    def _(t, fo=fo, hq_ve=hq_ve):
                        t.wait_ge(sm.h["ve"], hq_ve)
                        if fo >= 2:
                            t.wait_ge(sm.h["ve"], ve_d[fo - 2])
                        for it in range(IT):
                            md = t.matmul(psb[fo % 2][:, :],
                                          wdT_sb[:, it * DM + fo * 128:
                                                 it * DM + (fo + 1) * 128],
                                          hqb[:, it * 512:(it + 1) * 512],
                                          start=(it == 0), stop=(it == IT - 1))
                        sm.inc(md, "te", 1)
                    d_te = sm.val("te")

                    @block.vector
                    def _(v, fo=fo, d_te=d_te):
                        v.wait_ge(sm.h["te"], d_te)
                        sm.inc(v.scalar_tensor_tensor(
                            rc2[:, fo * 512:(fo + 1) * 512], psb[fo % 2][:, :],
                            deqd[:, fo:fo + 1], y1c2[:, fo * 512:(fo + 1) * 512],
                            ALU.mult, ALU.add), "ve", 1)
                    ve_d[fo] = sm.val("ve")

                out_done = emit_ln(rc2, sq2, strow2, None, outst, qc, out_st)

                @block.sync
                def _(sync, qc=qc, out_done=out_done):
                    sync.wait_ge(sm.h["ve"], out_done)
                    sm.inc(sync.dma_start(
                        out=AP(outfm.tensor, qc * 512,
                               [[S, 128], [S * 128, FT], [1, 512]]),
                        in_=AP(outst, 0,
                               [[FT * 512, 128], [512, FT], [1, 512]])), "st", 16)
                out_st = sm.val("st")

            @block.sync
            def _(sync, out_st=out_st):
                sync.wait_ge(sm.h["st"], out_st)
    return nc


_FS = _bucket_table(-DW, DW)


def kernel(L0_attn_hidden_states, L0_attn_rel_embeddings, L0_attn_Wq, L0_attn_bq,
           L0_attn_Wk, L0_attn_bk, L0_attn_Wv, L0_attn_bv, L0_ffn_clip_x,
           L0_ffn_clip_down, L0_ffn_w_gate, L0_ffn_w_up, L0_ffn_w_down):
    x = np.asarray(L0_attn_hidden_states)
    rel = np.asarray(L0_attn_rel_embeddings)
    relg = np.ascontiguousarray(rel[_FS].T)
    relgr = np.ascontiguousarray(rel[_FS[::-1]].T)
    shared = dict(
        wqi=np.ascontiguousarray(L0_attn_Wq), wki=np.ascontiguousarray(L0_attn_Wk),
        wvi=np.ascontiguousarray(L0_attn_Wv), bqi=np.ascontiguousarray(L0_attn_bq),
        bki=np.ascontiguousarray(L0_attn_bk), bvi=np.ascontiguousarray(L0_attn_bv),
        relgT=relg, relgrT=relgr,
        wgi=np.ascontiguousarray(L0_ffn_w_gate), wui=np.ascontiguousarray(L0_ffn_w_up),
        wdi=np.ascontiguousarray(L0_ffn_w_down),
        clipx=np.ascontiguousarray(L0_ffn_clip_x),
        clipd=np.ascontiguousarray(L0_ffn_clip_down),
        ident=np.eye(128, dtype=BF),
    )
    nc = build_nc()
    in_maps = []
    for b in range(8):
        m = dict(shared)
        m["xfm"] = np.ascontiguousarray(x[b].T)
        in_maps.append(m)
    res = run_bass_kernel_spmd(nc, in_maps, core_ids=list(range(8)))
    out = np.stack([np.asarray(res.results[b]["outfm"]).T for b in range(8)])
    return np.ascontiguousarray(out.astype(np.float32))


# version 16
# speedup vs baseline: 1.1603x; 1.1603x over previous
import contextlib
import math
import os

import numpy as np
import ml_dtypes
import concourse.bass as bass
import concourse.mybir as mybir
from concourse.ap import AP
from concourse.bass_utils import run_bass_kernel_spmd

F32 = mybir.dt.float32
BF16 = mybir.dt.bfloat16
I32 = mybir.dt.int32
AF = mybir.ActivationFunctionType
ALU = mybir.AluOpType
AX = mybir.AxisListType
BF = ml_dtypes.bfloat16

H, D, S, DM, DI = 16, 64, 4096, 1024, 2816
NT, FT, IT, NQC = S // 128, DM // 128, DI // 128, S // 512
BUCKETS, MAX_POS, ATT_SPAN = 256, 512, 256
A2 = 2 * ATT_SPAN
SCALE = math.sqrt(D * 3)
INV_SCALE = 1.0 / SCALE
DW = 640
WB = 2 * DW + 1
WP = 1281
QMAX = 127.0
EPS = 1e-5
NBAND = 17
DBG = os.environ.get("KDBG", "") == "1"
COL_FP = DW + 520
COL_FN = DW - 520
RCOL_FP = (WB - 1) - COL_FP
RCOL_FN = (WB - 1) - COL_FN


def _bucket_table(dmin, dmax):
    d = np.arange(dmin, dmax + 1)
    rel = np.clip(d, -MAX_POS + 1, MAX_POS - 1)
    sign = np.sign(rel).astype(np.float64)
    abs_pos = np.maximum(np.abs(rel), 1)
    log_base = math.log((MAX_POS - 1) / (BUCKETS // 2))
    mid = BUCKETS // 2
    log_pos = np.ceil(
        np.log(abs_pos.astype(np.float32) / mid) / np.float32(log_base) * (mid - 1)
    ) + mid
    b = np.where(abs_pos < mid, abs_pos.astype(np.float64), log_pos) * sign
    return np.clip(b.astype(np.int64) + ATT_SPAN, 0, A2 - 1)


def _zone_ranges(kt, qc):
    out = []
    for i in range(4):
        dd = (4 * qc + i) - kt
        z = "near" if abs(dd) <= 4 else ("farP" if dd >= 5 else "farN")
        if out and out[-1][0] == z:
            out[-1] = (z, out[-1][1], 128 * (i + 1))
        else:
            out.append((z, 128 * i, 128 * (i + 1)))
    return out


def _band_window(kt):
    return max(0, 128 * kt - 512), min(S, 128 * kt + 640)


class _Sems:
    def __init__(self):
        self.cnt = {}
        self.h = {}

    def add(self, name, handle):
        self.h[name] = handle
        self.cnt[name] = 0

    def inc(self, instr, name, n=1):
        instr.then_inc(self.h[name], n)
        self.cnt[name] += n
        return self.cnt[name]

    def val(self, name):
        return self.cnt[name]


def build_nc():
    nc = bass.Bass("TRN2", target_bir_lowering=False)

    xfm = nc.dram_tensor("xfm", [DM, S], F32, kind="ExternalInput").ap()
    wqi = nc.dram_tensor("wqi", [DM, DM], F32, kind="ExternalInput").ap()
    wki = nc.dram_tensor("wki", [DM, DM], F32, kind="ExternalInput").ap()
    wvi = nc.dram_tensor("wvi", [DM, DM], F32, kind="ExternalInput").ap()
    bqi = nc.dram_tensor("bqi", [DM], F32, kind="ExternalInput").ap()
    bki = nc.dram_tensor("bki", [DM], F32, kind="ExternalInput").ap()
    bvi = nc.dram_tensor("bvi", [DM], F32, kind="ExternalInput").ap()
    relgT = nc.dram_tensor("relgT", [DM, WB], F32, kind="ExternalInput").ap()
    relgrT = nc.dram_tensor("relgrT", [DM, WB], F32, kind="ExternalInput").ap()
    wgi = nc.dram_tensor("wgi", [DI, DM], F32, kind="ExternalInput").ap()
    wui = nc.dram_tensor("wui", [DI, DM], F32, kind="ExternalInput").ap()
    wdi = nc.dram_tensor("wdi", [DM, DI], F32, kind="ExternalInput").ap()
    ident = nc.dram_tensor("ident", [128, 128], BF16, kind="ExternalInput").ap()
    clipx = nc.dram_tensor("clipx", [1], F32, kind="ExternalInput").ap()
    clipd = nc.dram_tensor("clipd", [1], F32, kind="ExternalInput").ap()
    outfm = nc.dram_tensor("outfm", [DM, S], F32, kind="ExternalOutput").ap()

    dk = "ExternalOutput" if DBG else "Internal"
    qd = nc.dram_tensor("qd", [DM, S], BF16, kind=dk).ap()
    kd = nc.dram_tensor("kd", [DM, S], BF16, kind=dk).ap()
    vhatd = nc.dram_tensor("vhatd", [S, 16 * 65], BF16, kind=dk).ap()
    posd = nc.dram_tensor("posd", [2 * DM, WB], BF16, kind=dk).ap()
    pgd = nc.dram_tensor("pgd", [H * S, WP], BF16, kind="Internal").ap()
    cgd = nc.dram_tensor("cgd", [H * S, WP], BF16, kind="Internal").ap()
    ead = nc.dram_tensor("ead", [H * 2, S], F32, kind=dk).ap()
    biasd = nc.dram_tensor("biasd", [H * 2, S], BF16, kind=dk).ap()
    attnd = nc.dram_tensor("attnd", [DM, S], BF16, kind=dk).ap()
    y1d = nc.dram_tensor("y1d", [DM, S], BF16, kind=dk).ap()
    hd = nc.dram_tensor("hd", [DI, S], BF16, kind=dk).ap()
    xqdbg = nc.dram_tensor("xqdbg", [DM, S], BF16, kind=dk).ap()
    gdbg = nc.dram_tensor("gdbg", [DI, S], F32, kind=dk).ap()
    scdbg = nc.dram_tensor("scdbg", [300], F32, kind=dk).ap()
    udbg = nc.dram_tensor("udbg", [DI, S], F32, kind=dk).ap()
    wgqd = nc.dram_tensor("wgqd", [DI, DM], BF16, kind=dk).ap()
    wuqd = nc.dram_tensor("wuqd", [DI, DM], BF16, kind=dk).ap()
    wdqd = nc.dram_tensor("wdqd", [DM, DI], BF16, kind="Internal").ap()
    bounced = nc.dram_tensor("bounced", [256], F32, kind="Internal").ap()
    cc1i = nc.dram_tensor("cc1i", [128], F32, kind="Internal").ap()
    cc1o = nc.dram_tensor("cc1o", [128], F32, kind="Internal").ap()
    cc2i = nc.dram_tensor("cc2i", [128], F32, kind="Internal").ap()
    cc2o = nc.dram_tensor("cc2o", [128], F32, kind="Internal").ap()

    sm = _Sems()
    stack = contextlib.ExitStack()
    for name in ["ld", "gl", "te", "se", "ve", "st", "qk", "ex", "pv", "bd",
                 "cmb", "bandc", "bandp", "cc", "ph", "vb", "ebl"]:
        sm.add(name, stack.enter_context(nc.semaphore(name)))

    with stack:
        block = stack.enter_context(nc.Block())
        biasq = stack.enter_context(nc.sbuf_tensor("biasq", [128, FT], F32))
        biask = stack.enter_context(nc.sbuf_tensor("biask", [128, FT], F32))
        bvrowb = stack.enter_context(nc.sbuf_tensor("bvrowb", [1, DM], BF16))
        onesrow_f = stack.enter_context(nc.sbuf_tensor("onesrow_f", [1, 128], F32))
        onescol_f = stack.enter_context(nc.sbuf_tensor("onescol_f", [128, 1], F32))
        onescolb = stack.enter_context(nc.sbuf_tensor("onescolb", [1, 128], BF16))
        zrow_b = stack.enter_context(nc.sbuf_tensor("zrow_b", [1, 512], BF16))
        zcol_b = stack.enter_context(nc.sbuf_tensor("zcol_b", [1, 65], BF16))
        sg_sb = stack.enter_context(nc.sbuf_tensor("sg_sb", [128, IT], F32))
        su_sb = stack.enter_context(nc.sbuf_tensor("su_sb", [128, IT], F32))
        sd_sb = stack.enter_context(nc.sbuf_tensor("sd_sb", [128, FT], F32))
        deqg = stack.enter_context(nc.sbuf_tensor("deqg", [128, IT], F32))
        dequ = stack.enter_context(nc.sbuf_tensor("dequ", [128, IT], F32))
        deqd = stack.enter_context(nc.sbuf_tensor("deqd", [128, FT], F32))
        ident_sb = stack.enter_context(nc.sbuf_tensor("ident_sb", [128, 128], BF16))
        eacc = stack.enter_context(nc.sbuf_tensor("eacc", [128, 4 * NT], F32))
        bcacc = stack.enter_context(nc.sbuf_tensor("bcacc", [128, 4 * NT], BF16))
        maxc = stack.enter_context(nc.sbuf_tensor("maxc", [128, 2], F32))
        rowred = stack.enter_context(nc.sbuf_tensor("rowred", [1, 128], F32))
        sca = stack.enter_context(nc.sbuf_tensor("sca", [1, 8], F32))
        qsc = stack.enter_context(nc.sbuf_tensor("qsc", [128, 4], F32))
        psb = [stack.enter_context(nc.psum_tensor(f"ps{i}", [128, 512], F32))
               for i in range(4)]
        psd = [stack.enter_context(nc.psum_tensor(f"psd{i}", [128, 1024], F32))
               for i in range(2)]

        with (
            nc.sbuf_tensor("wstage", [128, 2 * DI], F32) as wstage,
            nc.sbuf_tensor("wtmp", [128, DI], F32) as wtmp,
            nc.sbuf_tensor("wint", [128, DI], I32) as wint,
            nc.sbuf_tensor("wqb", [128, 2 * DI], BF16) as wqb,
            nc.sbuf_tensor("rcol", [128, 1], F32) as rcol,
        ):
            jobs = []
            for w_in, w_out, s_sb, ncols, ntl in ((wgi, wgqd, sg_sb, DM, IT),
                                                  (wui, wuqd, su_sb, DM, IT),
                                                  (wdi, wdqd, sd_sb, DI, FT)):
                for t in range(ntl):
                    jobs.append((w_in, w_out, s_sb, ncols, t))
            ld0, ve0, st0 = sm.val("ld"), sm.val("ve"), sm.val("st")

            @block.sync
            def _(sync):
                for j, (w_in, w_out, s_sb, ncols, t) in enumerate(jobs):
                    b = j % 2
                    if j >= 2:
                        sync.wait_ge(sm.h["ve"], ve0 + (j - 1))
                    sm.inc(sync.dma_start(out=wstage[:, b * DI:b * DI + ncols],
                                          in_=w_in[t * 128:(t + 1) * 128, :]), "ld", 16)

            @block.vector
            def _(v):
                for j, (w_in, w_out, s_sb, ncols, t) in enumerate(jobs):
                    b = j % 2
                    v.wait_ge(sm.h["ld"], ld0 + 16 * (j + 1))
                    if j >= 2:
                        v.wait_ge(sm.h["st"], st0 + 16 * (j - 1))
                    st_ap = wstage[:, b * DI:b * DI + ncols]
                    v.reduce_max(s_sb[:, t:t + 1], st_ap, axis=AX.X,
                                 apply_absolute_value=True)
                    v.drain()
                    v.tensor_scalar(s_sb[:, t:t + 1], s_sb[:, t:t + 1], 1.0 / QMAX,
                                    None, ALU.mult)
                    v.drain()
                    v.reciprocal(rcol[:, 0:1], s_sb[:, t:t + 1])
                    v.drain()
                    v.tensor_scalar(wtmp[:, 0:ncols], st_ap, rcol[:, 0:1], None, ALU.mult)
                    v.drain()
                    v.tensor_copy(wint[:, 0:ncols], wtmp[:, 0:ncols])
                    v.drain()
                    sm.inc(v.tensor_copy(wqb[:, b * DI:b * DI + ncols],
                                         wint[:, 0:ncols]), "ve", 1)

            @block.scalar
            def _(s):
                for j, (w_in, w_out, s_sb, ncols, t) in enumerate(jobs):
                    b = j % 2
                    s.wait_ge(sm.h["ve"], ve0 + (j + 1))
                    sm.inc(s.dma_start(out=w_out[t * 128:(t + 1) * 128, :],
                                       in_=wqb[:, b * DI:b * DI + ncols]), "st", 16)
            w0_st = sm.val("st")

            @block.vector
            def _(v):
                v.wait_ge(sm.h["st"], w0_st)
                sm.inc(v.drain(), "ph", 1)
        ph_w0 = sm.val("ph")

        with nc.sbuf_tensor("X_sb", [128, FT * S], BF16) as X_sb:
            @block.gpsimd
            def _(g):
                g.wait_ge(sm.h["ph"], ph_w0)
                for ft in range(FT):
                    sm.inc(g.dma_start(out=X_sb[:, ft * S:(ft + 1) * S],
                                       in_=xfm[ft * 128:(ft + 1) * 128, :]), "gl", 16)
                sm.inc(g.dma_start(out=bvrowb[0:1, :],
                                   in_=AP(bvi.tensor, 0, [[DM, 1], [1, DM]])), "gl", 16)
            a_glx = sm.val("gl")

            @block.sync
            def _(sync):
                sync.wait_ge(sm.h["ph"], ph_w0)
                sm.inc(sync.dma_start(out=biasq[:, :],
                                      in_=AP(bqi.tensor, 0, [[1, 128], [128, FT]]),
                                      allow_slow_non_contiguous=True), "ld", 16)
                sm.inc(sync.dma_start(out=biask[:, :],
                                      in_=AP(bki.tensor, 0, [[1, 128], [128, FT]]),
                                      allow_slow_non_contiguous=True), "ld", 16)
                sm.inc(sync.dma_start(out=ident_sb[:, :], in_=ident[:, :]), "ld", 16)
            a_ld = sm.val("ld")

            @block.vector
            def _(v):
                v.wait_ge(sm.h["ph"], ph_w0)
                v.memset(onesrow_f[0:1, :], 1.0)
                v.memset(onescol_f[:, 0:1], 1.0)
                v.memset(onescolb[0:1, :], 1.0)
                v.memset(zrow_b[0:1, :], 1.0)
                sm.inc(v.memset(zcol_b[0:1, :], 0.0), "ve", 1)
            a_ve = sm.val("ve")

            with (
                nc.sbuf_tensor("Wq_sb", [128, FT * DM], BF16) as Wq_sb,
                nc.sbuf_tensor("Wk_sb", [128, FT * DM], BF16) as Wk_sb,
                nc.sbuf_tensor("relc", [128, FT * 512], BF16) as relc,
                nc.sbuf_tensor("posst", [128, 2 * 512], BF16) as posst,
                nc.sbuf_tensor("qstage", [128, 2 * 512], BF16) as qstage,
                nc.sbuf_tensor("kstage", [128, 2 * 512], BF16) as kstage,
            ):
                @block.gpsimd
                def _(g):
                    for w_in, w_sb in ((wqi, Wq_sb), (wki, Wk_sb)):
                        for ht in range(FT):
                            sm.inc(g.dma_start(out=w_sb[:, ht * DM:(ht + 1) * DM],
                                               in_=w_in[ht * 128:(ht + 1) * 128, :]),
                                   "gl", 16)
                a_glw = sm.val("gl")

                dchw = [512, 512, WB - 1024]
                te0, se0, st1, gl1 = (sm.val(x) for x in ("te", "se", "st", "gl"))
                gl_at = {ci: gl1 + 16 * FT * (ci + 1) for ci in range(6)}

                @block.gpsimd
                def _(g):
                    for ci in range(6):
                        side, dc = divmod(ci, 3)
                        w = dchw[dc]
                        src = relgT if side == 0 else relgrT
                        if ci >= 1:
                            g.wait_ge(sm.h["te"], te0 + FT * ci)
                        for ht in range(FT):
                            sm.inc(g.dma_start(
                                out=relc[:, ht * 512:ht * 512 + w],
                                in_=src[ht * 128:(ht + 1) * 128, dc * 512:dc * 512 + w]),
                                "gl", 16)

                @block.tensor
                def _(t):
                    t.wait_ge(sm.h["gl"], a_glw)
                    t.wait_ge(sm.h["ld"], a_ld)
                    for ci in range(6):
                        side, dc = divmod(ci, 3)
                        w = dchw[dc]
                        w_sb = Wq_sb if side == 0 else Wk_sb
                        t.wait_ge(sm.h["gl"], gl_at[ci])
                        for ft in range(FT):
                            j = ci * FT + ft
                            pp = psb[j % 2]
                            if j >= 2:
                                t.wait_ge(sm.h["se"], se0 + j - 1)
                            for ht in range(FT):
                                mmi = t.matmul(
                                    pp[:, 0:w],
                                    w_sb[:, ht * DM + ft * 128: ht * DM + (ft + 1) * 128],
                                    relc[:, ht * 512:ht * 512 + w],
                                    start=(ht == 0), stop=(ht == FT - 1))
                            sm.inc(mmi, "te", 1)
                a_tep = sm.val("te")

                @block.vector
                def _(v):
                    for ci in range(6):
                        side, dc = divmod(ci, 3)
                        w = dchw[dc]
                        bcol = biasq if side == 0 else biask
                        for ft in range(FT):
                            j = ci * FT + ft
                            pp = psb[j % 2]
                            v.wait_ge(sm.h["te"], te0 + j + 1)
                            if j >= 2:
                                v.wait_ge(sm.h["st"], st1 + 16 * (j - 1))
                            sm.inc(v.tensor_scalar(
                                posst[:, (j % 2) * 512:(j % 2) * 512 + w],
                                pp[:, 0:w], bcol[:, ft:ft + 1], None, ALU.add),
                                   "se", 1)

                @block.sync
                def _(sync):
                    for ci in range(6):
                        side, dc = divmod(ci, 3)
                        w = dchw[dc]
                        for ft in range(FT):
                            j = ci * FT + ft
                            sync.wait_ge(sm.h["se"], se0 + j + 1)
                            sm.inc(sync.dma_start(
                                out=posd[side * DM + ft * 128: side * DM + (ft + 1) * 128,
                                         dc * 512:dc * 512 + w],
                                in_=posst[:, (j % 2) * 512:(j % 2) * 512 + w]), "st", 16)

                te1, se1, st2 = sm.val("te"), sm.val("se"), sm.val("st")
                projjobs = [(w, ft, ch) for w in ("k", "q") for ft in range(FT)
                            for ch in range(8)]
                qof = {}
                nq = 0
                for j, (which, ft, ch) in enumerate(projjobs):
                    if which == "q":
                        qof[j] = nq
                        nq += 1

                @block.tensor
                def _(t):
                    t.wait_ge(sm.h["gl"], a_glx)
                    t.wait_ge(sm.h["se"], se1)
                    for j, (which, ft, ch) in enumerate(projjobs):
                        pp = psb[j % 2]
                        if j >= 2:
                            t.wait_ge(sm.h["se"], se1 + j - 1)
                        w_sb = Wk_sb if which == "k" else Wq_sb
                        for ht in range(FT):
                            mmi = t.matmul(
                                pp[:, :],
                                w_sb[:, ht * DM + ft * 128: ht * DM + (ft + 1) * 128],
                                X_sb[:, ht * S + ch * 512: ht * S + (ch + 1) * 512],
                                start=(ht == 0), stop=(ht == FT - 1))
                        sm.inc(mmi, "te", 1)

                @block.vector
                def _(v):
                    for j, (which, ft, ch) in enumerate(projjobs):
                        pp = psb[j % 2]
                        v.wait_ge(sm.h["te"], te1 + j + 1)
                        bcol = biask if which == "k" else biasq
                        stg = kstage if which == "k" else qstage
                        qb = j % 2
                        if j >= 4:
                            v.wait_ge(sm.h["st"], st2 + 16 * (j - 3))
                        dst = stg[:, qb * 512:(qb + 1) * 512]
                        sm.inc(v.tensor_scalar(dst, pp[:, :], bcol[:, ft:ft + 1],
                                               None, ALU.add), "se", 1)
                a_seq = sm.val("se")

                @block.sync
                def _(sync):
                    for j, (which, ft, ch) in enumerate(projjobs):
                        qb = j % 2
                        dstd = kd if which == "k" else qd
                        stg = kstage if which == "k" else qstage
                        sync.wait_ge(sm.h["se"], se1 + j + 1)
                        sm.inc(sync.dma_start(
                            out=dstd[ft * 128:(ft + 1) * 128, ch * 512:(ch + 1) * 512],
                            in_=stg[:, qb * 512:(qb + 1) * 512]), "st", 16)
                a_stq = sm.val("st")

                @block.vector
                def _(v):
                    v.wait_ge(sm.h["se"], a_seq)
                    v.wait_ge(sm.h["st"], a_stq)
                    sm.inc(v.drain(), "ph", 1)
            ph_a1 = sm.val("ph")

            with (
                nc.sbuf_tensor("Wv_sb", [128, FT * DM], BF16) as Wv_sb,
                nc.sbuf_tensor("vstage", [128, 2 * 1040], BF16) as vstage,
            ):
                tev0, vev0, stv0 = (sm.val(x) for x in ("te", "ve", "st"))

                @block.gpsimd
                def _(g):
                    g.wait_ge(sm.h["ph"], ph_a1)
                    for ht in range(FT):
                        sm.inc(g.dma_start(out=Wv_sb[:, ht * DM:(ht + 1) * DM],
                                           in_=wvi[ht * 128:(ht + 1) * 128, :]), "gl", 16)
                a_glv = sm.val("gl")

                vjobs = [(tt, fc) for tt in range(NT) for fc in range(2)]

                @block.tensor
                def _(t):
                    t.wait_ge(sm.h["gl"], a_glv)
                    t.wait_ge(sm.h["ve"], a_ve)
                    for j, (tt, fc) in enumerate(vjobs):
                        pp = psb[2 + j % 2]
                        if j >= 2:
                            t.wait_ge(sm.h["ve"], vev0 + j - 1)
                        for ht in range(FT):
                            t.matmul(pp[:, :],
                                     X_sb[:, ht * S + tt * 128: ht * S + (tt + 1) * 128],
                                     Wv_sb[:, ht * DM + fc * 512: ht * DM + (fc + 1) * 512],
                                     start=(ht == 0), stop=False)
                        sm.inc(t.matmul(pp[:, :], onescolb[0:1, :],
                                        bvrowb[0:1, fc * 512:(fc + 1) * 512],
                                        start=False, stop=True), "te", 1)

                @block.vector
                def _(v):
                    for j, (tt, fc) in enumerate(vjobs):
                        pp = psb[2 + j % 2]
                        vb = tt % 2
                        v.wait_ge(sm.h["te"], tev0 + j + 1)
                        if tt >= 2 and fc == 0:
                            v.wait_ge(sm.h["st"], stv0 + 16 * (tt - 1))
                        dst = AP(vstage, vb * 1040 + fc * 8 * 65,
                                 [[2080, 128], [65, 8], [1, 64]])
                        v.tensor_copy(dst, AP(pp, 0, [[512, 128], [64, 8], [1, 64]]))
                        if fc == 1:
                            v.drain()
                            sm.inc(v.memset(
                                AP(vstage, vb * 1040 + 64, [[2080, 128], [65, 16]]),
                                1.0), "ve", 1)
                        else:
                            sm.inc(v.drain(), "ve", 1)

                @block.sync
                def _(sync):
                    for tt in range(NT):
                        vb = tt % 2
                        sync.wait_ge(sm.h["ve"], vev0 + 2 * tt + 2)
                        sm.inc(sync.dma_start(out=vhatd[tt * 128:(tt + 1) * 128, :],
                                              in_=vstage[:, vb * 1040:(vb + 1) * 1040]),
                               "st", 16)
                a_stv = sm.val("st")

                @block.vector
                def _(v):
                    v.wait_ge(sm.h["st"], a_stv)
                    sm.inc(v.drain(), "ph", 1)
            ph_a2 = sm.val("ph")
        with (
            nc.sbuf_tensor("qftA", [128, S], BF16) as qftA,
            nc.sbuf_tensor("kftA", [128, S], BF16) as kftA,
            nc.sbuf_tensor("posq_sb", [128, WB], BF16) as posq_sb,
            nc.sbuf_tensor("posk_sb", [128, WB], BF16) as posk_sb,
            nc.sbuf_tensor("cpst", [128, 4 * WP], BF16) as cpst,
        ):
            dchw = [512, 512, WB - 1024]
            ea_st_prev = None
            a3_se_prev = None
            for ft in range(FT):
                ld0, te0, ve0, se0, st0 = (sm.val(x) for x in
                                           ("ld", "te", "ve", "se", "st"))
                cpjobs = [(side, tl, hh) for side in range(2) for tl in range(NT)
                          for hh in range(2)]

                @block.sync
                def _(sync, ft=ft, ve0=ve0):
                    if ft == 0:
                        sync.wait_ge(sm.h["ph"], ph_a2)
                    else:
                        sync.wait_ge(sm.h["ve"], ve0)
                    sm.inc(sync.dma_start(out=qftA[:, :],
                                          in_=qd[ft * 128:(ft + 1) * 128, :]), "ld", 16)
                    sm.inc(sync.dma_start(out=kftA[:, :],
                                          in_=kd[ft * 128:(ft + 1) * 128, :]), "ld", 16)
                    sm.inc(sync.dma_start(out=posq_sb[:, :],
                                          in_=posd[ft * 128:(ft + 1) * 128, :]), "ld", 16)
                    sm.inc(sync.dma_start(out=posk_sb[:, :],
                                          in_=posd[DM + ft * 128: DM + (ft + 1) * 128, :]),
                           "ld", 16)
                a3_ld = sm.val("ld")

                jps = [(side, tl) for side in range(2) for tl in range(NT)]

                def dve_idx(jp, dc, hh):
                    return 6 * jp + (1 + hh if dc == 0 else 3 + hh)

                def act_idx(jp, hh):
                    return 2 * jp + 1 + hh

                @block.tensor
                def _(t, ft=ft, te0=te0, ve0=ve0, se0=se0, a3_ld=a3_ld, jps=jps):
                    t.wait_ge(sm.h["ld"], a3_ld)
                    if ft == 0:
                        t.wait_ge(sm.h["ph"], ph_a2)
                    for jp, (side, tl) in enumerate(jps):
                        lhs_sb = kftA if side == 0 else qftA
                        lcols = tl * 128
                        rhs_sb = posq_sb if side == 0 else posk_sb
                        for dc in range(3):
                            for hh in range(2):
                                m = jp * 6 + dc * 2 + hh
                                w = dchw[dc]
                                pp = psb[m % 4]
                                if m >= 4:
                                    jp_, r = divmod(m - 4, 6)
                                    dc_, hh_ = r // 2, r % 2
                                    if dc_ == 1:
                                        t.wait_ge(sm.h["se"],
                                                  se0 + act_idx(jp_, hh_))
                                    else:
                                        t.wait_ge(sm.h["ve"],
                                                  ve0 + dve_idx(jp_, dc_, hh_))
                                mmi = t.matmul(
                                    pp[:, 0:w],
                                    lhs_sb[64 * hh:64 * (hh + 1),
                                           lcols:lcols + 128],
                                    rhs_sb[64 * hh:64 * (hh + 1),
                                           dc * 512:dc * 512 + w],
                                    start=True, stop=True,
                                    tile_position=(64 * hh, 0))
                                sm.inc(mmi, "te", 1)

                @block.vector
                def _(v, ft=ft, te0=te0, ve0=ve0, se0=se0, st0=st0, jps=jps,
                      ea_st_prev=ea_st_prev, a3_se_prev=a3_se_prev):
                    if ea_st_prev is not None:
                        v.wait_ge(sm.h["st"], ea_st_prev)
                        v.wait_ge(sm.h["se"], a3_se_prev)
                    for jp, (side, tl) in enumerate(jps):
                        for dc in (0, 2):
                            for hh in range(2):
                                j = 2 * jp + hh
                                stbase = (j % 4) * WP
                                if dc == 0 and j >= 4:
                                    v.wait_ge(sm.h["st"], st0 + 16 * (j - 3))
                                m = jp * 6 + dc * 2 + hh
                                w = dchw[dc]
                                pp = psb[m % 4]
                                v.wait_ge(sm.h["te"], te0 + m + 1)
                                sm.inc(v.tensor_copy(
                                    cpst[:, stbase + dc * 512:
                                         stbase + dc * 512 + w],
                                    pp[:, 0:w]), "ve", 1)
                        v.drain()
                        for hh in range(2):
                            j = 2 * jp + hh
                            stbase = (j % 4) * WP
                            base = hh * 2 * NT
                            if side == 0:
                                v.tensor_copy(
                                    bcacc[:, base + tl: base + tl + 1],
                                    cpst[:, stbase + COL_FP: stbase + COL_FP + 1])
                                sm.inc(v.tensor_copy(
                                    bcacc[:, base + NT + tl: base + NT + tl + 1],
                                    cpst[:, stbase + COL_FN: stbase + COL_FN + 1]),
                                    "ve", 1)
                            else:
                                v.tensor_copy(
                                    eacc[:, base + tl: base + tl + 1],
                                    cpst[:, stbase + RCOL_FP: stbase + RCOL_FP + 1])
                                sm.inc(v.tensor_copy(
                                    eacc[:, base + NT + tl: base + NT + tl + 1],
                                    cpst[:, stbase + RCOL_FN: stbase + RCOL_FN + 1]),
                                    "ve", 1)

                @block.scalar
                def _(s, ft=ft, te0=te0, st0=st0, jps=jps):
                    for jp, (side, tl) in enumerate(jps):
                        for hh in range(2):
                            j = 2 * jp + hh
                            stbase = (j % 4) * WP
                            if j >= 4:
                                s.wait_ge(sm.h["st"], st0 + 16 * (j - 3))
                            m = jp * 6 + 2 + hh
                            pp = psb[m % 4]
                            s.wait_ge(sm.h["te"], te0 + m + 1)
                            sm.inc(s.copy(cpst[:, stbase + 512: stbase + 1024],
                                          pp[:, 0:512]), "se", 1)

                @block.scalar
                def _(s, ft=ft, ve0=ve0, ea_st_prev=ea_st_prev, jps=jps):
                    if ea_st_prev is not None:
                        s.wait_ge(sm.h["st"], ea_st_prev)
                    s.wait_ge(sm.h["ve"], ve0 + 6 * len(jps))
                    s.activation(bcacc[:, :], bcacc[:, :], AF.Exp,
                                 scale=INV_SCALE)
                    sm.inc(s.activation(eacc[:, :], eacc[:, :], AF.Exp,
                                        scale=INV_SCALE), "se", 1)
                a3_se = sm.val("se")

                @block.sync
                def _(sync, ft=ft, ve0=ve0, se0=se0, a3_se=a3_se, jps=jps):
                    for jp, (side, tl) in enumerate(jps):
                        for hh in range(2):
                            j = 2 * jp + hh
                            h = 2 * ft + hh
                            stbase = (j % 4) * WP
                            sync.wait_ge(sm.h["ve"], ve0 + 6 * jp + 3 + hh)
                            sync.wait_ge(sm.h["se"], se0 + act_idx(jp, hh))
                            dstd = pgd if side == 0 else cgd
                            sm.inc(sync.dma_start(
                                out=AP(dstd.tensor, (h * S + tl * 128) * WP,
                                       [[WP, 128], [1, WB]]),
                                in_=cpst[:, stbase:stbase + WB]), "st", 16)
                    sync.wait_ge(sm.h["se"], a3_se)
                    for hh in range(2):
                        h = 2 * ft + hh
                        base = hh * 2 * NT
                        for side in range(2):
                            sm.inc(sync.dma_start(
                                out=AP(ead.tensor, (h * 2 + side) * S,
                                       [[1, 128], [128, NT]]),
                                in_=eacc[:, base + side * NT: base + (side + 1) * NT],
                                allow_slow_non_contiguous=True), "st", 16)
                            sm.inc(sync.dma_start(
                                out=AP(biasd.tensor, (h * 2 + side) * S,
                                       [[NT, 128], [1, NT]]),
                                in_=bcacc[:, base + side * NT: base + (side + 1) * NT]),
                                "st", 16)
                ea_st_prev = sm.val("st")
                a3_se_prev = a3_se

            @block.vector
            def _(v):
                v.wait_ge(sm.h["st"], sm.val("st"))
                sm.inc(v.drain(), "ph", 1)
        ph_a3 = sm.val("ph")

        with (
            nc.sbuf_tensor("qhB", [128, 2 * S], BF16) as qhB,
            nc.sbuf_tensor("khB", [128, 2 * S], BF16) as khB,
            nc.sbuf_tensor("vh_sb", [128, 2 * NT * 65], BF16) as vh_sb,
            nc.sbuf_tensor("earow", [1, 4 * S], F32) as earow,
            nc.sbuf_tensor("band_sb", [128, NBAND * 1152], BF16) as band_sb,
            nc.sbuf_tensor("expS", [128, 2 * 1024], BF16) as expS,
            nc.sbuf_tensor("vhp_sb", [128, 2 * NT * 65], BF16) as vhp_sb,
            nc.sbuf_tensor("vhn_sb", [128, 2 * NT * 65], BF16) as vhn_sb,
            nc.sbuf_tensor("ebraw", [128, 4 * NT], F32) as ebraw,
            nc.sbuf_tensor("zsb", [128, 2 * 3 * 512], F32) as zsb,
            nc.sbuf_tensor("csum", [128, 2 * 512], F32) as csum,
            nc.sbuf_tensor("rrow", [1, 2 * 512], F32) as rrow,
            nc.sbuf_tensor("attnst", [128, 2 * 512], BF16) as attnst,
        ):
            slot_need = {i: 0 for i in range(NBAND)}
            bandc_at, bandp_at, issued = {}, {}, set()
            qk_at, ex_at, pv_after, bd_at = {}, {}, {}, {}
            cmb_at = {}
            comb_done = {}
            attnbuf_need = {0: 0, 1: 0}
            ps5_need = [0]
            qft_ld_at = {}
            vh_ld_at = {}
            vb_at = {}
            eb_ld_at = {}
            pending_combine = []

            def emit_combine_step(step):
                (ch, cqc, chb) = pending_combine[0]
                zb = (cqc % 2) * 1536
                cb = (cqc % 2) * 512
                if step == 0:
                    @block.tensor
                    def _(t, ch=ch, cqc=cqc, chb=chb, zb=zb):
                        t.wait_ge(sm.h["cmb"], cmb_at[(ch, cqc)])
                        t.wait_ge(sm.h["ve"], ps5_need[0])
                        sm.inc(t.matmul(psb[3][0:65, 0:512], onesrow_f[0:1, 0:65],
                                        earow[0:1, chb * 2 * S + cqc * 512:
                                              chb * 2 * S + (cqc + 1) * 512],
                                        start=True, stop=True), "te", 1)
                    comb_te[0] = sm.val("te")
                elif step == 1:
                    @block.vector
                    def _(v, zb=zb, t1=comb_te[0]):
                        v.wait_ge(sm.h["te"], t1)
                        sm.inc(v.tensor_tensor(zsb[0:65, zb + 512:zb + 1024],
                                               zsb[0:65, zb + 512:zb + 1024],
                                               psb[3][0:65, 0:512], ALU.mult),
                               "ve", 1)
                    comb_ve[0] = sm.val("ve")
                elif step == 2:
                    @block.tensor
                    def _(t, ch=ch, cqc=cqc, chb=chb, v1=comb_ve[0]):
                        t.wait_ge(sm.h["ve"], v1)
                        sm.inc(t.matmul(psb[3][0:65, 0:512], onesrow_f[0:1, 0:65],
                                        earow[0:1, chb * 2 * S + S + cqc * 512:
                                              chb * 2 * S + S + (cqc + 1) * 512],
                                        start=True, stop=True), "te", 1)
                    comb_te[0] = sm.val("te")
                elif step == 3:
                    @block.vector
                    def _(v, zb=zb, cb=cb, t2=comb_te[0]):
                        v.wait_ge(sm.h["te"], t2)
                        v.tensor_tensor(zsb[0:65, zb + 1024:zb + 1536],
                                        zsb[0:65, zb + 1024:zb + 1536],
                                        psb[3][0:65, 0:512], ALU.mult)
                        v.drain()
                        v.tensor_tensor(csum[0:65, cb:cb + 512],
                                        zsb[0:65, zb:zb + 512],
                                        zsb[0:65, zb + 512:zb + 1024], ALU.add)
                        v.drain()
                        v.tensor_tensor(csum[0:65, cb:cb + 512],
                                        csum[0:65, cb:cb + 512],
                                        zsb[0:65, zb + 1024:zb + 1536], ALU.add)
                        v.drain()
                        v.reciprocal(rrow[0:1, cb:cb + 512],
                                     csum[64:65, cb:cb + 512])
                        sm.inc(v.drain(), "ve", 1)
                    comb_ve[0] = sm.val("ve")
                elif step == 4:
                    @block.tensor
                    def _(t, cb=cb, s2=comb_ve[0]):
                        t.wait_ge(sm.h["ve"], s2)
                        sm.inc(t.matmul(psb[3][0:64, 0:512], onesrow_f[0:1, 0:64],
                                        rrow[0:1, cb:cb + 512],
                                        start=True, stop=True), "te", 1)
                    comb_te[0] = sm.val("te")
                else:
                    (chx, cqcx, chbx) = pending_combine.pop(0)

                    @block.vector
                    def _(v, cb=cb, t3=comb_te[0], bneed=attnbuf_need[cqc % 2]):
                        v.wait_ge(sm.h["te"], t3)
                        v.wait_ge(sm.h["st"], bneed)
                        v.tensor_tensor(csum[0:64, cb:cb + 512],
                                        csum[0:64, cb:cb + 512],
                                        psb[3][0:64, 0:512], ALU.mult)
                        v.drain()
                        sm.inc(v.tensor_copy(
                            attnst[0:64, (cqc % 2) * 512:(cqc % 2 + 1) * 512],
                            csum[0:64, cb:cb + 512]), "ve", 1)
                    ps5_need[0] = sm.val("ve")
                    comb_done[(chx, cqcx)] = sm.val("ve")
                    veA = sm.val("ve")

                    @block.sync
                    def _(sync, chx=chx, cqcx=cqcx, veA=veA):
                        sync.wait_ge(sm.h["ve"], veA)
                        sm.inc(sync.dma_start(
                            out=attnd[64 * chx:64 * (chx + 1),
                                      cqcx * 512:(cqcx + 1) * 512],
                            in_=attnst[0:64, (cqcx % 2) * 512:(cqcx % 2 + 1) * 512]),
                            "st", 16)
                    attnbuf_need[cqcx % 2] = sm.val("st")

            comb_te = [0]
            comb_ve = [0]
            comb_se = [0]

            def first_qc(kt):
                return max(0, (kt - 7 + 3) // 4)

            def kts_first_at(qc):
                return [kt for kt in range(NT) if first_qc(kt) == qc]

            for h in range(H):
                hh = h % 2
                ft = h // 2
                hb = h % 2
                if True:

                    @block.sync
                    def _(sync, h=h, hb=hb, ft=ft):
                        if h == 0:
                            sync.wait_ge(sm.h["ph"], ph_a3)
                        if h >= 2:
                            sync.wait_ge(sm.h["pv"], pv_after[(h - 2, NQC - 1, NT - 1)])
                            sync.wait_ge(sm.h["qk"], qk_at[(h - 2, NQC - 1, NT - 1)])
                        for par in range(2):
                            sm.inc(sync.dma_start(
                                out=qhB[64 * par:64 * par + 64, hb * S:(hb + 1) * S],
                                in_=qd[ft * 128 + 64 * hh: ft * 128 + 64 * hh + 64, :]),
                                "ld", 16)
                            sm.inc(sync.dma_start(
                                out=khB[64 * par:64 * par + 64, hb * S:(hb + 1) * S],
                                in_=kd[ft * 128 + 64 * hh: ft * 128 + 64 * hh + 64, :]),
                                "ld", 16)
                        sm.inc(sync.dma_start(
                            out=vh_sb[:, hb * NT * 65:(hb + 1) * NT * 65],
                            in_=AP(vhatd.tensor, h * 65,
                                   [[1040, 128], [1040 * 128, NT], [1, 65]])), "ld", 16)
                        sm.inc(sync.dma_start(
                            out=earow[0:1, hb * 2 * S:(hb + 1) * 2 * S],
                            in_=AP(ead.tensor, h * 2 * S, [[2 * S, 1], [1, 2 * S]])),
                               "ld", 16)
                    vh_ld_at[h] = sm.val("ld")

                    @block.gpsimd
                    def _(g, h=h, hb=hb):
                        if h >= 2:
                            g.wait_ge(sm.h["vb"], vb_at[h - 2])
                        else:
                            g.wait_ge(sm.h["ph"], ph_a3)
                        for side in range(2):
                            sm.inc(g.dma_start(
                                out=ebraw[:, hb * 2 * NT + side * NT:
                                          hb * 2 * NT + (side + 1) * NT],
                                in_=AP(biasd.tensor, (h * 2 + side) * S,
                                       [[NT, 128], [1, NT]])), "ebl", 16)
                    eb_ld_at[h] = sm.val("ebl")

                    @block.vector
                    def _(v, h=h, hb=hb):
                        v.wait_ge(sm.h["ld"], vh_ld_at[h])
                        v.wait_ge(sm.h["ebl"], eb_ld_at[h])
                        op = None
                        for side in range(2):
                            dst = vhp_sb if side == 0 else vhn_sb
                            for kt in range(NT):
                                op = v.tensor_scalar(
                                    dst[:, hb * NT * 65 + kt * 65:
                                        hb * NT * 65 + (kt + 1) * 65],
                                    vh_sb[:, hb * NT * 65 + kt * 65:
                                          hb * NT * 65 + (kt + 1) * 65],
                                    ebraw[:, hb * 2 * NT + side * NT + kt:
                                          hb * 2 * NT + side * NT + kt + 1],
                                    None, ALU.mult)
                        sm.inc(op, "vb", 1)
                    vb_at[h] = sm.val("vb")

                    for qc in range(NQC):
                        want = []
                        if qc == 0:
                            want += kts_first_at(0) + kts_first_at(1)
                        elif qc < NQC - 1:
                            want += kts_first_at(qc + 1)
                        for kt in [k for k in want if (h, k) not in issued]:
                            issued.add((h, kt))
                            slot = kt % NBAND
                            qlo, qhi = _band_window(kt)
                            Fw = qhi - qlo
                            k0 = 128 * kt
                            need = slot_need[slot]

                            @block.sync
                            def _(sync, h=h, kt=kt, slot=slot, qlo=qlo, Fw=Fw,
                                  k0=k0, need=need):
                                sync.wait_ge(sm.h["qk"], need)
                                src = AP(cgd.tensor,
                                         (h * S + qlo) * WP + (DW + k0 - qlo),
                                         [[WP - 1, Fw], [1, 128]])
                                sm.inc(sync.dma_start(
                                    out=band_sb[:, slot * 1152: slot * 1152 + Fw],
                                    in_=src, transpose=True), "bandc", 16)
                            bandc_at[(h, kt)] = sm.val("bandc")

                            @block.gpsimd
                            def _(g, h=h, kt=kt, slot=slot, qlo=qlo, Fw=Fw, k0=k0):
                                g.wait_ge(sm.h["bandc"], bandc_at[(h, kt)])
                                src = AP(pgd.tensor,
                                         (h * S + k0) * WP + (qlo - k0 + DW),
                                         [[WP - 1, 128], [1, Fw]])
                                sm.inc(g.dma_start(
                                    out=band_sb[:, slot * 1152: slot * 1152 + Fw],
                                    in_=src, accum_op=ALU.add), "bandp", 16)
                            bandp_at[(h, kt)] = sm.val("bandp")

                        if qc == 0:
                            @block.tensor
                            def _(t, h=h):
                                t.wait_ge(sm.h["ld"], vh_ld_at[h])
                                t.wait_ge(sm.h["vb"], vb_at[h])

                        def emit_pv(t, kt2, h=h, qc=qc, hb=hb):
                            eb = ((kt2 // 2) % 2) * 1024 + (kt2 % 2) * 512
                            for z, c0, c1 in _zone_ranges(kt2, qc):
                                zi = {"near": 0, "farP": 1, "farN": 2}[z]
                                vsrc = (vh_sb, vhp_sb, vhn_sb)[zi]
                                sm.inc(t.matmul(
                                    psb[zi][0:65, c0:c1],
                                    vsrc[:, hb * NT * 65 + kt2 * 65:
                                         hb * NT * 65 + (kt2 + 1) * 65],
                                    expS[:, eb + c0: eb + c1],
                                    start=False, stop=True), "pv", 1)
                            pv_after[(h, qc, kt2)] = sm.val("pv")

                        def emit_qk_mm(t, kt, h=h, qc=qc, hb=hb):
                            sbank = psd[(kt // 2) % 2]
                            half = (kt % 2) * 512
                            par = kt % 2
                            mmi = t.matmul(
                                sbank[:, half:half + 512],
                                khB[64 * par:64 * par + 64,
                                    hb * S + kt * 128: hb * S + (kt + 1) * 128],
                                qhB[64 * par:64 * par + 64,
                                    hb * S + qc * 512: hb * S + (qc + 1) * 512],
                                start=True, stop=True,
                                tile_position=(64 * par, 0))
                            if not any(z == "near"
                                       for z, _, _ in _zone_ranges(kt, qc)):
                                sm.inc(mmi, "qk", 1)
                                qk_at[(h, qc, kt)] = sm.val("qk")

                        def emit_qk_post(t, kt, h=h, qc=qc, hb=hb):
                            sbank = psd[(kt // 2) % 2]
                            half = (kt % 2) * 512
                            qlo, qhi = _band_window(kt)
                            near = [(c0, c1)
                                    for z, c0, c1 in _zone_ranges(kt, qc)
                                    if z == "near"]
                            if not near:
                                return
                            slot = kt % NBAND
                            boff = slot * 1152 + (qc * 512 - qlo)
                            t.wait_ge(sm.h["bandp"], bandp_at[(h, kt)])
                            mmi = None
                            for c0, c1 in near:
                                mmi = t.matmul(
                                    sbank[:, half + c0: half + c1], ident_sb[:, :],
                                    band_sb[:, boff + c0: boff + c1],
                                    start=False, stop=True)
                            sm.inc(mmi, "qk", 1)
                            qk_at[(h, qc, kt)] = sm.val("qk")
                            slot_need[kt % NBAND] = sm.val("qk")

                        NP = NT // 2
                        ex_base = sm.val("ex")
                        for p in range(NP):
                            ex_at[(h, qc, 2 * p)] = ex_base + p + 1
                            ex_at[(h, qc, 2 * p + 1)] = ex_base + p + 1
                        comb_sched = {1: [0], 2: [1, 2], 3: [3], 8: [4], 10: [5]}
                        for p in range(NP):
                            if pending_combine:
                                for st_ in comb_sched.get(p, []):
                                    emit_combine_step(st_)
                            if p == 1:
                                @block.tensor
                                def _(t, h=h, qc=qc):
                                    if (h, qc) != (0, 0):
                                        pq = ((h, qc - 1) if qc > 0
                                              else (h - 1, NQC - 1))
                                        t.wait_ge(sm.h["cmb"], cmb_at[pq])
                                    for z in range(3):
                                        sm.inc(t.matmul(psb[z][0:65, 0:512],
                                                        zcol_b[0:1, :], zrow_b[0:1, :],
                                                        start=True, stop=True),
                                               "te", 1)

                            @block.tensor
                            def _(t, h=h, qc=qc, p=p):
                                prev = None
                                if (h, qc, 2 * (p - 2) + 1) in ex_at:
                                    prev = ex_at[(h, qc, 2 * (p - 2) + 1)]
                                elif qc > 0:
                                    prev = ex_at[(h, qc - 1, NT + 2 * (p - 2) + 1)]
                                elif h > 0:
                                    prev = ex_at[(h - 1, NQC - 1, NT + 2 * (p - 2) + 1)]
                                if prev is not None:
                                    t.wait_ge(sm.h["ex"], prev)
                                emit_qk_mm(t, 2 * p)
                                emit_qk_mm(t, 2 * p + 1)
                                emit_qk_post(t, 2 * p)
                                emit_qk_post(t, 2 * p + 1)
                                if p >= 2:
                                    emit_pv(t, 2 * (p - 2))
                                    emit_pv(t, 2 * (p - 2) + 1)

                        @block.tensor
                        def _(t, h=h, qc=qc):
                            t.wait_ge(sm.h["ex"], ex_at[(h, qc, NT - 1)])
                            for kt2 in (NT - 4, NT - 3, NT - 2, NT - 1):
                                emit_pv(t, kt2)

                        @block.scalar
                        def _(s, h=h, qc=qc):
                            for p in range(NP):
                                prev = None
                                if p >= 2:
                                    prev = pv_after[(h, qc, 2 * (p - 2) + 1)]
                                elif qc > 0:
                                    prev = pv_after[(h, qc - 1,
                                                     NT + 2 * (p - 2) + 1)]
                                elif h > 0:
                                    prev = pv_after[(h - 1, NQC - 1,
                                                     NT + 2 * (p - 2) + 1)]
                                if prev is not None:
                                    s.wait_ge(sm.h["pv"], prev)
                                s.wait_ge(sm.h["qk"],
                                          max(qk_at[(h, qc, 2 * p)],
                                              qk_at[(h, qc, 2 * p + 1)]))
                                eb = (p % 2) * 1024
                                act = s.activation(expS[:, eb: eb + 1024],
                                                   psd[p % 2][:, :], AF.Exp,
                                                   scale=INV_SCALE)
                                sm.inc(act, "ex", 1)
                            s.wait_ge(sm.h["pv"], pv_after[(h, qc, NT - 1)])
                            if (h, qc) != (0, 0):
                                pq = (h, qc - 1) if qc > 0 else (h - 1, NQC - 1)
                                s.wait_ge(sm.h["ve"], comb_done[pq])
                            zb = (qc % 2) * 1536
                            s.copy(zsb[0:65, zb:zb + 512], psb[0][0:65, 0:512])
                            s.copy(zsb[0:65, zb + 512:zb + 1024], psb[1][0:65, 0:512])
                            sm.inc(s.copy(zsb[0:65, zb + 1024:zb + 1536],
                                          psb[2][0:65, 0:512]), "cmb", 1)
                        assert sm.val("ex") == ex_base + NP
                        cmb_at[(h, qc)] = sm.val("cmb")

                        pending_combine.append((h, qc, hb))


            while pending_combine:
                for st_ in range(6):
                    emit_combine_step(st_)

            @block.vector
            def _(v):
                v.wait_ge(sm.h["st"], sm.val("st"))
                sm.inc(v.drain(), "ph", 1)
        ph_b = sm.val("ph")

        def emit_ln(src_sb, sq_sb, strow, track_max, out_sb, qc, st_wait):
            ve_in = sm.val("ve")

            @block.scalar
            def _(s, ve_in=ve_in):
                s.wait_ge(sm.h["ve"], ve_in)
                sm.inc(s.activation(sq_sb[:, :], src_sb[:, :], AF.Square), "se", 1)
            seq = sm.val("se")

            @block.tensor
            def _(t, ve_in=ve_in, seq=seq):
                t.wait_ge(sm.h["ve"], ve_in)
                for ftt in range(FT):
                    sm.inc(t.matmul(psb[0][0:1, 0:512], onescol_f[:, 0:1],
                                    src_sb[:, ftt * 512:(ftt + 1) * 512],
                                    start=(ftt == 0), stop=(ftt == FT - 1)), "te", 1)
                t.wait_ge(sm.h["se"], seq)
                for ftt in range(FT):
                    sm.inc(t.matmul(psb[1][0:1, 0:512], onescol_f[:, 0:1],
                                    sq_sb[:, ftt * 512:(ftt + 1) * 512],
                                    start=(ftt == 0), stop=(ftt == FT - 1)), "te", 1)
            tec = sm.val("te")

            @block.vector
            def _(v, tec=tec, strow=strow):
                v.wait_ge(sm.h["te"], tec)
                v.tensor_scalar(strow[0:1, 0:512], psb[0][0:1, 0:512],
                                1.0 / DM, None, ALU.mult)
                v.tensor_scalar(strow[0:1, 512:1024], psb[1][0:1, 0:512],
                                1.0 / DM, None, ALU.mult)
                v.drain()
                v.tensor_tensor(strow[0:1, 1024:1536], strow[0:1, 0:512],
                                strow[0:1, 0:512], ALU.mult)
                v.drain()
                v.tensor_tensor(strow[0:1, 1024:1536], strow[0:1, 512:1024],
                                strow[0:1, 1024:1536], ALU.subtract)
                v.drain()
                sm.inc(v.tensor_scalar(strow[0:1, 1024:1536], strow[0:1, 1024:1536],
                                       EPS, None, ALU.add), "ve", 1)
            vev = sm.val("ve")

            @block.scalar
            def _(s, vev=vev, strow=strow):
                s.wait_ge(sm.h["ve"], vev)
                sm.inc(s.activation(strow[0:1, 1536:2048], strow[0:1, 1024:1536],
                                    AF.Sqrt), "se", 1)
            ses = sm.val("se")

            @block.vector
            def _(v, ses=ses, strow=strow):
                v.wait_ge(sm.h["se"], ses)
                v.reciprocal(strow[0:1, 1536:2048], strow[0:1, 1536:2048])
                v.drain()
                v.tensor_tensor(strow[0:1, 2048:2560], strow[0:1, 1536:2048],
                                strow[0:1, 1536:2048], ALU.mult)
                v.drain()
                v.tensor_tensor(strow[0:1, 2048:2560], strow[0:1, 1024:1536],
                                strow[0:1, 2048:2560], ALU.mult)
                v.drain()
                v.tensor_scalar(strow[0:1, 2048:2560], strow[0:1, 2048:2560],
                                -0.5, 1.5, ALU.mult, ALU.add)
                v.drain()
                v.tensor_tensor(strow[0:1, 1536:2048], strow[0:1, 1536:2048],
                                strow[0:1, 2048:2560], ALU.mult)
                v.drain()
                sm.inc(v.tensor_tensor(strow[0:1, 2560:3072], strow[0:1, 0:512],
                                       strow[0:1, 1536:2048], ALU.mult), "ve", 1)
            ver = sm.val("ve")

            @block.tensor
            def _(t, ver=ver, strow=strow):
                t.wait_ge(sm.h["ve"], ver)
                sm.inc(t.matmul(psb[2][:, 0:512], onesrow_f[0:1, 0:128],
                                strow[0:1, 1536:2048], start=True, stop=True), "te", 1)
                sm.inc(t.matmul(psb[3][:, 0:512], onesrow_f[0:1, 0:128],
                                strow[0:1, 2560:3072], start=True, stop=True), "te", 1)
            teb = sm.val("te")

            @block.vector
            def _(v, teb=teb, track_max=track_max, st_wait=st_wait):
                v.wait_ge(sm.h["te"], teb)
                if st_wait:
                    v.wait_ge(sm.h["st"], st_wait)
                for ftt in range(FT):
                    sl = slice(ftt * 512, (ftt + 1) * 512)
                    v.tensor_tensor(sq_sb[:, sl], src_sb[:, sl],
                                    psb[2][:, 0:512], ALU.mult)
                    v.drain()
                    v.tensor_tensor(sq_sb[:, sl], sq_sb[:, sl],
                                    psb[3][:, 0:512], ALU.subtract)
                    v.drain()
                    if track_max is not None:
                        v.reduce_max(maxc[:, track_max + 1:track_max + 2] if False
                                     else qsc[:, 3:4],
                                     sq_sb[:, sl], axis=AX.X,
                                     apply_absolute_value=True)
                        v.drain()
                        v.tensor_tensor(maxc[:, track_max:track_max + 1],
                                        maxc[:, track_max:track_max + 1],
                                        qsc[:, 3:4], ALU.max)
                        v.drain()
                    sm.inc(v.tensor_copy(out_sb[:, sl], sq_sb[:, sl]), "ve", 1)
            return sm.val("ve")

        with (
            nc.sbuf_tensor("xc", [128, FT * 512], F32) as xc,
            nc.sbuf_tensor("sqc", [128, FT * 512], F32) as sqc,
            nc.sbuf_tensor("ac", [128, FT * 512], BF16) as ac,
            nc.sbuf_tensor("y1st", [128, FT * 512], BF16) as y1st,
            nc.sbuf_tensor("strow", [1, 3072], F32) as strow,
        ):
            @block.vector
            def _(v):
                v.wait_ge(sm.h["ph"], ph_b)
                v.memset(maxc[:, 0:1], 0.0)
                sm.inc(v.memset(maxc[:, 1:2], 0.0), "ve", 1)
            c1_ve0 = sm.val("ve")

            y1_store_at = 0
            for qc in range(NQC):
                ld0 = sm.val("ld")

                @block.sync
                def _(sync, qc=qc, prev_ve=sm.val("ve")):
                    sync.wait_ge(sm.h["ph"], ph_b)
                    sync.wait_ge(sm.h["ve"], prev_ve)
                    sm.inc(sync.dma_start(
                        out=AP(xc, 0, [[FT * 512, 128], [512, FT], [1, 512]]),
                        in_=AP(xfm.tensor, qc * 512,
                               [[S, 128], [S * 128, FT], [1, 512]])), "ld", 16)
                    sm.inc(sync.dma_start(
                        out=AP(ac, 0, [[FT * 512, 128], [512, FT], [1, 512]]),
                        in_=AP(attnd.tensor, qc * 512,
                               [[S, 128], [S * 128, FT], [1, 512]])), "ld", 16)
                c1_ld = sm.val("ld")

                @block.vector
                def _(v, c1_ld=c1_ld):
                    v.wait_ge(sm.h["ld"], c1_ld)
                    v.wait_ge(sm.h["ve"], c1_ve0)
                    sm.inc(v.tensor_tensor(xc[:, :], xc[:, :], ac[:, :], ALU.add),
                           "ve", 1)

                y1_done = emit_ln(xc, sqc, strow, 0, y1st, qc, y1_store_at)

                @block.sync
                def _(sync, qc=qc, y1_done=y1_done):
                    sync.wait_ge(sm.h["ve"], y1_done)
                    sm.inc(sync.dma_start(
                        out=AP(y1d.tensor, qc * 512,
                               [[S, 128], [S * 128, FT], [1, 512]]),
                        in_=AP(y1st, 0,
                               [[FT * 512, 128], [512, FT], [1, 512]])), "st", 16)
                y1_store_at = sm.val("st")

            c1_ve_end = sm.val("ve")

            @block.gpsimd
            def _(g, c1_ve_end=c1_ve_end, y1_store_at=y1_store_at):
                g.wait_ge(sm.h["ve"], c1_ve_end)
                sm.inc(g.dma_start(out=AP(cc1i.tensor, 0, [[1, 128], [1, 1]]),
                                   in_=maxc[:, 0:1]), "cc", 16)
                g.wait_ge(sm.h["cc"], sm.val("cc"))
                sm.inc(g.collective_compute(
                    "AllReduce", ALU.max, replica_groups=[list(range(8))],
                    ins=[cc1i.opt()], outs=[cc1o.opt()]), "cc", 1)
                g.wait_ge(sm.h["cc"], sm.val("cc"))
                sm.inc(g.dma_start(out=rowred[0:1, :],
                                   in_=AP(cc1o.tensor, 0, [[128, 1], [1, 128]])),
                       "cc", 16)
            cc1 = sm.val("cc")

            @block.sync
            def _(sync):
                sm.inc(sync.dma_start(out=sca[0:1, 1:2],
                                      in_=AP(clipx.tensor, 0, [[1, 1], [1, 1]])),
                       "ld", 16)
                sm.inc(sync.dma_start(out=sca[0:1, 6:7],
                                      in_=AP(clipd.tensor, 0, [[1, 1], [1, 1]])),
                       "ld", 16)
            clip_ld = sm.val("ld")

            @block.vector
            def _(v, cc1=cc1, clip_ld=clip_ld):
                v.wait_ge(sm.h["cc"], cc1)
                sm.inc(v.reduce_max(sca[0:1, 0:1], rowred[0:1, :], axis=AX.X),
                       "ve", 1)
            gm_ve = sm.val("ve")

            @block.scalar
            def _(s, gm_ve=gm_ve, clip_ld=clip_ld):
                s.wait_ge(sm.h["ve"], gm_ve)
                s.wait_ge(sm.h["ld"], clip_ld)
                sm.inc(s.activation(sca[0:1, 2:3], sca[0:1, 1:2], AF.Sigmoid),
                       "se", 1)
            sig_se = sm.val("se")

            @block.vector
            def _(v, sig_se=sig_se):
                v.wait_ge(sm.h["se"], sig_se)
                v.tensor_tensor(sca[0:1, 3:4], sca[0:1, 2:3], sca[0:1, 0:1], ALU.mult)
                v.drain()
                v.tensor_scalar(sca[0:1, 3:4], sca[0:1, 3:4], 1.0 / QMAX, None,
                                ALU.mult)
                v.drain()
                v.reciprocal(sca[0:1, 4:5], sca[0:1, 3:4])
                sm.inc(v.drain(), "ve", 1)
            sx_ve = sm.val("ve")

            @block.gpsimd
            def _(g, sx_ve=sx_ve):
                g.wait_ge(sm.h["ve"], sx_ve)
                sm.inc(g.dma_start(out=bounced[128:130], in_=sca[0:1, 3:5]), "cc", 16)
                g.wait_ge(sm.h["cc"], sm.val("cc"))
                sm.inc(g.dma_start(out=qsc[:, 1:2],
                                   in_=AP(bounced.tensor, 128, [[0, 128], [1, 1]])),
                       "cc", 16)
                sm.inc(g.dma_start(out=qsc[:, 0:1],
                                   in_=AP(bounced.tensor, 129, [[0, 128], [1, 1]])),
                       "cc", 16)
            sx_cc = sm.val("cc")

            @block.vector
            def _(v, sx_cc=sx_cc):
                v.wait_ge(sm.h["cc"], sx_cc)
                v.tensor_scalar(deqg[:, :], sg_sb[:, :], qsc[:, 1:2], None, ALU.mult)
                v.drain()
                sm.inc(v.tensor_scalar(dequ[:, :], su_sb[:, :], qsc[:, 1:2], None,
                                       ALU.mult), "ve", 1)
            deq_ve = sm.val("ve")

            if DBG:
                @block.sync
                def _(sync, deq_ve=deq_ve):
                    sync.wait_ge(sm.h["ve"], deq_ve)
                    sm.inc(sync.dma_start(out=scdbg[0:8], in_=sca[0:1, 0:8]), "st", 16)
                    sm.inc(sync.dma_start(out=AP(scdbg.tensor, 8, [[1, 128], [1, 1]]),
                                          in_=qsc[:, 0:1]), "st", 16)
                    sm.inc(sync.dma_start(out=AP(scdbg.tensor, 136, [[1, 128], [1, 1]]),
                                          in_=qsc[:, 1:2]), "st", 16)

        with (
            nc.sbuf_tensor("wgT_sb", [128, FT * DI], BF16) as wgT_sb,
            nc.sbuf_tensor("wuT_sb", [128, FT * DI], BF16) as wuT_sb,
            nc.sbuf_tensor("y1c", [128, FT * 512], BF16) as y1c,
            nc.sbuf_tensor("xtf", [128, FT * 512], F32) as xtf,
            nc.sbuf_tensor("xti", [128, FT * 512], I32) as xti,
            nc.sbuf_tensor("xqb", [128, FT * 512], BF16) as xqb,
            nc.sbuf_tensor("gsb", [128, 2 * 512], F32) as gsb,
            nc.sbuf_tensor("usb", [128, 2 * 512], F32) as usb,
            nc.sbuf_tensor("hfb", [128, 2 * 512], F32) as hfb,
            nc.sbuf_tensor("hst", [128, 2 * 512], BF16) as hst,
        ):
            @block.sync
            def _(sync, deq_ve=deq_ve):
                sync.wait_ge(sm.h["ve"], deq_ve)
                for j in range(FT):
                    src = AP(wgqd.tensor, j * 128, [[DM, DI], [1, 128]])
                    sm.inc(sync.dma_start(out=wgT_sb[:, j * DI:(j + 1) * DI],
                                          in_=src, transpose=True), "ld", 16)
                for j in range(FT):
                    src = AP(wuqd.tensor, j * 128, [[DM, DI], [1, 128]])
                    sm.inc(sync.dma_start(out=wuT_sb[:, j * DI:(j + 1) * DI],
                                          in_=src, transpose=True), "ld", 16)
            wgt_ld = sm.val("ld")

            h_store_at = 0
            hst_at = {}
            for qc in range(NQC):
                @block.sync
                def _(sync, qc=qc, prev=sm.val("ve"), hs=h_store_at):
                    sync.wait_ge(sm.h["ve"], prev)
                    sync.wait_ge(sm.h["st"], hs)
                    sm.inc(sync.dma_start(
                        out=AP(y1c, 0, [[FT * 512, 128], [512, FT], [1, 512]]),
                        in_=AP(y1d.tensor, qc * 512,
                               [[S, 128], [S * 128, FT], [1, 512]])), "ld", 16)
                c2_ld = sm.val("ld")

                @block.vector
                def _(v, c2_ld=c2_ld, wgt_ld=wgt_ld):
                    v.wait_ge(sm.h["ld"], max(c2_ld, wgt_ld))
                    v.tensor_scalar(xtf[:, :], y1c[:, :], qsc[:, 0:1], None, ALU.mult)
                    v.drain()
                    v.tensor_copy(xti[:, :], xtf[:, :])
                    v.drain()
                    sm.inc(v.tensor_scalar(xqb[:, :], xti[:, :], QMAX, -QMAX - 1.0,
                                           ALU.min, ALU.max), "ve", 1)
                xq_ve = sm.val("ve")

                if DBG:
                    @block.sync
                    def _(sync, qc=qc, xq_ve=xq_ve):
                        sync.wait_ge(sm.h["ve"], xq_ve)
                        sm.inc(sync.dma_start(
                            out=AP(xqdbg.tensor, qc * 512,
                                   [[S, 128], [S * 128, FT], [1, 512]]),
                            in_=AP(xqb, 0, [[FT * 512, 128], [512, FT], [1, 512]])),
                            "st", 16)

                se_g = {}
                ve_u = {}
                gd_ve_all = {}
                se_base = sm.val("se")
                for it in range(IT):
                    se_g[it] = se_base + it + 1
                for it in range(IT):
                    @block.tensor
                    def _(t, it=it, qc=qc, xq_ve=xq_ve):
                        t.wait_ge(sm.h["ve"], xq_ve)
                        if it >= 2:
                            t.wait_ge(sm.h["se"], se_g[it - 2])
                            t.wait_ge(sm.h["ve"], ve_u[it - 2])
                        for ht in range(FT):
                            mg = t.matmul(psb[it % 2][:, :],
                                          wgT_sb[:, ht * DI + it * 128:
                                                 ht * DI + (it + 1) * 128],
                                          xqb[:, ht * 512:(ht + 1) * 512],
                                          start=(ht == 0), stop=(ht == FT - 1))
                        sm.inc(mg, "te", 1)
                        for ht in range(FT):
                            mu = t.matmul(psb[2 + it % 2][:, :],
                                          wuT_sb[:, ht * DI + it * 128:
                                                 ht * DI + (it + 1) * 128],
                                          xqb[:, ht * 512:(ht + 1) * 512],
                                          start=(ht == 0), stop=(ht == FT - 1))
                        sm.inc(mu, "te", 1)
                    gu_te = sm.val("te")

                    @block.vector
                    def _(v, it=it, qc=qc, gu_te=gu_te):
                        v.wait_ge(sm.h["te"], gu_te - 1)
                        b = it % 2
                        if DBG:
                            v.tensor_copy(hfb[:, b * 512:(b + 1) * 512], psb[b][:, :])
                            v.drain()
                        sm.inc(v.tensor_scalar(gsb[:, b * 512:(b + 1) * 512],
                                               psb[b][:, :], deqg[:, it:it + 1],
                                               None, ALU.mult), "bd", 1)
                    gd_ve = sm.val("bd")

                    if DBG:
                        @block.sync
                        def _(sync, it=it, qc=qc, gd_ve=gd_ve):
                            sync.wait_ge(sm.h["bd"], gd_ve)
                            sm.inc(sync.dma_start(
                                out=gdbg[it * 128:(it + 1) * 128,
                                         qc * 512:(qc + 1) * 512],
                                in_=hfb[:, (it % 2) * 512:(it % 2 + 1) * 512]),
                                "st", 16)

                    gd_ve_all[it] = gd_ve

                    @block.vector
                    def _(v, it=it, qc=qc, gu_te=gu_te):
                        v.wait_ge(sm.h["te"], gu_te)
                        v.wait_ge(sm.h["se"], se_g[it])
                        pit = it - 2 if it >= 2 else IT - 2 + it
                        if pit in hst_at:
                            v.wait_ge(sm.h["st"], hst_at[pit])
                        b = it % 2
                        v.tensor_scalar(usb[:, b * 512:(b + 1) * 512],
                                        psb[2 + b][:, :], dequ[:, it:it + 1],
                                        None, ALU.mult)
                        v.drain()
                        v.tensor_tensor(hfb[:, b * 512:(b + 1) * 512],
                                        gsb[:, b * 512:(b + 1) * 512],
                                        usb[:, b * 512:(b + 1) * 512], ALU.mult)
                        v.drain()
                        v.reduce_max(qsc[:, 3:4], hfb[:, b * 512:(b + 1) * 512],
                                     axis=AX.X, apply_absolute_value=True)
                        v.drain()
                        v.tensor_tensor(maxc[:, 1:2], maxc[:, 1:2], qsc[:, 3:4],
                                        ALU.max)
                        v.drain()
                        sm.inc(v.tensor_copy(hst[:, b * 512:(b + 1) * 512],
                                             hfb[:, b * 512:(b + 1) * 512]), "ve", 1)
                    ve_u[it] = sm.val("ve")

                    @block.sync
                    def _(sync, it=it, qc=qc):
                        sync.wait_ge(sm.h["ve"], ve_u[it])
                        sm.inc(sync.dma_start(
                            out=hd[it * 128:(it + 1) * 128, qc * 512:(qc + 1) * 512],
                            in_=hst[:, (it % 2) * 512:(it % 2 + 1) * 512]), "st", 16)
                    hst_at[it] = sm.val("st")

                @block.scalar
                def _(s, qc=qc, gd_ve_all=dict(gd_ve_all)):
                    for it in range(IT):
                        s.wait_ge(sm.h["bd"], gd_ve_all[it])
                        b = it % 2
                        sm.inc(s.activation(gsb[:, b * 512:(b + 1) * 512],
                                            gsb[:, b * 512:(b + 1) * 512],
                                            AF.Silu), "se", 1)
                assert sm.val("se") == se_base + IT
                h_store_at = sm.val("st")

            c2_ve_end = sm.val("ve")

            @block.gpsimd
            def _(g, c2_ve_end=c2_ve_end):
                g.wait_ge(sm.h["ve"], c2_ve_end)
                sm.inc(g.dma_start(out=AP(cc2i.tensor, 0, [[1, 128], [1, 1]]),
                                   in_=maxc[:, 1:2]), "cc", 16)
                g.wait_ge(sm.h["cc"], sm.val("cc"))
                sm.inc(g.collective_compute(
                    "AllReduce", ALU.max, replica_groups=[list(range(8))],
                    ins=[cc2i.opt()], outs=[cc2o.opt()]), "cc", 1)
                g.wait_ge(sm.h["cc"], sm.val("cc"))
                sm.inc(g.dma_start(out=rowred[0:1, :],
                                   in_=AP(cc2o.tensor, 0, [[128, 1], [1, 128]])),
                       "cc", 16)
            cc2 = sm.val("cc")

            @block.vector
            def _(v, cc2=cc2):
                v.wait_ge(sm.h["cc"], cc2)
                sm.inc(v.reduce_max(sca[0:1, 5:6], rowred[0:1, :], axis=AX.X),
                       "ve", 1)
            gm2_ve = sm.val("ve")

            @block.scalar
            def _(s, gm2_ve=gm2_ve):
                s.wait_ge(sm.h["ve"], gm2_ve)
                sm.inc(s.activation(sca[0:1, 7:8], sca[0:1, 6:7], AF.Sigmoid),
                       "se", 1)
            sig2_se = sm.val("se")

            @block.vector
            def _(v, sig2_se=sig2_se):
                v.wait_ge(sm.h["se"], sig2_se)
                v.tensor_tensor(sca[0:1, 6:7], sca[0:1, 7:8], sca[0:1, 5:6], ALU.mult)
                v.drain()
                v.tensor_scalar(sca[0:1, 6:7], sca[0:1, 6:7], 1.0 / QMAX, None,
                                ALU.mult)
                v.drain()
                v.reciprocal(sca[0:1, 7:8], sca[0:1, 6:7])
                sm.inc(v.drain(), "ve", 1)
            sh_ve = sm.val("ve")

            @block.gpsimd
            def _(g, sh_ve=sh_ve):
                g.wait_ge(sm.h["ve"], sh_ve)
                sm.inc(g.dma_start(out=bounced[130:132], in_=sca[0:1, 6:8]), "cc", 16)
                g.wait_ge(sm.h["cc"], sm.val("cc"))
                sm.inc(g.dma_start(out=qsc[:, 3:4],
                                   in_=AP(bounced.tensor, 130, [[0, 128], [1, 1]])),
                       "cc", 16)
                sm.inc(g.dma_start(out=qsc[:, 2:3],
                                   in_=AP(bounced.tensor, 131, [[0, 128], [1, 1]])),
                       "cc", 16)
            sh_cc = sm.val("cc")

            @block.vector
            def _(v, sh_cc=sh_cc):
                v.wait_ge(sm.h["cc"], sh_cc)
                sm.inc(v.tensor_scalar(deqd[:, :], sd_sb[:, :], qsc[:, 3:4], None,
                                       ALU.mult), "ve", 1)
            deqd_ve = sm.val("ve")

        with (
            nc.sbuf_tensor("wdT_sb", [128, IT * DM], BF16) as wdT_sb,
            nc.sbuf_tensor("hc", [128, IT * 512], BF16) as hc,
            nc.sbuf_tensor("hqb", [128, IT * 512], BF16) as hqb,
            nc.sbuf_tensor("tf", [128, 2048], F32) as tf,
            nc.sbuf_tensor("ti", [128, 2048], I32) as ti,
            nc.sbuf_tensor("y1c2", [128, FT * 512], BF16) as y1c2,
            nc.sbuf_tensor("rc2", [128, FT * 512], F32) as rc2,
            nc.sbuf_tensor("sq2", [128, FT * 512], F32) as sq2,
            nc.sbuf_tensor("outst", [128, FT * 512], F32) as outst,
            nc.sbuf_tensor("strow2", [1, 3072], F32) as strow2,
        ):
            @block.sync
            def _(sync, deqd_ve=deqd_ve):
                sync.wait_ge(sm.h["ve"], deqd_ve)
                for j in range(IT):
                    src = AP(wdqd.tensor, j * 128, [[DI, DM], [1, 128]])
                    sm.inc(sync.dma_start(out=wdT_sb[:, j * DM:(j + 1) * DM],
                                          in_=src, transpose=True), "ld", 16)
            wdt_ld = sm.val("ld")

            out_st = 0
            for qc in range(NQC):
                @block.sync
                def _(sync, qc=qc, prev=sm.val("ve"), os=out_st):
                    sync.wait_ge(sm.h["ve"], prev)
                    sync.wait_ge(sm.h["st"], os)
                    sm.inc(sync.dma_start(
                        out=AP(hc, 0, [[IT * 512, 128], [512, IT], [1, 512]]),
                        in_=AP(hd.tensor, qc * 512,
                               [[S, 128], [S * 128, IT], [1, 512]])), "ld", 16)
                    sm.inc(sync.dma_start(
                        out=AP(y1c2, 0, [[FT * 512, 128], [512, FT], [1, 512]]),
                        in_=AP(y1d.tensor, qc * 512,
                               [[S, 128], [S * 128, FT], [1, 512]])), "ld", 16)
                c3_ld = sm.val("ld")

                groups = [(0, 4), (4, 8), (8, 12), (12, 16), (16, 20), (20, 22)]

                @block.vector
                def _(v, c3_ld=c3_ld, wdt_ld=wdt_ld, groups=groups):
                    v.wait_ge(sm.h["ld"], max(c3_ld, wdt_ld))
                    for g0, g1 in groups:
                        w = (g1 - g0) * 512
                        v.tensor_scalar(tf[:, 0:w], hc[:, g0 * 512:g1 * 512],
                                        qsc[:, 2:3], None, ALU.mult)
                        v.drain()
                        v.tensor_copy(ti[:, 0:w], tf[:, 0:w])
                        v.drain()
                        v.tensor_scalar(hqb[:, g0 * 512:g1 * 512], ti[:, 0:w],
                                        QMAX, -QMAX - 1.0, ALU.min, ALU.max)
                        v.drain()
                    sm.inc(v.drain(), "ve", 1)
                hq_ve = sm.val("ve")

                ve_d = {}
                for fo in range(FT):
                    @block.tensor
                    def _(t, fo=fo, hq_ve=hq_ve):
                        t.wait_ge(sm.h["ve"], hq_ve)
                        if fo >= 2:
                            t.wait_ge(sm.h["ve"], ve_d[fo - 2])
                        for it in range(IT):
                            md = t.matmul(psb[fo % 2][:, :],
                                          wdT_sb[:, it * DM + fo * 128:
                                                 it * DM + (fo + 1) * 128],
                                          hqb[:, it * 512:(it + 1) * 512],
                                          start=(it == 0), stop=(it == IT - 1))
                        sm.inc(md, "te", 1)
                    d_te = sm.val("te")

                    @block.vector
                    def _(v, fo=fo, d_te=d_te):
                        v.wait_ge(sm.h["te"], d_te)
                        sm.inc(v.scalar_tensor_tensor(
                            rc2[:, fo * 512:(fo + 1) * 512], psb[fo % 2][:, :],
                            deqd[:, fo:fo + 1], y1c2[:, fo * 512:(fo + 1) * 512],
                            ALU.mult, ALU.add), "ve", 1)
                    ve_d[fo] = sm.val("ve")

                out_done = emit_ln(rc2, sq2, strow2, None, outst, qc, out_st)

                @block.sync
                def _(sync, qc=qc, out_done=out_done):
                    sync.wait_ge(sm.h["ve"], out_done)
                    sm.inc(sync.dma_start(
                        out=AP(outfm.tensor, qc * 512,
                               [[S, 128], [S * 128, FT], [1, 512]]),
                        in_=AP(outst, 0,
                               [[FT * 512, 128], [512, FT], [1, 512]])), "st", 16)
                out_st = sm.val("st")

            @block.sync
            def _(sync, out_st=out_st):
                sync.wait_ge(sm.h["st"], out_st)
    return nc


_FS = _bucket_table(-DW, DW)


def kernel(L0_attn_hidden_states, L0_attn_rel_embeddings, L0_attn_Wq, L0_attn_bq,
           L0_attn_Wk, L0_attn_bk, L0_attn_Wv, L0_attn_bv, L0_ffn_clip_x,
           L0_ffn_clip_down, L0_ffn_w_gate, L0_ffn_w_up, L0_ffn_w_down):
    x = np.asarray(L0_attn_hidden_states)
    rel = np.asarray(L0_attn_rel_embeddings)
    relg = np.ascontiguousarray(rel[_FS].T)
    relgr = np.ascontiguousarray(rel[_FS[::-1]].T)
    shared = dict(
        wqi=np.ascontiguousarray(L0_attn_Wq), wki=np.ascontiguousarray(L0_attn_Wk),
        wvi=np.ascontiguousarray(L0_attn_Wv), bqi=np.ascontiguousarray(L0_attn_bq),
        bki=np.ascontiguousarray(L0_attn_bk), bvi=np.ascontiguousarray(L0_attn_bv),
        relgT=relg, relgrT=relgr,
        wgi=np.ascontiguousarray(L0_ffn_w_gate), wui=np.ascontiguousarray(L0_ffn_w_up),
        wdi=np.ascontiguousarray(L0_ffn_w_down),
        clipx=np.ascontiguousarray(L0_ffn_clip_x),
        clipd=np.ascontiguousarray(L0_ffn_clip_down),
        ident=np.eye(128, dtype=BF),
    )
    nc = build_nc()
    in_maps = []
    for b in range(8):
        m = dict(shared)
        m["xfm"] = np.ascontiguousarray(x[b].T)
        in_maps.append(m)
    res = run_bass_kernel_spmd(nc, in_maps, core_ids=list(range(8)))
    out = np.stack([np.asarray(res.results[b]["outfm"]).T for b in range(8)])
    return np.ascontiguousarray(out.astype(np.float32))


# version 18 (final)
# speedup vs baseline: 1.2325x; 1.0622x over previous
import contextlib
import math
import os

import numpy as np
import ml_dtypes
import concourse.bass as bass
import concourse.mybir as mybir
from concourse.ap import AP
from concourse.bass_utils import run_bass_kernel_spmd

F32 = mybir.dt.float32
BF16 = mybir.dt.bfloat16
I32 = mybir.dt.int32
AF = mybir.ActivationFunctionType
ALU = mybir.AluOpType
AX = mybir.AxisListType
BF = ml_dtypes.bfloat16

H, D, S, DM, DI = 16, 64, 4096, 1024, 2816
NT, FT, IT, NQC = S // 128, DM // 128, DI // 128, S // 512
BUCKETS, MAX_POS, ATT_SPAN = 256, 512, 256
A2 = 2 * ATT_SPAN
SCALE = math.sqrt(D * 3)
INV_SCALE = 1.0 / SCALE
DW = 640
WB = 2 * DW + 1
WP = 1281
QMAX = 127.0
EPS = 1e-5
NBAND = 17
DBG = os.environ.get("KDBG", "") == "1"
COL_FP = DW + 520
COL_FN = DW - 520
RCOL_FP = (WB - 1) - COL_FP
RCOL_FN = (WB - 1) - COL_FN


def _bucket_table(dmin, dmax):
    d = np.arange(dmin, dmax + 1)
    rel = np.clip(d, -MAX_POS + 1, MAX_POS - 1)
    sign = np.sign(rel).astype(np.float64)
    abs_pos = np.maximum(np.abs(rel), 1)
    log_base = math.log((MAX_POS - 1) / (BUCKETS // 2))
    mid = BUCKETS // 2
    log_pos = np.ceil(
        np.log(abs_pos.astype(np.float32) / mid) / np.float32(log_base) * (mid - 1)
    ) + mid
    b = np.where(abs_pos < mid, abs_pos.astype(np.float64), log_pos) * sign
    return np.clip(b.astype(np.int64) + ATT_SPAN, 0, A2 - 1)


def _zone_ranges(kt, qc):
    out = []
    for i in range(4):
        dd = (4 * qc + i) - kt
        z = "near" if abs(dd) <= 4 else ("farP" if dd >= 5 else "farN")
        if out and out[-1][0] == z:
            out[-1] = (z, out[-1][1], 128 * (i + 1))
        else:
            out.append((z, 128 * i, 128 * (i + 1)))
    return out


def _band_window(kt):
    return max(0, 128 * kt - 512), min(S, 128 * kt + 640)


class _Sems:
    def __init__(self):
        self.cnt = {}
        self.h = {}

    def add(self, name, handle):
        self.h[name] = handle
        self.cnt[name] = 0

    def inc(self, instr, name, n=1):
        instr.then_inc(self.h[name], n)
        self.cnt[name] += n
        return self.cnt[name]

    def val(self, name):
        return self.cnt[name]


def build_nc():
    nc = bass.Bass("TRN2", target_bir_lowering=False)

    xfm = nc.dram_tensor("xfm", [DM, S], F32, kind="ExternalInput").ap()
    wqi = nc.dram_tensor("wqi", [DM, DM], F32, kind="ExternalInput").ap()
    wki = nc.dram_tensor("wki", [DM, DM], F32, kind="ExternalInput").ap()
    wvi = nc.dram_tensor("wvi", [DM, DM], F32, kind="ExternalInput").ap()
    bqi = nc.dram_tensor("bqi", [DM], F32, kind="ExternalInput").ap()
    bki = nc.dram_tensor("bki", [DM], F32, kind="ExternalInput").ap()
    bvi = nc.dram_tensor("bvi", [DM], F32, kind="ExternalInput").ap()
    relgT = nc.dram_tensor("relgT", [DM, WB], F32, kind="ExternalInput").ap()
    relgrT = nc.dram_tensor("relgrT", [DM, WB], F32, kind="ExternalInput").ap()
    wgi = nc.dram_tensor("wgi", [DI, DM], F32, kind="ExternalInput").ap()
    wui = nc.dram_tensor("wui", [DI, DM], F32, kind="ExternalInput").ap()
    wdi = nc.dram_tensor("wdi", [DM, DI], F32, kind="ExternalInput").ap()
    ident = nc.dram_tensor("ident", [128, 128], BF16, kind="ExternalInput").ap()
    clipx = nc.dram_tensor("clipx", [1], F32, kind="ExternalInput").ap()
    clipd = nc.dram_tensor("clipd", [1], F32, kind="ExternalInput").ap()
    outfm = nc.dram_tensor("outfm", [DM, S], F32, kind="ExternalOutput").ap()

    dk = "ExternalOutput" if DBG else "Internal"
    qd = nc.dram_tensor("qd", [DM, S], BF16, kind=dk).ap()
    kd = nc.dram_tensor("kd", [DM, S], BF16, kind=dk).ap()
    vhatd = nc.dram_tensor("vhatd", [S, 16 * 65], BF16, kind=dk).ap()
    posd = nc.dram_tensor("posd", [2 * DM, WB], BF16, kind=dk).ap()
    pgd = nc.dram_tensor("pgd", [H * S, WP], BF16, kind="Internal").ap()
    cgd = nc.dram_tensor("cgd", [H * S, WP], BF16, kind="Internal").ap()
    ead = nc.dram_tensor("ead", [H * 2, S], F32, kind=dk).ap()
    biasd = nc.dram_tensor("biasd", [H * 2, S], BF16, kind=dk).ap()
    attnd = nc.dram_tensor("attnd", [DM, S], BF16, kind=dk).ap()
    y1d = nc.dram_tensor("y1d", [DM, S], BF16, kind=dk).ap()
    hd = nc.dram_tensor("hd", [DI, S], BF16, kind=dk).ap()
    xqdbg = nc.dram_tensor("xqdbg", [DM, S], BF16, kind=dk).ap()
    gdbg = nc.dram_tensor("gdbg", [DI, S], F32, kind=dk).ap()
    scdbg = nc.dram_tensor("scdbg", [300], F32, kind=dk).ap()
    udbg = nc.dram_tensor("udbg", [DI, S], F32, kind=dk).ap()
    wgqd = nc.dram_tensor("wgqd", [DI, DM], BF16, kind=dk).ap()
    wuqd = nc.dram_tensor("wuqd", [DI, DM], BF16, kind=dk).ap()
    wdqd = nc.dram_tensor("wdqd", [DM, DI], BF16, kind="Internal").ap()
    bounced = nc.dram_tensor("bounced", [256], F32, kind="Internal").ap()
    cc1i = nc.dram_tensor("cc1i", [128], F32, kind="Internal").ap()
    cc1o = nc.dram_tensor("cc1o", [128], F32, kind="Internal").ap()
    cc2i = nc.dram_tensor("cc2i", [128], F32, kind="Internal").ap()
    cc2o = nc.dram_tensor("cc2o", [128], F32, kind="Internal").ap()

    sm = _Sems()
    stack = contextlib.ExitStack()
    for name in ["ld", "gl", "te", "se", "ve", "st", "qk", "ex", "pv", "bd",
                 "cmb", "bandc", "bandp", "cc", "ph", "vb", "ebl"]:
        sm.add(name, stack.enter_context(nc.semaphore(name)))

    with stack:
        block = stack.enter_context(nc.Block())
        biasq = stack.enter_context(nc.sbuf_tensor("biasq", [128, FT], F32))
        biask = stack.enter_context(nc.sbuf_tensor("biask", [128, FT], F32))
        bvrowb = stack.enter_context(nc.sbuf_tensor("bvrowb", [1, DM], BF16))
        onesrow_f = stack.enter_context(nc.sbuf_tensor("onesrow_f", [1, 128], F32))
        onescol_f = stack.enter_context(nc.sbuf_tensor("onescol_f", [128, 1], F32))
        onescolb = stack.enter_context(nc.sbuf_tensor("onescolb", [1, 128], BF16))
        zrow_b = stack.enter_context(nc.sbuf_tensor("zrow_b", [1, 512], BF16))
        zcol_b = stack.enter_context(nc.sbuf_tensor("zcol_b", [1, 65], BF16))
        sg_sb = stack.enter_context(nc.sbuf_tensor("sg_sb", [128, IT], F32))
        su_sb = stack.enter_context(nc.sbuf_tensor("su_sb", [128, IT], F32))
        sd_sb = stack.enter_context(nc.sbuf_tensor("sd_sb", [128, FT], F32))
        deqg = stack.enter_context(nc.sbuf_tensor("deqg", [128, IT], F32))
        dequ = stack.enter_context(nc.sbuf_tensor("dequ", [128, IT], F32))
        deqd = stack.enter_context(nc.sbuf_tensor("deqd", [128, FT], F32))
        ident_sb = stack.enter_context(nc.sbuf_tensor("ident_sb", [128, 128], BF16))
        eacc = stack.enter_context(nc.sbuf_tensor("eacc", [128, 4 * NT], F32))
        bcacc = stack.enter_context(nc.sbuf_tensor("bcacc", [128, 4 * NT], BF16))
        maxc = stack.enter_context(nc.sbuf_tensor("maxc", [128, 2], F32))
        rowred = stack.enter_context(nc.sbuf_tensor("rowred", [1, 128], F32))
        sca = stack.enter_context(nc.sbuf_tensor("sca", [1, 8], F32))
        qsc = stack.enter_context(nc.sbuf_tensor("qsc", [128, 4], F32))
        psb = [stack.enter_context(nc.psum_tensor(f"ps{i}", [128, 512], F32))
               for i in range(4)]
        psd = [stack.enter_context(nc.psum_tensor(f"psd{i}", [128, 1024], F32))
               for i in range(2)]

        X_stack = contextlib.ExitStack()
        X_sb = X_stack.enter_context(nc.sbuf_tensor("X_sb", [128, FT * S], BF16))
        Wv_sb = X_stack.enter_context(nc.sbuf_tensor("Wv_sb", [128, FT * DM], BF16))
        vstage = X_stack.enter_context(nc.sbuf_tensor("vstage", [128, 2 * 1040], BF16))
        A2_stack = contextlib.ExitStack()
        Wq_sb = A2_stack.enter_context(nc.sbuf_tensor("Wq_sb", [128, FT * DM], BF16))
        Wk_sb = A2_stack.enter_context(nc.sbuf_tensor("Wk_sb", [128, FT * DM], BF16))
        relc = A2_stack.enter_context(nc.sbuf_tensor("relc", [128, FT * 512], BF16))
        posst = A2_stack.enter_context(nc.sbuf_tensor("posst", [128, 2 * 512], BF16))
        qstage = A2_stack.enter_context(nc.sbuf_tensor("qstage", [128, 2 * 512], BF16))
        kstage = A2_stack.enter_context(nc.sbuf_tensor("kstage", [128, 2 * 512], BF16))

        with (
            nc.sbuf_tensor("wstage", [128, 2 * DI], F32) as wstage,
            nc.sbuf_tensor("wtmp", [128, DI], F32) as wtmp,
            nc.sbuf_tensor("wint", [128, DI], I32) as wint,
            nc.sbuf_tensor("wqb", [128, 2 * DI], BF16) as wqb,
            nc.sbuf_tensor("rcol", [128, 1], F32) as rcol,
        ):
            jobs = []
            for w_in, w_out, s_sb, ncols, ntl in ((wgi, wgqd, sg_sb, DM, IT),
                                                  (wui, wuqd, su_sb, DM, IT),
                                                  (wdi, wdqd, sd_sb, DI, FT)):
                for t in range(ntl):
                    jobs.append((w_in, w_out, s_sb, ncols, t))
            ld0, ve0, st0 = sm.val("ld"), sm.val("ve"), sm.val("st")

            @block.sync
            def _(sync):
                for j, (w_in, w_out, s_sb, ncols, t) in enumerate(jobs):
                    b = j % 2
                    if j >= 2:
                        sync.wait_ge(sm.h["ve"], ve0 + (j - 1))
                    sm.inc(sync.dma_start(out=wstage[:, b * DI:b * DI + ncols],
                                          in_=w_in[t * 128:(t + 1) * 128, :]), "ld", 16)

            @block.vector
            def _(v):
                for j, (w_in, w_out, s_sb, ncols, t) in enumerate(jobs):
                    b = j % 2
                    v.wait_ge(sm.h["ld"], ld0 + 16 * (j + 1))
                    if j >= 2:
                        v.wait_ge(sm.h["st"], st0 + 16 * (j - 1))
                    st_ap = wstage[:, b * DI:b * DI + ncols]
                    v.reduce_max(s_sb[:, t:t + 1], st_ap, axis=AX.X,
                                 apply_absolute_value=True)
                    v.drain()
                    v.tensor_scalar(s_sb[:, t:t + 1], s_sb[:, t:t + 1], 1.0 / QMAX,
                                    None, ALU.mult)
                    v.drain()
                    v.reciprocal(rcol[:, 0:1], s_sb[:, t:t + 1])
                    v.drain()
                    v.tensor_scalar(wtmp[:, 0:ncols], st_ap, rcol[:, 0:1], None, ALU.mult)
                    v.drain()
                    v.tensor_copy(wint[:, 0:ncols], wtmp[:, 0:ncols])
                    v.drain()
                    sm.inc(v.tensor_copy(wqb[:, b * DI:b * DI + ncols],
                                         wint[:, 0:ncols]), "ve", 1)

            @block.scalar
            def _(s):
                for j, (w_in, w_out, s_sb, ncols, t) in enumerate(jobs):
                    b = j % 2
                    s.wait_ge(sm.h["ve"], ve0 + (j + 1))
                    sm.inc(s.dma_start(out=w_out[t * 128:(t + 1) * 128, :],
                                       in_=wqb[:, b * DI:b * DI + ncols]), "st", 16)
            w0_st = sm.val("st")

            @block.vector
            def _(v):
                v.wait_ge(sm.h["st"], w0_st)
                sm.inc(v.drain(), "ph", 1)
        ph_w0 = sm.val("ph")

        with X_stack:
            @block.gpsimd
            def _(g):
                for ft in range(FT):
                    sm.inc(g.dma_start(out=X_sb[:, ft * S:(ft + 1) * S],
                                       in_=xfm[ft * 128:(ft + 1) * 128, :]), "gl", 16)
                sm.inc(g.dma_start(out=bvrowb[0:1, :],
                                   in_=AP(bvi.tensor, 0, [[DM, 1], [1, DM]])), "gl", 16)
            a_glx = sm.val("gl")

            @block.sync
            def _(sync):
                sm.inc(sync.dma_start(out=biasq[:, :],
                                      in_=AP(bqi.tensor, 0, [[1, 128], [128, FT]]),
                                      allow_slow_non_contiguous=True), "ld", 16)
                sm.inc(sync.dma_start(out=biask[:, :],
                                      in_=AP(bki.tensor, 0, [[1, 128], [128, FT]]),
                                      allow_slow_non_contiguous=True), "ld", 16)
                sm.inc(sync.dma_start(out=ident_sb[:, :], in_=ident[:, :]), "ld", 16)
            a_ld = sm.val("ld")

            @block.vector
            def _(v):
                v.memset(onesrow_f[0:1, :], 1.0)
                v.memset(onescol_f[:, 0:1], 1.0)
                v.memset(onescolb[0:1, :], 1.0)
                v.memset(zrow_b[0:1, :], 1.0)
                sm.inc(v.memset(zcol_b[0:1, :], 0.0), "ve", 1)
            a_ve = sm.val("ve")

            with A2_stack:
                @block.gpsimd
                def _(g):
                    for w_in, w_sb in ((wqi, Wq_sb), (wki, Wk_sb)):
                        for ht in range(FT):
                            sm.inc(g.dma_start(out=w_sb[:, ht * DM:(ht + 1) * DM],
                                               in_=w_in[ht * 128:(ht + 1) * 128, :]),
                                   "gl", 16)
                a_glw = sm.val("gl")

                dchw = [512, 512, WB - 1024]
                te0, se0, st1, gl1 = (sm.val(x) for x in ("te", "se", "st", "gl"))
                gl_at = {ci: gl1 + 16 * FT * (ci + 1) for ci in range(6)}

                @block.gpsimd
                def _(g):
                    for ci in range(6):
                        side, dc = divmod(ci, 3)
                        w = dchw[dc]
                        src = relgT if side == 0 else relgrT
                        if ci >= 1:
                            g.wait_ge(sm.h["te"], te0 + FT * ci)
                        for ht in range(FT):
                            sm.inc(g.dma_start(
                                out=relc[:, ht * 512:ht * 512 + w],
                                in_=src[ht * 128:(ht + 1) * 128, dc * 512:dc * 512 + w]),
                                "gl", 16)

                @block.tensor
                def _(t):
                    t.wait_ge(sm.h["gl"], a_glw)
                    t.wait_ge(sm.h["ld"], a_ld)
                    for ci in range(6):
                        side, dc = divmod(ci, 3)
                        w = dchw[dc]
                        w_sb = Wq_sb if side == 0 else Wk_sb
                        t.wait_ge(sm.h["gl"], gl_at[ci])
                        for ft in range(FT):
                            j = ci * FT + ft
                            pp = psb[j % 2]
                            if j >= 2:
                                t.wait_ge(sm.h["se"], se0 + j - 1)
                            for ht in range(FT):
                                mmi = t.matmul(
                                    pp[:, 0:w],
                                    w_sb[:, ht * DM + ft * 128: ht * DM + (ft + 1) * 128],
                                    relc[:, ht * 512:ht * 512 + w],
                                    start=(ht == 0), stop=(ht == FT - 1))
                            sm.inc(mmi, "te", 1)
                a_tep = sm.val("te")

                @block.vector
                def _(v):
                    for ci in range(6):
                        side, dc = divmod(ci, 3)
                        w = dchw[dc]
                        bcol = biasq if side == 0 else biask
                        for ft in range(FT):
                            j = ci * FT + ft
                            pp = psb[j % 2]
                            v.wait_ge(sm.h["te"], te0 + j + 1)
                            if j >= 2:
                                v.wait_ge(sm.h["st"], st1 + 16 * (j - 1))
                            sm.inc(v.tensor_scalar(
                                posst[:, (j % 2) * 512:(j % 2) * 512 + w],
                                pp[:, 0:w], bcol[:, ft:ft + 1], None, ALU.add),
                                   "se", 1)

                @block.sync
                def _(sync):
                    for ci in range(6):
                        side, dc = divmod(ci, 3)
                        w = dchw[dc]
                        for ft in range(FT):
                            j = ci * FT + ft
                            sync.wait_ge(sm.h["se"], se0 + j + 1)
                            sm.inc(sync.dma_start(
                                out=posd[side * DM + ft * 128: side * DM + (ft + 1) * 128,
                                         dc * 512:dc * 512 + w],
                                in_=posst[:, (j % 2) * 512:(j % 2) * 512 + w]), "st", 16)

                te1, se1, st2 = sm.val("te"), sm.val("se"), sm.val("st")
                projjobs = [(w, ft, ch) for w in ("k", "q") for ft in range(FT)
                            for ch in range(8)]
                qof = {}
                nq = 0
                for j, (which, ft, ch) in enumerate(projjobs):
                    if which == "q":
                        qof[j] = nq
                        nq += 1

                @block.tensor
                def _(t):
                    t.wait_ge(sm.h["gl"], a_glx)
                    t.wait_ge(sm.h["se"], se1)
                    for j, (which, ft, ch) in enumerate(projjobs):
                        pp = psb[j % 2]
                        if j >= 2:
                            t.wait_ge(sm.h["se"], se1 + j - 1)
                        w_sb = Wk_sb if which == "k" else Wq_sb
                        for ht in range(FT):
                            mmi = t.matmul(
                                pp[:, :],
                                w_sb[:, ht * DM + ft * 128: ht * DM + (ft + 1) * 128],
                                X_sb[:, ht * S + ch * 512: ht * S + (ch + 1) * 512],
                                start=(ht == 0), stop=(ht == FT - 1))
                        sm.inc(mmi, "te", 1)

                @block.vector
                def _(v):
                    for j, (which, ft, ch) in enumerate(projjobs):
                        pp = psb[j % 2]
                        v.wait_ge(sm.h["te"], te1 + j + 1)
                        bcol = biask if which == "k" else biasq
                        stg = kstage if which == "k" else qstage
                        qb = j % 2
                        if j >= 4:
                            v.wait_ge(sm.h["st"], st2 + 16 * (j - 3))
                        dst = stg[:, qb * 512:(qb + 1) * 512]
                        sm.inc(v.tensor_scalar(dst, pp[:, :], bcol[:, ft:ft + 1],
                                               None, ALU.add), "se", 1)
                a_seq = sm.val("se")

                @block.sync
                def _(sync):
                    for j, (which, ft, ch) in enumerate(projjobs):
                        qb = j % 2
                        dstd = kd if which == "k" else qd
                        stg = kstage if which == "k" else qstage
                        sync.wait_ge(sm.h["se"], se1 + j + 1)
                        sm.inc(sync.dma_start(
                            out=dstd[ft * 128:(ft + 1) * 128, ch * 512:(ch + 1) * 512],
                            in_=stg[:, qb * 512:(qb + 1) * 512]), "st", 16)
                a_stq = sm.val("st")

                @block.vector
                def _(v):
                    v.wait_ge(sm.h["se"], a_seq)
                    v.wait_ge(sm.h["st"], a_stq)
                    sm.inc(v.drain(), "ph", 1)
            ph_a1 = sm.val("ph")

            if True:
                tev0, vev0, stv0 = (sm.val(x) for x in ("te", "ve", "st"))

                @block.gpsimd
                def _(g):
                    g.wait_ge(sm.h["ph"], ph_a1)
                    for ht in range(FT):
                        sm.inc(g.dma_start(out=Wv_sb[:, ht * DM:(ht + 1) * DM],
                                           in_=wvi[ht * 128:(ht + 1) * 128, :]), "gl", 16)
                a_glv = sm.val("gl")

                vjobs = [(tt, fc) for tt in range(NT) for fc in range(2)]

                @block.tensor
                def _(t):
                    t.wait_ge(sm.h["gl"], a_glv)
                    t.wait_ge(sm.h["ve"], a_ve)
                    for j, (tt, fc) in enumerate(vjobs):
                        pp = psb[2 + j % 2]
                        if j >= 2:
                            t.wait_ge(sm.h["ve"], vev0 + j - 1)
                        for ht in range(FT):
                            t.matmul(pp[:, :],
                                     X_sb[:, ht * S + tt * 128: ht * S + (tt + 1) * 128],
                                     Wv_sb[:, ht * DM + fc * 512: ht * DM + (fc + 1) * 512],
                                     start=(ht == 0), stop=False)
                        sm.inc(t.matmul(pp[:, :], onescolb[0:1, :],
                                        bvrowb[0:1, fc * 512:(fc + 1) * 512],
                                        start=False, stop=True), "te", 1)

                @block.vector
                def _(v):
                    for j, (tt, fc) in enumerate(vjobs):
                        pp = psb[2 + j % 2]
                        vb = tt % 2
                        v.wait_ge(sm.h["te"], tev0 + j + 1)
                        if tt >= 2 and fc == 0:
                            v.wait_ge(sm.h["st"], stv0 + 16 * (tt - 1))
                        dst = AP(vstage, vb * 1040 + fc * 8 * 65,
                                 [[2080, 128], [65, 8], [1, 64]])
                        v.tensor_copy(dst, AP(pp, 0, [[512, 128], [64, 8], [1, 64]]))
                        if fc == 1:
                            v.drain()
                            sm.inc(v.memset(
                                AP(vstage, vb * 1040 + 64, [[2080, 128], [65, 16]]),
                                1.0), "ve", 1)
                        else:
                            sm.inc(v.drain(), "ve", 1)

                @block.sync
                def _(sync):
                    for tt in range(NT):
                        vb = tt % 2
                        sync.wait_ge(sm.h["ve"], vev0 + 2 * tt + 2)
                        sm.inc(sync.dma_start(out=vhatd[tt * 128:(tt + 1) * 128, :],
                                              in_=vstage[:, vb * 1040:(vb + 1) * 1040]),
                               "st", 16)
                a_stv = sm.val("st")

                @block.vector
                def _(v):
                    v.wait_ge(sm.h["st"], a_stv)
                    sm.inc(v.drain(), "ph", 1)
            ph_a2 = sm.val("ph")
        with (
            nc.sbuf_tensor("qftA", [128, S], BF16) as qftA,
            nc.sbuf_tensor("kftA", [128, S], BF16) as kftA,
            nc.sbuf_tensor("posq_sb", [128, WB], BF16) as posq_sb,
            nc.sbuf_tensor("posk_sb", [128, WB], BF16) as posk_sb,
            nc.sbuf_tensor("cpst", [128, 4 * WP], BF16) as cpst,
            nc.sbuf_tensor("ecb", [128, 128], BF16) as ecb,
            nc.sbuf_tensor("etr", [128, 128], F32) as etr,
        ):
            dchw = [512, 512, WB - 1024]
            ea_tr_ve = {}
            ea_st_prev = None
            a3_se_prev = None
            for ft in range(FT):
                ld0, te0, ve0, se0, st0 = (sm.val(x) for x in
                                           ("ld", "te", "ve", "se", "st"))
                cpjobs = [(side, tl, hh) for side in range(2) for tl in range(NT)
                          for hh in range(2)]

                @block.sync
                def _(sync, ft=ft, ve0=ve0):
                    if ft == 0:
                        sync.wait_ge(sm.h["ph"], ph_a2)
                    else:
                        sync.wait_ge(sm.h["ve"], ve0)
                    sm.inc(sync.dma_start(out=qftA[:, :],
                                          in_=qd[ft * 128:(ft + 1) * 128, :]), "ld", 16)
                    sm.inc(sync.dma_start(out=kftA[:, :],
                                          in_=kd[ft * 128:(ft + 1) * 128, :]), "ld", 16)
                    sm.inc(sync.dma_start(out=posq_sb[:, :],
                                          in_=posd[ft * 128:(ft + 1) * 128, :]), "ld", 16)
                    sm.inc(sync.dma_start(out=posk_sb[:, :],
                                          in_=posd[DM + ft * 128: DM + (ft + 1) * 128, :]),
                           "ld", 16)
                a3_ld = sm.val("ld")

                jps = [(side, tl) for side in range(2) for tl in range(NT)]

                def dve_idx(jp, dc, hh):
                    return 6 * jp + (1 + hh if dc == 0 else 3 + hh)

                def act_idx(jp, hh):
                    return 2 * jp + 1 + hh

                @block.tensor
                def _(t, ft=ft, te0=te0, ve0=ve0, se0=se0, a3_ld=a3_ld, jps=jps,
                      a3_se_prev=a3_se_prev,
                      tr_prev=(ea_tr_ve.get(ft - 1))):
                    t.wait_ge(sm.h["ld"], a3_ld)
                    if ft == 0:
                        t.wait_ge(sm.h["ph"], ph_a2)
                    else:
                        t.wait_ge(sm.h["se"], a3_se_prev)
                        t.wait_ge(sm.h["ve"], tr_prev)
                    for jp, (side, tl) in enumerate(jps):
                        lhs_sb = kftA if side == 0 else qftA
                        lcols = tl * 128
                        rhs_sb = posq_sb if side == 0 else posk_sb
                        for dc in range(3):
                            for hh in range(2):
                                m = jp * 6 + dc * 2 + hh
                                w = dchw[dc]
                                pp = psb[m % 4]
                                if m >= 4:
                                    jp_, r = divmod(m - 4, 6)
                                    dc_, hh_ = r // 2, r % 2
                                    if dc_ == 1:
                                        t.wait_ge(sm.h["se"],
                                                  se0 + act_idx(jp_, hh_))
                                    else:
                                        t.wait_ge(sm.h["ve"],
                                                  ve0 + dve_idx(jp_, dc_, hh_))
                                mmi = t.matmul(
                                    pp[:, 0:w],
                                    lhs_sb[64 * hh:64 * (hh + 1),
                                           lcols:lcols + 128],
                                    rhs_sb[64 * hh:64 * (hh + 1),
                                           dc * 512:dc * 512 + w],
                                    start=True, stop=True,
                                    tile_position=(64 * hh, 0))
                                sm.inc(mmi, "te", 1)

                @block.vector
                def _(v, ft=ft, te0=te0, ve0=ve0, se0=se0, st0=st0, jps=jps,
                      ea_st_prev=ea_st_prev, a3_se_prev=a3_se_prev):
                    if ea_st_prev is not None:
                        v.wait_ge(sm.h["st"], ea_st_prev)
                        v.wait_ge(sm.h["se"], a3_se_prev)
                    for jp, (side, tl) in enumerate(jps):
                        for dc in (0, 2):
                            for hh in range(2):
                                j = 2 * jp + hh
                                stbase = (j % 4) * WP
                                if dc == 0 and j >= 4:
                                    v.wait_ge(sm.h["st"], st0 + 16 * (j - 3))
                                m = jp * 6 + dc * 2 + hh
                                w = dchw[dc]
                                pp = psb[m % 4]
                                v.wait_ge(sm.h["te"], te0 + m + 1)
                                sm.inc(v.tensor_copy(
                                    cpst[:, stbase + dc * 512:
                                         stbase + dc * 512 + w],
                                    pp[:, 0:w]), "ve", 1)
                        v.drain()
                        for hh in range(2):
                            j = 2 * jp + hh
                            stbase = (j % 4) * WP
                            base = hh * 2 * NT
                            if side == 0:
                                v.tensor_copy(
                                    bcacc[:, base + tl: base + tl + 1],
                                    cpst[:, stbase + COL_FP: stbase + COL_FP + 1])
                                sm.inc(v.tensor_copy(
                                    bcacc[:, base + NT + tl: base + NT + tl + 1],
                                    cpst[:, stbase + COL_FN: stbase + COL_FN + 1]),
                                    "ve", 1)
                            else:
                                v.tensor_copy(
                                    eacc[:, base + tl: base + tl + 1],
                                    cpst[:, stbase + RCOL_FP: stbase + RCOL_FP + 1])
                                sm.inc(v.tensor_copy(
                                    eacc[:, base + NT + tl: base + NT + tl + 1],
                                    cpst[:, stbase + RCOL_FN: stbase + RCOL_FN + 1]),
                                    "ve", 1)

                @block.scalar
                def _(s, ft=ft, te0=te0, st0=st0, jps=jps):
                    for jp, (side, tl) in enumerate(jps):
                        for hh in range(2):
                            j = 2 * jp + hh
                            stbase = (j % 4) * WP
                            if j >= 4:
                                s.wait_ge(sm.h["st"], st0 + 16 * (j - 3))
                            m = jp * 6 + 2 + hh
                            pp = psb[m % 4]
                            s.wait_ge(sm.h["te"], te0 + m + 1)
                            sm.inc(s.copy(cpst[:, stbase + 512: stbase + 1024],
                                          pp[:, 0:512]), "se", 1)

                @block.scalar
                def _(s, ft=ft, ve0=ve0, ea_st_prev=ea_st_prev, jps=jps):
                    if ea_st_prev is not None:
                        s.wait_ge(sm.h["st"], ea_st_prev)
                    s.wait_ge(sm.h["ve"], ve0 + 6 * len(jps))
                    s.activation(bcacc[:, :], bcacc[:, :], AF.Exp,
                                 scale=INV_SCALE)
                    s.activation(eacc[:, :], eacc[:, :], AF.Exp,
                                 scale=INV_SCALE)
                    sm.inc(s.copy(ecb[:, :], eacc[:, :]), "se", 1)
                a3_se = sm.val("se")

                @block.tensor
                def _(t, a3_se=a3_se):
                    t.wait_ge(sm.h["se"], a3_se)
                    sm.inc(t.matmul(psb[0][:, 0:128], ecb[:, :], ident_sb[:, :],
                                    start=True, stop=True), "te", 1)
                tr_te = sm.val("te")

                @block.vector
                def _(v, tr_te=tr_te):
                    v.wait_ge(sm.h["te"], tr_te)
                    sm.inc(v.tensor_copy(etr[:, :], psb[0][:, 0:128]), "ve", 1)
                ea_tr_ve[ft] = sm.val("ve")

                @block.sync
                def _(sync, ft=ft, ve0=ve0, se0=se0, a3_se=a3_se, jps=jps):
                    for jp, (side, tl) in enumerate(jps):
                        for hh in range(2):
                            j = 2 * jp + hh
                            h = 2 * ft + hh
                            stbase = (j % 4) * WP
                            sync.wait_ge(sm.h["ve"], ve0 + 6 * jp + 3 + hh)
                            sync.wait_ge(sm.h["se"], se0 + act_idx(jp, hh))
                            dstd = pgd if side == 0 else cgd
                            sm.inc(sync.dma_start(
                                out=AP(dstd.tensor, (h * S + tl * 128) * WP,
                                       [[WP, 128], [1, WB]]),
                                in_=cpst[:, stbase:stbase + WB]), "st", 16)
                    sync.wait_ge(sm.h["se"], a3_se)
                    sync.wait_ge(sm.h["ve"], ea_tr_ve[ft])
                    sm.inc(sync.dma_start(
                        out=AP(ead.tensor, 4 * ft * S,
                               [[2 * S, 2], [S, 2], [1, NT * 128]]),
                        in_=etr[:, :]), "st", 16)
                    for hh in range(2):
                        h = 2 * ft + hh
                        base = hh * 2 * NT
                        for side in range(2):
                            sm.inc(sync.dma_start(
                                out=AP(biasd.tensor, (h * 2 + side) * S,
                                       [[NT, 128], [1, NT]]),
                                in_=bcacc[:, base + side * NT: base + (side + 1) * NT]),
                                "st", 16)
                ea_st_prev = sm.val("st")
                a3_se_prev = a3_se

            @block.vector
            def _(v):
                v.wait_ge(sm.h["st"], sm.val("st"))
                sm.inc(v.drain(), "ph", 1)
        ph_a3 = sm.val("ph")

        with (
            nc.sbuf_tensor("qhB", [128, 2 * S], BF16) as qhB,
            nc.sbuf_tensor("khB", [128, 2 * S], BF16) as khB,
            nc.sbuf_tensor("vh_sb", [128, 2 * NT * 65], BF16) as vh_sb,
            nc.sbuf_tensor("earow", [1, 4 * S], F32) as earow,
            nc.sbuf_tensor("band_sb", [128, NBAND * 1152], BF16) as band_sb,
            nc.sbuf_tensor("expS", [128, 2 * 1024], BF16) as expS,
            nc.sbuf_tensor("vhp_sb", [128, 2 * NT * 65], BF16) as vhp_sb,
            nc.sbuf_tensor("vhn_sb", [128, 2 * NT * 65], BF16) as vhn_sb,
            nc.sbuf_tensor("ebraw", [128, 4 * NT], F32) as ebraw,
            nc.sbuf_tensor("zsb", [128, 2 * 3 * 512], F32) as zsb,
            nc.sbuf_tensor("csum", [128, 2 * 512], F32) as csum,
            nc.sbuf_tensor("rrow", [1, 2 * 512], F32) as rrow,
            nc.sbuf_tensor("attnst", [128, 2 * 512], BF16) as attnst,
        ):
            slot_need = {i: 0 for i in range(NBAND)}
            bandc_at, bandp_at, issued = {}, {}, set()
            qk_at, ex_at, pv_after, bd_at = {}, {}, {}, {}
            cmb_at = {}
            comb_done = {}
            attnbuf_need = {0: 0, 1: 0}
            ps5_need = [0]
            qft_ld_at = {}
            vh_ld_at = {}
            vb_at = {}
            eb_ld_at = {}
            pending_combine = []

            def emit_combine_step(step):
                (ch, cqc, chb) = pending_combine[0]
                zb = (cqc % 2) * 1536
                cb = (cqc % 2) * 512
                if step == 0:
                    @block.tensor
                    def _(t, ch=ch, cqc=cqc, chb=chb, zb=zb):
                        t.wait_ge(sm.h["cmb"], cmb_at[(ch, cqc)])
                        t.wait_ge(sm.h["ve"], ps5_need[0])
                        sm.inc(t.matmul(psb[3][0:65, 0:512], onesrow_f[0:1, 0:65],
                                        earow[0:1, chb * 2 * S + cqc * 512:
                                              chb * 2 * S + (cqc + 1) * 512],
                                        start=True, stop=True), "te", 1)
                    comb_te[0] = sm.val("te")
                elif step == 1:
                    @block.vector
                    def _(v, zb=zb, t1=comb_te[0]):
                        v.wait_ge(sm.h["te"], t1)
                        sm.inc(v.tensor_tensor(zsb[0:65, zb + 512:zb + 1024],
                                               zsb[0:65, zb + 512:zb + 1024],
                                               psb[3][0:65, 0:512], ALU.mult),
                               "ve", 1)
                    comb_ve[0] = sm.val("ve")
                elif step == 2:
                    @block.tensor
                    def _(t, ch=ch, cqc=cqc, chb=chb, v1=comb_ve[0]):
                        t.wait_ge(sm.h["ve"], v1)
                        sm.inc(t.matmul(psb[3][0:65, 0:512], onesrow_f[0:1, 0:65],
                                        earow[0:1, chb * 2 * S + S + cqc * 512:
                                              chb * 2 * S + S + (cqc + 1) * 512],
                                        start=True, stop=True), "te", 1)
                    comb_te[0] = sm.val("te")
                elif step == 3:
                    @block.vector
                    def _(v, zb=zb, cb=cb, t2=comb_te[0]):
                        v.wait_ge(sm.h["te"], t2)
                        v.tensor_tensor(zsb[0:65, zb + 1024:zb + 1536],
                                        zsb[0:65, zb + 1024:zb + 1536],
                                        psb[3][0:65, 0:512], ALU.mult)
                        v.drain()
                        v.tensor_tensor(csum[0:65, cb:cb + 512],
                                        zsb[0:65, zb:zb + 512],
                                        zsb[0:65, zb + 512:zb + 1024], ALU.add)
                        v.drain()
                        v.tensor_tensor(csum[0:65, cb:cb + 512],
                                        csum[0:65, cb:cb + 512],
                                        zsb[0:65, zb + 1024:zb + 1536], ALU.add)
                        v.drain()
                        v.reciprocal(rrow[0:1, cb:cb + 512],
                                     csum[64:65, cb:cb + 512])
                        sm.inc(v.drain(), "ve", 1)
                    comb_ve[0] = sm.val("ve")
                elif step == 4:
                    @block.tensor
                    def _(t, cb=cb, s2=comb_ve[0]):
                        t.wait_ge(sm.h["ve"], s2)
                        sm.inc(t.matmul(psb[3][0:64, 0:512], onesrow_f[0:1, 0:64],
                                        rrow[0:1, cb:cb + 512],
                                        start=True, stop=True), "te", 1)
                    comb_te[0] = sm.val("te")
                else:
                    (chx, cqcx, chbx) = pending_combine.pop(0)

                    @block.vector
                    def _(v, cb=cb, t3=comb_te[0], bneed=attnbuf_need[cqc % 2]):
                        v.wait_ge(sm.h["te"], t3)
                        v.wait_ge(sm.h["st"], bneed)
                        v.tensor_tensor(csum[0:64, cb:cb + 512],
                                        csum[0:64, cb:cb + 512],
                                        psb[3][0:64, 0:512], ALU.mult)
                        v.drain()
                        sm.inc(v.tensor_copy(
                            attnst[0:64, (cqc % 2) * 512:(cqc % 2 + 1) * 512],
                            csum[0:64, cb:cb + 512]), "ve", 1)
                    ps5_need[0] = sm.val("ve")
                    comb_done[(chx, cqcx)] = sm.val("ve")
                    veA = sm.val("ve")

                    @block.sync
                    def _(sync, chx=chx, cqcx=cqcx, veA=veA):
                        sync.wait_ge(sm.h["ve"], veA)
                        sm.inc(sync.dma_start(
                            out=attnd[64 * chx:64 * (chx + 1),
                                      cqcx * 512:(cqcx + 1) * 512],
                            in_=attnst[0:64, (cqcx % 2) * 512:(cqcx % 2 + 1) * 512]),
                            "st", 16)
                    attnbuf_need[cqcx % 2] = sm.val("st")

            comb_te = [0]
            comb_ve = [0]
            comb_se = [0]

            def first_qc(kt):
                return max(0, (kt - 7 + 3) // 4)

            def kts_first_at(qc):
                return [kt for kt in range(NT) if first_qc(kt) == qc]

            for h in range(H):
                hh = h % 2
                ft = h // 2
                hb = h % 2
                if True:

                    @block.sync
                    def _(sync, h=h, hb=hb, ft=ft):
                        if h == 0:
                            sync.wait_ge(sm.h["ph"], ph_a3)
                        if h >= 2:
                            sync.wait_ge(sm.h["pv"], pv_after[(h - 2, NQC - 1, NT - 1)])
                            sync.wait_ge(sm.h["qk"], qk_at[(h - 2, NQC - 1, NT - 1)])
                        for par in range(2):
                            sm.inc(sync.dma_start(
                                out=qhB[64 * par:64 * par + 64, hb * S:(hb + 1) * S],
                                in_=qd[ft * 128 + 64 * hh: ft * 128 + 64 * hh + 64, :]),
                                "ld", 16)
                            sm.inc(sync.dma_start(
                                out=khB[64 * par:64 * par + 64, hb * S:(hb + 1) * S],
                                in_=kd[ft * 128 + 64 * hh: ft * 128 + 64 * hh + 64, :]),
                                "ld", 16)
                        sm.inc(sync.dma_start(
                            out=vh_sb[:, hb * NT * 65:(hb + 1) * NT * 65],
                            in_=AP(vhatd.tensor, h * 65,
                                   [[1040, 128], [1040 * 128, NT], [1, 65]])), "ld", 16)
                        sm.inc(sync.dma_start(
                            out=earow[0:1, hb * 2 * S:(hb + 1) * 2 * S],
                            in_=AP(ead.tensor, h * 2 * S, [[2 * S, 1], [1, 2 * S]])),
                               "ld", 16)
                    vh_ld_at[h] = sm.val("ld")

                    @block.gpsimd
                    def _(g, h=h, hb=hb):
                        if h >= 2:
                            g.wait_ge(sm.h["vb"], vb_at[h - 2])
                        else:
                            g.wait_ge(sm.h["ph"], ph_a3)
                        for side in range(2):
                            sm.inc(g.dma_start(
                                out=ebraw[:, hb * 2 * NT + side * NT:
                                          hb * 2 * NT + (side + 1) * NT],
                                in_=AP(biasd.tensor, (h * 2 + side) * S,
                                       [[NT, 128], [1, NT]])), "ebl", 16)
                    eb_ld_at[h] = sm.val("ebl")

                    @block.vector
                    def _(v, h=h, hb=hb):
                        v.wait_ge(sm.h["ld"], vh_ld_at[h])
                        v.wait_ge(sm.h["ebl"], eb_ld_at[h])
                        op = None
                        for side in range(2):
                            dst = vhp_sb if side == 0 else vhn_sb
                            for kt in range(NT):
                                op = v.tensor_scalar(
                                    dst[:, hb * NT * 65 + kt * 65:
                                        hb * NT * 65 + (kt + 1) * 65],
                                    vh_sb[:, hb * NT * 65 + kt * 65:
                                          hb * NT * 65 + (kt + 1) * 65],
                                    ebraw[:, hb * 2 * NT + side * NT + kt:
                                          hb * 2 * NT + side * NT + kt + 1],
                                    None, ALU.mult)
                        sm.inc(op, "vb", 1)
                    vb_at[h] = sm.val("vb")

                    for qc in range(NQC):
                        want = []
                        if qc == 0:
                            want += kts_first_at(0) + kts_first_at(1)
                        elif qc < NQC - 1:
                            want += kts_first_at(qc + 1)
                        for kt in [k for k in want if (h, k) not in issued]:
                            issued.add((h, kt))
                            slot = kt % NBAND
                            qlo, qhi = _band_window(kt)
                            Fw = qhi - qlo
                            k0 = 128 * kt
                            need = slot_need[slot]

                            @block.sync
                            def _(sync, h=h, kt=kt, slot=slot, qlo=qlo, Fw=Fw,
                                  k0=k0, need=need):
                                sync.wait_ge(sm.h["qk"], need)
                                src = AP(cgd.tensor,
                                         (h * S + qlo) * WP + (DW + k0 - qlo),
                                         [[WP - 1, Fw], [1, 128]])
                                sm.inc(sync.dma_start(
                                    out=band_sb[:, slot * 1152: slot * 1152 + Fw],
                                    in_=src, transpose=True), "bandc", 16)
                            bandc_at[(h, kt)] = sm.val("bandc")

                            @block.gpsimd
                            def _(g, h=h, kt=kt, slot=slot, qlo=qlo, Fw=Fw, k0=k0):
                                g.wait_ge(sm.h["bandc"], bandc_at[(h, kt)])
                                src = AP(pgd.tensor,
                                         (h * S + k0) * WP + (qlo - k0 + DW),
                                         [[WP - 1, 128], [1, Fw]])
                                sm.inc(g.dma_start(
                                    out=band_sb[:, slot * 1152: slot * 1152 + Fw],
                                    in_=src, accum_op=ALU.add), "bandp", 16)
                            bandp_at[(h, kt)] = sm.val("bandp")

                        if qc == 0:
                            @block.tensor
                            def _(t, h=h):
                                t.wait_ge(sm.h["ld"], vh_ld_at[h])
                                t.wait_ge(sm.h["vb"], vb_at[h])

                        def emit_pv(t, kt2, h=h, qc=qc, hb=hb):
                            eb = ((kt2 // 2) % 2) * 1024 + (kt2 % 2) * 512
                            for z, c0, c1 in _zone_ranges(kt2, qc):
                                zi = {"near": 0, "farP": 1, "farN": 2}[z]
                                vsrc = (vh_sb, vhp_sb, vhn_sb)[zi]
                                sm.inc(t.matmul(
                                    psb[zi][0:65, c0:c1],
                                    vsrc[:, hb * NT * 65 + kt2 * 65:
                                         hb * NT * 65 + (kt2 + 1) * 65],
                                    expS[:, eb + c0: eb + c1],
                                    start=False, stop=True), "pv", 1)
                            pv_after[(h, qc, kt2)] = sm.val("pv")

                        def emit_qk_mm(t, kt, h=h, qc=qc, hb=hb):
                            sbank = psd[(kt // 2) % 2]
                            half = (kt % 2) * 512
                            par = kt % 2
                            mmi = t.matmul(
                                sbank[:, half:half + 512],
                                khB[64 * par:64 * par + 64,
                                    hb * S + kt * 128: hb * S + (kt + 1) * 128],
                                qhB[64 * par:64 * par + 64,
                                    hb * S + qc * 512: hb * S + (qc + 1) * 512],
                                start=True, stop=True,
                                tile_position=(64 * par, 0))
                            if not any(z == "near"
                                       for z, _, _ in _zone_ranges(kt, qc)):
                                sm.inc(mmi, "qk", 1)
                                qk_at[(h, qc, kt)] = sm.val("qk")

                        def emit_qk_post(t, kt, h=h, qc=qc, hb=hb):
                            sbank = psd[(kt // 2) % 2]
                            half = (kt % 2) * 512
                            qlo, qhi = _band_window(kt)
                            near = [(c0, c1)
                                    for z, c0, c1 in _zone_ranges(kt, qc)
                                    if z == "near"]
                            if not near:
                                return
                            slot = kt % NBAND
                            boff = slot * 1152 + (qc * 512 - qlo)
                            t.wait_ge(sm.h["bandp"], bandp_at[(h, kt)])
                            mmi = None
                            for c0, c1 in near:
                                mmi = t.matmul(
                                    sbank[:, half + c0: half + c1], ident_sb[:, :],
                                    band_sb[:, boff + c0: boff + c1],
                                    start=False, stop=True)
                            sm.inc(mmi, "qk", 1)
                            qk_at[(h, qc, kt)] = sm.val("qk")
                            slot_need[kt % NBAND] = sm.val("qk")

                        NP = NT // 2
                        ex_base = sm.val("ex")
                        for p in range(NP):
                            ex_at[(h, qc, 2 * p)] = ex_base + p + 1
                            ex_at[(h, qc, 2 * p + 1)] = ex_base + p + 1
                        comb_sched = {1: [0], 2: [1, 2], 3: [3], 8: [4], 10: [5]}
                        for p in range(NP):
                            if pending_combine:
                                for st_ in comb_sched.get(p, []):
                                    emit_combine_step(st_)
                            if p == 1:
                                @block.tensor
                                def _(t, h=h, qc=qc):
                                    if (h, qc) != (0, 0):
                                        pq = ((h, qc - 1) if qc > 0
                                              else (h - 1, NQC - 1))
                                        t.wait_ge(sm.h["cmb"], cmb_at[pq])
                                    for z in range(3):
                                        sm.inc(t.matmul(psb[z][0:65, 0:512],
                                                        zcol_b[0:1, :], zrow_b[0:1, :],
                                                        start=True, stop=True),
                                               "te", 1)

                            @block.tensor
                            def _(t, h=h, qc=qc, p=p):
                                prev = None
                                if (h, qc, 2 * (p - 2) + 1) in ex_at:
                                    prev = ex_at[(h, qc, 2 * (p - 2) + 1)]
                                elif qc > 0:
                                    prev = ex_at[(h, qc - 1, NT + 2 * (p - 2) + 1)]
                                elif h > 0:
                                    prev = ex_at[(h - 1, NQC - 1, NT + 2 * (p - 2) + 1)]
                                if prev is not None:
                                    t.wait_ge(sm.h["ex"], prev)
                                emit_qk_mm(t, 2 * p)
                                emit_qk_mm(t, 2 * p + 1)
                                emit_qk_post(t, 2 * p)
                                emit_qk_post(t, 2 * p + 1)
                                if p >= 2:
                                    emit_pv(t, 2 * (p - 2))
                                    emit_pv(t, 2 * (p - 2) + 1)

                        @block.tensor
                        def _(t, h=h, qc=qc):
                            t.wait_ge(sm.h["ex"], ex_at[(h, qc, NT - 1)])
                            for kt2 in (NT - 4, NT - 3, NT - 2, NT - 1):
                                emit_pv(t, kt2)

                        @block.scalar
                        def _(s, h=h, qc=qc):
                            for p in range(NP):
                                prev = None
                                if p >= 2:
                                    prev = pv_after[(h, qc, 2 * (p - 2) + 1)]
                                elif qc > 0:
                                    prev = pv_after[(h, qc - 1,
                                                     NT + 2 * (p - 2) + 1)]
                                elif h > 0:
                                    prev = pv_after[(h - 1, NQC - 1,
                                                     NT + 2 * (p - 2) + 1)]
                                if prev is not None:
                                    s.wait_ge(sm.h["pv"], prev)
                                s.wait_ge(sm.h["qk"],
                                          max(qk_at[(h, qc, 2 * p)],
                                              qk_at[(h, qc, 2 * p + 1)]))
                                eb = (p % 2) * 1024
                                act = s.activation(expS[:, eb: eb + 1024],
                                                   psd[p % 2][:, :], AF.Exp,
                                                   scale=INV_SCALE)
                                sm.inc(act, "ex", 1)
                            s.wait_ge(sm.h["pv"], pv_after[(h, qc, NT - 1)])
                            if (h, qc) != (0, 0):
                                pq = (h, qc - 1) if qc > 0 else (h - 1, NQC - 1)
                                s.wait_ge(sm.h["ve"], comb_done[pq])
                            zb = (qc % 2) * 1536
                            s.copy(zsb[0:65, zb:zb + 512], psb[0][0:65, 0:512])
                            s.copy(zsb[0:65, zb + 512:zb + 1024], psb[1][0:65, 0:512])
                            sm.inc(s.copy(zsb[0:65, zb + 1024:zb + 1536],
                                          psb[2][0:65, 0:512]), "cmb", 1)
                        assert sm.val("ex") == ex_base + NP
                        cmb_at[(h, qc)] = sm.val("cmb")

                        pending_combine.append((h, qc, hb))


            while pending_combine:
                for st_ in range(6):
                    emit_combine_step(st_)

            @block.vector
            def _(v):
                v.wait_ge(sm.h["st"], sm.val("st"))
                sm.inc(v.drain(), "ph", 1)
        ph_b = sm.val("ph")

        def emit_ln(src_sb, sq_sb, strow, track_max, out_sb, qc, st_wait):
            ve_in = sm.val("ve")

            @block.scalar
            def _(s, ve_in=ve_in):
                s.wait_ge(sm.h["ve"], ve_in)
                sm.inc(s.activation(sq_sb[:, :], src_sb[:, :], AF.Square), "se", 1)
            seq = sm.val("se")

            @block.tensor
            def _(t, ve_in=ve_in, seq=seq):
                t.wait_ge(sm.h["ve"], ve_in)
                for ftt in range(FT):
                    sm.inc(t.matmul(psb[0][0:1, 0:512], onescol_f[:, 0:1],
                                    src_sb[:, ftt * 512:(ftt + 1) * 512],
                                    start=(ftt == 0), stop=(ftt == FT - 1)), "te", 1)
                t.wait_ge(sm.h["se"], seq)
                for ftt in range(FT):
                    sm.inc(t.matmul(psb[1][0:1, 0:512], onescol_f[:, 0:1],
                                    sq_sb[:, ftt * 512:(ftt + 1) * 512],
                                    start=(ftt == 0), stop=(ftt == FT - 1)), "te", 1)
            tec = sm.val("te")

            @block.vector
            def _(v, tec=tec, strow=strow):
                v.wait_ge(sm.h["te"], tec)
                v.tensor_scalar(strow[0:1, 0:512], psb[0][0:1, 0:512],
                                1.0 / DM, None, ALU.mult)
                v.tensor_scalar(strow[0:1, 512:1024], psb[1][0:1, 0:512],
                                1.0 / DM, None, ALU.mult)
                v.drain()
                v.tensor_tensor(strow[0:1, 1024:1536], strow[0:1, 0:512],
                                strow[0:1, 0:512], ALU.mult)
                v.drain()
                v.tensor_tensor(strow[0:1, 1024:1536], strow[0:1, 512:1024],
                                strow[0:1, 1024:1536], ALU.subtract)
                v.drain()
                sm.inc(v.tensor_scalar(strow[0:1, 1024:1536], strow[0:1, 1024:1536],
                                       EPS, None, ALU.add), "ve", 1)
            vev = sm.val("ve")

            @block.scalar
            def _(s, vev=vev, strow=strow):
                s.wait_ge(sm.h["ve"], vev)
                sm.inc(s.activation(strow[0:1, 1536:2048], strow[0:1, 1024:1536],
                                    AF.Sqrt), "se", 1)
            ses = sm.val("se")

            @block.vector
            def _(v, ses=ses, strow=strow):
                v.wait_ge(sm.h["se"], ses)
                v.reciprocal(strow[0:1, 1536:2048], strow[0:1, 1536:2048])
                v.drain()
                v.tensor_tensor(strow[0:1, 2048:2560], strow[0:1, 1536:2048],
                                strow[0:1, 1536:2048], ALU.mult)
                v.drain()
                v.tensor_tensor(strow[0:1, 2048:2560], strow[0:1, 1024:1536],
                                strow[0:1, 2048:2560], ALU.mult)
                v.drain()
                v.tensor_scalar(strow[0:1, 2048:2560], strow[0:1, 2048:2560],
                                -0.5, 1.5, ALU.mult, ALU.add)
                v.drain()
                v.tensor_tensor(strow[0:1, 1536:2048], strow[0:1, 1536:2048],
                                strow[0:1, 2048:2560], ALU.mult)
                v.drain()
                sm.inc(v.tensor_tensor(strow[0:1, 2560:3072], strow[0:1, 0:512],
                                       strow[0:1, 1536:2048], ALU.mult), "ve", 1)
            ver = sm.val("ve")

            @block.tensor
            def _(t, ver=ver, strow=strow):
                t.wait_ge(sm.h["ve"], ver)
                sm.inc(t.matmul(psb[2][:, 0:512], onesrow_f[0:1, 0:128],
                                strow[0:1, 1536:2048], start=True, stop=True), "te", 1)
                sm.inc(t.matmul(psb[3][:, 0:512], onesrow_f[0:1, 0:128],
                                strow[0:1, 2560:3072], start=True, stop=True), "te", 1)
            teb = sm.val("te")

            @block.vector
            def _(v, teb=teb, track_max=track_max, st_wait=st_wait):
                v.wait_ge(sm.h["te"], teb)
                if st_wait:
                    v.wait_ge(sm.h["st"], st_wait)
                for ftt in range(FT):
                    sl = slice(ftt * 512, (ftt + 1) * 512)
                    v.tensor_tensor(sq_sb[:, sl], src_sb[:, sl],
                                    psb[2][:, 0:512], ALU.mult)
                    v.drain()
                    v.tensor_tensor(sq_sb[:, sl], sq_sb[:, sl],
                                    psb[3][:, 0:512], ALU.subtract)
                    v.drain()
                    if track_max is not None:
                        v.reduce_max(maxc[:, track_max + 1:track_max + 2] if False
                                     else qsc[:, 3:4],
                                     sq_sb[:, sl], axis=AX.X,
                                     apply_absolute_value=True)
                        v.drain()
                        v.tensor_tensor(maxc[:, track_max:track_max + 1],
                                        maxc[:, track_max:track_max + 1],
                                        qsc[:, 3:4], ALU.max)
                        v.drain()
                    sm.inc(v.tensor_copy(out_sb[:, sl], sq_sb[:, sl]), "ve", 1)
            return sm.val("ve")

        with (
            nc.sbuf_tensor("xc", [128, FT * 512], F32) as xc,
            nc.sbuf_tensor("sqc", [128, FT * 512], F32) as sqc,
            nc.sbuf_tensor("ac", [128, FT * 512], BF16) as ac,
            nc.sbuf_tensor("y1st", [128, FT * 512], BF16) as y1st,
            nc.sbuf_tensor("strow", [1, 3072], F32) as strow,
        ):
            @block.vector
            def _(v):
                v.wait_ge(sm.h["ph"], ph_b)
                v.memset(maxc[:, 0:1], 0.0)
                sm.inc(v.memset(maxc[:, 1:2], 0.0), "ve", 1)
            c1_ve0 = sm.val("ve")

            y1_store_at = 0
            for qc in range(NQC):
                ld0 = sm.val("ld")

                @block.sync
                def _(sync, qc=qc, prev_ve=sm.val("ve")):
                    sync.wait_ge(sm.h["ph"], ph_b)
                    sync.wait_ge(sm.h["ve"], prev_ve)
                    sm.inc(sync.dma_start(
                        out=AP(xc, 0, [[FT * 512, 128], [512, FT], [1, 512]]),
                        in_=AP(xfm.tensor, qc * 512,
                               [[S, 128], [S * 128, FT], [1, 512]])), "ld", 16)
                    sm.inc(sync.dma_start(
                        out=AP(ac, 0, [[FT * 512, 128], [512, FT], [1, 512]]),
                        in_=AP(attnd.tensor, qc * 512,
                               [[S, 128], [S * 128, FT], [1, 512]])), "ld", 16)
                c1_ld = sm.val("ld")

                @block.vector
                def _(v, c1_ld=c1_ld):
                    v.wait_ge(sm.h["ld"], c1_ld)
                    v.wait_ge(sm.h["ve"], c1_ve0)
                    sm.inc(v.tensor_tensor(xc[:, :], xc[:, :], ac[:, :], ALU.add),
                           "ve", 1)

                y1_done = emit_ln(xc, sqc, strow, 0, y1st, qc, y1_store_at)

                @block.sync
                def _(sync, qc=qc, y1_done=y1_done):
                    sync.wait_ge(sm.h["ve"], y1_done)
                    sm.inc(sync.dma_start(
                        out=AP(y1d.tensor, qc * 512,
                               [[S, 128], [S * 128, FT], [1, 512]]),
                        in_=AP(y1st, 0,
                               [[FT * 512, 128], [512, FT], [1, 512]])), "st", 16)
                y1_store_at = sm.val("st")

            c1_ve_end = sm.val("ve")

            @block.gpsimd
            def _(g, c1_ve_end=c1_ve_end, y1_store_at=y1_store_at):
                g.wait_ge(sm.h["ve"], c1_ve_end)
                sm.inc(g.dma_start(out=AP(cc1i.tensor, 0, [[1, 128], [1, 1]]),
                                   in_=maxc[:, 0:1]), "cc", 16)
                g.wait_ge(sm.h["cc"], sm.val("cc"))
                sm.inc(g.collective_compute(
                    "AllReduce", ALU.max, replica_groups=[list(range(8))],
                    ins=[cc1i.opt()], outs=[cc1o.opt()]), "cc", 1)
                g.wait_ge(sm.h["cc"], sm.val("cc"))
                sm.inc(g.dma_start(out=rowred[0:1, :],
                                   in_=AP(cc1o.tensor, 0, [[128, 1], [1, 128]])),
                       "cc", 16)
            cc1 = sm.val("cc")

            @block.sync
            def _(sync):
                sm.inc(sync.dma_start(out=sca[0:1, 1:2],
                                      in_=AP(clipx.tensor, 0, [[1, 1], [1, 1]])),
                       "ld", 16)
                sm.inc(sync.dma_start(out=sca[0:1, 6:7],
                                      in_=AP(clipd.tensor, 0, [[1, 1], [1, 1]])),
                       "ld", 16)
            clip_ld = sm.val("ld")

            @block.vector
            def _(v, cc1=cc1, clip_ld=clip_ld):
                v.wait_ge(sm.h["cc"], cc1)
                sm.inc(v.reduce_max(sca[0:1, 0:1], rowred[0:1, :], axis=AX.X),
                       "ve", 1)
            gm_ve = sm.val("ve")

            @block.scalar
            def _(s, gm_ve=gm_ve, clip_ld=clip_ld):
                s.wait_ge(sm.h["ve"], gm_ve)
                s.wait_ge(sm.h["ld"], clip_ld)
                sm.inc(s.activation(sca[0:1, 2:3], sca[0:1, 1:2], AF.Sigmoid),
                       "se", 1)
            sig_se = sm.val("se")

            @block.vector
            def _(v, sig_se=sig_se):
                v.wait_ge(sm.h["se"], sig_se)
                v.tensor_tensor(sca[0:1, 3:4], sca[0:1, 2:3], sca[0:1, 0:1], ALU.mult)
                v.drain()
                v.tensor_scalar(sca[0:1, 3:4], sca[0:1, 3:4], 1.0 / QMAX, None,
                                ALU.mult)
                v.drain()
                v.reciprocal(sca[0:1, 4:5], sca[0:1, 3:4])
                sm.inc(v.drain(), "ve", 1)
            sx_ve = sm.val("ve")

            @block.gpsimd
            def _(g, sx_ve=sx_ve):
                g.wait_ge(sm.h["ve"], sx_ve)
                sm.inc(g.dma_start(out=bounced[128:130], in_=sca[0:1, 3:5]), "cc", 16)
                g.wait_ge(sm.h["cc"], sm.val("cc"))
                sm.inc(g.dma_start(out=qsc[:, 1:2],
                                   in_=AP(bounced.tensor, 128, [[0, 128], [1, 1]])),
                       "cc", 16)
                sm.inc(g.dma_start(out=qsc[:, 0:1],
                                   in_=AP(bounced.tensor, 129, [[0, 128], [1, 1]])),
                       "cc", 16)
            sx_cc = sm.val("cc")

            @block.vector
            def _(v, sx_cc=sx_cc):
                v.wait_ge(sm.h["cc"], sx_cc)
                v.tensor_scalar(deqg[:, :], sg_sb[:, :], qsc[:, 1:2], None, ALU.mult)
                v.drain()
                sm.inc(v.tensor_scalar(dequ[:, :], su_sb[:, :], qsc[:, 1:2], None,
                                       ALU.mult), "ve", 1)
            deq_ve = sm.val("ve")

            if DBG:
                @block.sync
                def _(sync, deq_ve=deq_ve):
                    sync.wait_ge(sm.h["ve"], deq_ve)
                    sm.inc(sync.dma_start(out=scdbg[0:8], in_=sca[0:1, 0:8]), "st", 16)
                    sm.inc(sync.dma_start(out=AP(scdbg.tensor, 8, [[1, 128], [1, 1]]),
                                          in_=qsc[:, 0:1]), "st", 16)
                    sm.inc(sync.dma_start(out=AP(scdbg.tensor, 136, [[1, 128], [1, 1]]),
                                          in_=qsc[:, 1:2]), "st", 16)

        with (
            nc.sbuf_tensor("wgT_sb", [128, FT * DI], BF16) as wgT_sb,
            nc.sbuf_tensor("wuT_sb", [128, FT * DI], BF16) as wuT_sb,
            nc.sbuf_tensor("y1c", [128, FT * 512], BF16) as y1c,
            nc.sbuf_tensor("xtf", [128, FT * 512], F32) as xtf,
            nc.sbuf_tensor("xti", [128, FT * 512], I32) as xti,
            nc.sbuf_tensor("xqb", [128, FT * 512], BF16) as xqb,
            nc.sbuf_tensor("gsb", [128, 2 * 512], F32) as gsb,
            nc.sbuf_tensor("usb", [128, 2 * 512], F32) as usb,
            nc.sbuf_tensor("hfb", [128, 2 * 512], F32) as hfb,
            nc.sbuf_tensor("hst", [128, 2 * 512], BF16) as hst,
        ):
            @block.sync
            def _(sync, deq_ve=deq_ve):
                sync.wait_ge(sm.h["ve"], deq_ve)
                for j in range(FT):
                    src = AP(wgqd.tensor, j * 128, [[DM, DI], [1, 128]])
                    sm.inc(sync.dma_start(out=wgT_sb[:, j * DI:(j + 1) * DI],
                                          in_=src, transpose=True), "ld", 16)
                for j in range(FT):
                    src = AP(wuqd.tensor, j * 128, [[DM, DI], [1, 128]])
                    sm.inc(sync.dma_start(out=wuT_sb[:, j * DI:(j + 1) * DI],
                                          in_=src, transpose=True), "ld", 16)
            wgt_ld = sm.val("ld")

            h_store_at = 0
            hst_at = {}
            for qc in range(NQC):
                @block.sync
                def _(sync, qc=qc, prev=sm.val("ve"), hs=h_store_at):
                    sync.wait_ge(sm.h["ve"], prev)
                    sync.wait_ge(sm.h["st"], hs)
                    sm.inc(sync.dma_start(
                        out=AP(y1c, 0, [[FT * 512, 128], [512, FT], [1, 512]]),
                        in_=AP(y1d.tensor, qc * 512,
                               [[S, 128], [S * 128, FT], [1, 512]])), "ld", 16)
                c2_ld = sm.val("ld")

                @block.vector
                def _(v, c2_ld=c2_ld, wgt_ld=wgt_ld):
                    v.wait_ge(sm.h["ld"], max(c2_ld, wgt_ld))
                    v.tensor_scalar(xtf[:, :], y1c[:, :], qsc[:, 0:1], None, ALU.mult)
                    v.drain()
                    v.tensor_copy(xti[:, :], xtf[:, :])
                    v.drain()
                    sm.inc(v.tensor_scalar(xqb[:, :], xti[:, :], QMAX, -QMAX - 1.0,
                                           ALU.min, ALU.max), "ve", 1)
                xq_ve = sm.val("ve")

                if DBG:
                    @block.sync
                    def _(sync, qc=qc, xq_ve=xq_ve):
                        sync.wait_ge(sm.h["ve"], xq_ve)
                        sm.inc(sync.dma_start(
                            out=AP(xqdbg.tensor, qc * 512,
                                   [[S, 128], [S * 128, FT], [1, 512]]),
                            in_=AP(xqb, 0, [[FT * 512, 128], [512, FT], [1, 512]])),
                            "st", 16)

                se_g = {}
                ve_u = {}
                gd_ve_all = {}
                se_base = sm.val("se")
                for it in range(IT):
                    se_g[it] = se_base + it + 1
                for it in range(IT):
                    @block.tensor
                    def _(t, it=it, qc=qc, xq_ve=xq_ve):
                        t.wait_ge(sm.h["ve"], xq_ve)
                        if it >= 2:
                            t.wait_ge(sm.h["se"], se_g[it - 2])
                            t.wait_ge(sm.h["ve"], ve_u[it - 2])
                        for ht in range(FT):
                            mg = t.matmul(psb[it % 2][:, :],
                                          wgT_sb[:, ht * DI + it * 128:
                                                 ht * DI + (it + 1) * 128],
                                          xqb[:, ht * 512:(ht + 1) * 512],
                                          start=(ht == 0), stop=(ht == FT - 1))
                        sm.inc(mg, "te", 1)
                        for ht in range(FT):
                            mu = t.matmul(psb[2 + it % 2][:, :],
                                          wuT_sb[:, ht * DI + it * 128:
                                                 ht * DI + (it + 1) * 128],
                                          xqb[:, ht * 512:(ht + 1) * 512],
                                          start=(ht == 0), stop=(ht == FT - 1))
                        sm.inc(mu, "te", 1)
                    gu_te = sm.val("te")

                    @block.vector
                    def _(v, it=it, qc=qc, gu_te=gu_te):
                        v.wait_ge(sm.h["te"], gu_te - 1)
                        b = it % 2
                        if DBG:
                            v.tensor_copy(hfb[:, b * 512:(b + 1) * 512], psb[b][:, :])
                            v.drain()
                        sm.inc(v.tensor_scalar(gsb[:, b * 512:(b + 1) * 512],
                                               psb[b][:, :], deqg[:, it:it + 1],
                                               None, ALU.mult), "bd", 1)
                    gd_ve = sm.val("bd")

                    if DBG:
                        @block.sync
                        def _(sync, it=it, qc=qc, gd_ve=gd_ve):
                            sync.wait_ge(sm.h["bd"], gd_ve)
                            sm.inc(sync.dma_start(
                                out=gdbg[it * 128:(it + 1) * 128,
                                         qc * 512:(qc + 1) * 512],
                                in_=hfb[:, (it % 2) * 512:(it % 2 + 1) * 512]),
                                "st", 16)

                    gd_ve_all[it] = gd_ve

                    @block.vector
                    def _(v, it=it, qc=qc, gu_te=gu_te):
                        v.wait_ge(sm.h["te"], gu_te)
                        v.wait_ge(sm.h["se"], se_g[it])
                        pit = it - 2 if it >= 2 else IT - 2 + it
                        if pit in hst_at:
                            v.wait_ge(sm.h["st"], hst_at[pit])
                        b = it % 2
                        v.tensor_scalar(usb[:, b * 512:(b + 1) * 512],
                                        psb[2 + b][:, :], dequ[:, it:it + 1],
                                        None, ALU.mult)
                        v.drain()
                        v.tensor_tensor(hfb[:, b * 512:(b + 1) * 512],
                                        gsb[:, b * 512:(b + 1) * 512],
                                        usb[:, b * 512:(b + 1) * 512], ALU.mult)
                        v.drain()
                        v.reduce_max(qsc[:, 3:4], hfb[:, b * 512:(b + 1) * 512],
                                     axis=AX.X, apply_absolute_value=True)
                        v.drain()
                        v.tensor_tensor(maxc[:, 1:2], maxc[:, 1:2], qsc[:, 3:4],
                                        ALU.max)
                        v.drain()
                        sm.inc(v.tensor_copy(hst[:, b * 512:(b + 1) * 512],
                                             hfb[:, b * 512:(b + 1) * 512]), "ve", 1)
                    ve_u[it] = sm.val("ve")

                    @block.sync
                    def _(sync, it=it, qc=qc):
                        sync.wait_ge(sm.h["ve"], ve_u[it])
                        sm.inc(sync.dma_start(
                            out=hd[it * 128:(it + 1) * 128, qc * 512:(qc + 1) * 512],
                            in_=hst[:, (it % 2) * 512:(it % 2 + 1) * 512]), "st", 16)
                    hst_at[it] = sm.val("st")

                @block.scalar
                def _(s, qc=qc, gd_ve_all=dict(gd_ve_all)):
                    for it in range(IT):
                        s.wait_ge(sm.h["bd"], gd_ve_all[it])
                        b = it % 2
                        sm.inc(s.activation(gsb[:, b * 512:(b + 1) * 512],
                                            gsb[:, b * 512:(b + 1) * 512],
                                            AF.Silu), "se", 1)
                assert sm.val("se") == se_base + IT
                h_store_at = sm.val("st")

            c2_ve_end = sm.val("ve")

            @block.gpsimd
            def _(g, c2_ve_end=c2_ve_end):
                g.wait_ge(sm.h["ve"], c2_ve_end)
                sm.inc(g.dma_start(out=AP(cc2i.tensor, 0, [[1, 128], [1, 1]]),
                                   in_=maxc[:, 1:2]), "cc", 16)
                g.wait_ge(sm.h["cc"], sm.val("cc"))
                sm.inc(g.collective_compute(
                    "AllReduce", ALU.max, replica_groups=[list(range(8))],
                    ins=[cc2i.opt()], outs=[cc2o.opt()]), "cc", 1)
                g.wait_ge(sm.h["cc"], sm.val("cc"))
                sm.inc(g.dma_start(out=rowred[0:1, :],
                                   in_=AP(cc2o.tensor, 0, [[128, 1], [1, 128]])),
                       "cc", 16)
            cc2 = sm.val("cc")

            @block.vector
            def _(v, cc2=cc2):
                v.wait_ge(sm.h["cc"], cc2)
                sm.inc(v.reduce_max(sca[0:1, 5:6], rowred[0:1, :], axis=AX.X),
                       "ve", 1)
            gm2_ve = sm.val("ve")

            @block.scalar
            def _(s, gm2_ve=gm2_ve):
                s.wait_ge(sm.h["ve"], gm2_ve)
                sm.inc(s.activation(sca[0:1, 7:8], sca[0:1, 6:7], AF.Sigmoid),
                       "se", 1)
            sig2_se = sm.val("se")

            @block.vector
            def _(v, sig2_se=sig2_se):
                v.wait_ge(sm.h["se"], sig2_se)
                v.tensor_tensor(sca[0:1, 6:7], sca[0:1, 7:8], sca[0:1, 5:6], ALU.mult)
                v.drain()
                v.tensor_scalar(sca[0:1, 6:7], sca[0:1, 6:7], 1.0 / QMAX, None,
                                ALU.mult)
                v.drain()
                v.reciprocal(sca[0:1, 7:8], sca[0:1, 6:7])
                sm.inc(v.drain(), "ve", 1)
            sh_ve = sm.val("ve")

            @block.gpsimd
            def _(g, sh_ve=sh_ve):
                g.wait_ge(sm.h["ve"], sh_ve)
                sm.inc(g.dma_start(out=bounced[130:132], in_=sca[0:1, 6:8]), "cc", 16)
                g.wait_ge(sm.h["cc"], sm.val("cc"))
                sm.inc(g.dma_start(out=qsc[:, 3:4],
                                   in_=AP(bounced.tensor, 130, [[0, 128], [1, 1]])),
                       "cc", 16)
                sm.inc(g.dma_start(out=qsc[:, 2:3],
                                   in_=AP(bounced.tensor, 131, [[0, 128], [1, 1]])),
                       "cc", 16)
            sh_cc = sm.val("cc")

            @block.vector
            def _(v, sh_cc=sh_cc):
                v.wait_ge(sm.h["cc"], sh_cc)
                sm.inc(v.tensor_scalar(deqd[:, :], sd_sb[:, :], qsc[:, 3:4], None,
                                       ALU.mult), "ve", 1)
            deqd_ve = sm.val("ve")

        with (
            nc.sbuf_tensor("wdT_sb", [128, IT * DM], BF16) as wdT_sb,
            nc.sbuf_tensor("hc", [128, IT * 512], BF16) as hc,
            nc.sbuf_tensor("hqb", [128, IT * 512], BF16) as hqb,
            nc.sbuf_tensor("tf", [128, 2048], F32) as tf,
            nc.sbuf_tensor("ti", [128, 2048], I32) as ti,
            nc.sbuf_tensor("y1c2", [128, FT * 512], BF16) as y1c2,
            nc.sbuf_tensor("rc2", [128, FT * 512], F32) as rc2,
            nc.sbuf_tensor("sq2", [128, FT * 512], F32) as sq2,
            nc.sbuf_tensor("outst", [128, FT * 512], F32) as outst,
            nc.sbuf_tensor("strow2", [1, 3072], F32) as strow2,
        ):
            @block.sync
            def _(sync, deqd_ve=deqd_ve):
                sync.wait_ge(sm.h["ve"], deqd_ve)
                for j in range(IT):
                    src = AP(wdqd.tensor, j * 128, [[DI, DM], [1, 128]])
                    sm.inc(sync.dma_start(out=wdT_sb[:, j * DM:(j + 1) * DM],
                                          in_=src, transpose=True), "ld", 16)
            wdt_ld = sm.val("ld")

            out_st = 0
            for qc in range(NQC):
                @block.sync
                def _(sync, qc=qc, prev=sm.val("ve"), os=out_st):
                    sync.wait_ge(sm.h["ve"], prev)
                    sync.wait_ge(sm.h["st"], os)
                    sm.inc(sync.dma_start(
                        out=AP(hc, 0, [[IT * 512, 128], [512, IT], [1, 512]]),
                        in_=AP(hd.tensor, qc * 512,
                               [[S, 128], [S * 128, IT], [1, 512]])), "ld", 16)
                    sm.inc(sync.dma_start(
                        out=AP(y1c2, 0, [[FT * 512, 128], [512, FT], [1, 512]]),
                        in_=AP(y1d.tensor, qc * 512,
                               [[S, 128], [S * 128, FT], [1, 512]])), "ld", 16)
                c3_ld = sm.val("ld")

                groups = [(0, 4), (4, 8), (8, 12), (12, 16), (16, 20), (20, 22)]

                @block.vector
                def _(v, c3_ld=c3_ld, wdt_ld=wdt_ld, groups=groups):
                    v.wait_ge(sm.h["ld"], max(c3_ld, wdt_ld))
                    for g0, g1 in groups:
                        w = (g1 - g0) * 512
                        v.tensor_scalar(tf[:, 0:w], hc[:, g0 * 512:g1 * 512],
                                        qsc[:, 2:3], None, ALU.mult)
                        v.drain()
                        v.tensor_copy(ti[:, 0:w], tf[:, 0:w])
                        v.drain()
                        v.tensor_scalar(hqb[:, g0 * 512:g1 * 512], ti[:, 0:w],
                                        QMAX, -QMAX - 1.0, ALU.min, ALU.max)
                        v.drain()
                    sm.inc(v.drain(), "ve", 1)
                hq_ve = sm.val("ve")

                ve_d = {}
                for fo in range(FT):
                    @block.tensor
                    def _(t, fo=fo, hq_ve=hq_ve):
                        t.wait_ge(sm.h["ve"], hq_ve)
                        if fo >= 2:
                            t.wait_ge(sm.h["ve"], ve_d[fo - 2])
                        for it in range(IT):
                            md = t.matmul(psb[fo % 2][:, :],
                                          wdT_sb[:, it * DM + fo * 128:
                                                 it * DM + (fo + 1) * 128],
                                          hqb[:, it * 512:(it + 1) * 512],
                                          start=(it == 0), stop=(it == IT - 1))
                        sm.inc(md, "te", 1)
                    d_te = sm.val("te")

                    @block.vector
                    def _(v, fo=fo, d_te=d_te):
                        v.wait_ge(sm.h["te"], d_te)
                        sm.inc(v.scalar_tensor_tensor(
                            rc2[:, fo * 512:(fo + 1) * 512], psb[fo % 2][:, :],
                            deqd[:, fo:fo + 1], y1c2[:, fo * 512:(fo + 1) * 512],
                            ALU.mult, ALU.add), "ve", 1)
                    ve_d[fo] = sm.val("ve")

                out_done = emit_ln(rc2, sq2, strow2, None, outst, qc, out_st)

                @block.sync
                def _(sync, qc=qc, out_done=out_done):
                    sync.wait_ge(sm.h["ve"], out_done)
                    sm.inc(sync.dma_start(
                        out=AP(outfm.tensor, qc * 512,
                               [[S, 128], [S * 128, FT], [1, 512]]),
                        in_=AP(outst, 0,
                               [[FT * 512, 128], [512, FT], [1, 512]])), "st", 16)
                out_st = sm.val("st")

            @block.sync
            def _(sync, out_st=out_st):
                sync.wait_ge(sm.h["st"], out_st)
    return nc


_FS = _bucket_table(-DW, DW)


def kernel(L0_attn_hidden_states, L0_attn_rel_embeddings, L0_attn_Wq, L0_attn_bq,
           L0_attn_Wk, L0_attn_bk, L0_attn_Wv, L0_attn_bv, L0_ffn_clip_x,
           L0_ffn_clip_down, L0_ffn_w_gate, L0_ffn_w_up, L0_ffn_w_down):
    x = np.asarray(L0_attn_hidden_states)
    rel = np.asarray(L0_attn_rel_embeddings)
    relg = np.ascontiguousarray(rel[_FS].T)
    relgr = np.ascontiguousarray(rel[_FS[::-1]].T)
    shared = dict(
        wqi=np.ascontiguousarray(L0_attn_Wq), wki=np.ascontiguousarray(L0_attn_Wk),
        wvi=np.ascontiguousarray(L0_attn_Wv), bqi=np.ascontiguousarray(L0_attn_bq),
        bki=np.ascontiguousarray(L0_attn_bk), bvi=np.ascontiguousarray(L0_attn_bv),
        relgT=relg, relgrT=relgr,
        wgi=np.ascontiguousarray(L0_ffn_w_gate), wui=np.ascontiguousarray(L0_ffn_w_up),
        wdi=np.ascontiguousarray(L0_ffn_w_down),
        clipx=np.ascontiguousarray(L0_ffn_clip_x),
        clipd=np.ascontiguousarray(L0_ffn_clip_down),
        ident=np.eye(128, dtype=BF),
    )
    nc = build_nc()
    in_maps = []
    for b in range(8):
        m = dict(shared)
        m["xfm"] = np.ascontiguousarray(x[b].T)
        in_maps.append(m)
    res = run_bass_kernel_spmd(nc, in_maps, core_ids=list(range(8)))
    out = np.stack([np.asarray(res.results[b]["outfm"]).T for b in range(8)])
    return np.ascontiguousarray(out.astype(np.float32))
